# version 19
# baseline (speedup 1.0000x reference)
import numpy as np
import ml_dtypes
import concourse.bass as bass
import concourse.mybir as mybir
from concourse.bass_utils import run_bass_kernel_spmd
from contextlib import ExitStack

F32 = mybir.dt.float32
BF16 = mybir.dt.bfloat16
AF = mybir.ActivationFunctionType
ALU = mybir.AluOpType
AX = mybir.AxisListType

D = 1024
HID = 4096
T = 2048
NBATCH = 2
TOK = NBATCH * T
TB = 1024
NBLK = TOK // TB
EPS = 1e-5

STAGES = "full"

_SEMS = [
    "pe", "act", "dve",
    "cst", "rin0", "rin1", "w12a", "w12b", "whf0", "whf1", "whf2", "whf3",
    "w3h0", "w3h1", "jb0", "jb1", "vb", "wob", "row", "outw", "wbk",
]


TAPS_ON = False


class Walk:

    def __init__(self, which, engine, semmap, tapdict=None):
        self.w = which
        self.eng = engine
        self.sems = semmap
        self.cnt = {k: 0 for k in semmap}
        self.taps = []
        self.tapdict = tapdict

    def wait(self, eng, sem, val):
        if val is None or val <= 0:
            return
        assert val <= self.cnt[sem], f"wait-ahead {eng}.{sem} {val}>{self.cnt[sem]}"
        if self.w == eng:
            self.eng.wait_ge(self.sems[sem], val)

    def _fin(self, ins, inc, k):
        if inc is not None and ins is not None:
            ins.then_inc(self.sems[inc], k)

    def _bump(self, inc, k):
        if inc is None:
            return None
        self.cnt[inc] += k
        return self.cnt[inc]

    def mm(self, out, lhsT, rhs, start, stop, inc=None, k=1):
        if self.w == "tensor":
            ins = self.eng.matmul(out, lhsT, rhs, start=start, stop=stop)
            self._fin(ins, inc, k)
        return self._bump(inc, k)

    def tp(self, out, in_, ident, inc=None, k=1):
        if self.w == "tensor":
            ins = self.eng.transpose(out, in_, ident)
            self._fin(ins, inc, k)
        return self._bump(inc, k)

    def act(self, fn, *a, inc=None, k=1, **kw):
        if self.w == "scalar":
            ins = getattr(self.eng, fn)(*a, **kw)
            self._fin(ins, inc, k)
        return self._bump(inc, k)

    def dve(self, fn, *a, inc=None, k=1, **kw):
        if self.w == "vector":
            ins = getattr(self.eng, fn)(*a, **kw)
            self._fin(ins, inc, k)
        return self._bump(inc, k)

    def dma(self, eng, out, in_, inc, k=16):
        if self.w == eng:
            ins = self.eng.dma_start(out=out, in_=in_)
            self._fin(ins, inc, k)
        return self._bump(inc, k)

    def tap(self, name, ap, gates=()):
        if not TAPS_ON:
            return
        self.taps.append((name, tuple(ap.shape), ap.dtype))
        for sem, val in gates:
            self.wait("sync", sem, val)
        if self.w == "sync" and self.tapdict is not None:
            ins = self.eng.dma_start(out=self.tapdict[name][:], in_=ap)
            ins.then_inc(self.sems["outw"], 16)
        self.cnt["outw"] += 16


def _schedule(em, H):
    c = em.cnt

    em.dma("sync", H["scols"][:], H["d_scols"][:], "cst")
    em.dma("sync", H["idbf"][:], H["d_idbf"][:], "cst")
    em.dma("sync", H["id32"][:], H["d_id32"][:], "cst")
    em.dma("sync", H["ones1"][:], H["d_ones1"][:], "cst")
    em.dma("sync", H["sel4"][:], H["d_sel4"][:], "cst")
    em.dma("sync", H["fc"][:], H["d_fc"][:], "cst")
    em.dma("sync", H["zrow"][:], H["d_zrow"][:], "cst")
    ROW_N = em.dma("gpsimd", H["brows"][0:1, 0:2048], H["d_brows"][:], "row")
    CST_N = c["cst"]
    for e in ("tensor", "scalar", "vector"):
        em.wait(e, "cst", CST_N)
    em.dve("tensor_scalar", H["s3h"][:], H["scols"][:, 128:136], 0.5, None,
           ALU.mult, ALU.bypass)

    def ffn(src_d, dst_d, first=False):
        A = H["ffn"]
        rbuf = [A["rb0"], A["rb1"]]
        rsem = ["rin0", "rin1"]
        rfree = [0, 0]
        rload = [0, 0]
        outw_gate = c["outw"]
        zb_free = [0, 0]
        pt_dve_s = [0]
        last_z = [0]

        def ln_stats(b, z_gate):
            tok0 = b * TB
            for t in range(8):
                s = t % 2
                em.wait("sync", "dve", rfree[s])
                rload[s] = em.dma("sync", rbuf[s][:],
                                  src_d[tok0 + t * 128: tok0 + (t + 1) * 128, :], rsem[s])
                em.wait("vector", rsem[s], rload[s])
                em.wait("vector", "dve", z_gate if t == 0 else 0)
                em.dve("bn_stats", H["st6"][:, 0:6], rbuf[s][:, 0:512])
                em.dve("bn_stats", H["st6"][:, 6:12], rbuf[s][:, 512:1024])
                em.dve("drain")
                em.dve("bn_aggr", H["smv"][:, 2 * t:2 * t + 2], H["st6"][:])
                ST = em.dve("drain", inc="dve")
                rfree[s] = ST
            em.wait("scalar", "dve", ST)
            em.wait("scalar", "dve", z_gate)
            smv_var = H["smv"][:, 0:16].rearrange("p (t two) -> p t two", two=2)[:, :, 1]
            em.act("activation", H["rstd"][:, 0:8], smv_var, AF.Ln,
                   bias=H["fc"][:, 0:1], scale=1.0)
            RS = em.act("activation", H["rstd"][:, 0:8], H["rstd"][:, 0:8],
                        AF.Exp, scale=-0.5, inc="act")
            return RS

        def ln_apply_tile(b, RS, t, st):
            tok0 = b * TB
            pt_dve = st["pt"]
            if True:
                s = t % 2
                em.wait("sync", "dve", rfree[s])
                rload[s] = em.dma("sync", rbuf[s][:],
                                  src_d[tok0 + t * 128: tok0 + (t + 1) * 128, :], rsem[s])
                em.wait("vector", rsem[s], rload[s])
                em.wait("vector", "act", RS if t == 0 else 0)
                em.wait("vector", "pe", zb_free[t % 2])
                ZD = em.dve("tensor_scalar", A["zb"][t % 2][:], rbuf[s][:],
                            H["smv"][:, 2 * t:2 * t + 1], H["rstd"][:, t:t + 1],
                            ALU.subtract, ALU.mult, inc="dve")
                rfree[s] = ZD
                em.wait("tensor", "dve", ZD)
                em.wait("tensor", "dve", pt_dve)
                for q in range(8):
                    TPQ = em.tp(H["PT"][:, q * 128:(q + 1) * 128],
                                A["zb"][t % 2][:, q * 128:(q + 1) * 128], H["idbf"][:],
                                inc="pe" if q == 7 else None)
                zb_free[t % 2] = TPQ
                em.wait("vector", "pe", TPQ)
                dst = A["xlnT"].rearrange("p (i tt q) -> p i tt q", i=8, tt=8)[:, :, t, :]
                src = H["PT"][:].rearrange("p (i q) -> p i q", i=8)
                if t in (3, 7):
                    em.dve("tensor_copy", dst, src)
                    pt_dve = em.dve("drain", inc="dve")
                else:
                    pt_dve = em.dve("tensor_copy", dst, src, inc="dve")
                if t == 3:
                    st["half0"] = pt_dve
            st["pt"] = pt_dve
            st["ZD"] = ZD

        def ln_apply(b, RS, tiles=range(8), st=None):
            if st is None:
                st = {"pt": pt_dve_s[0], "half0": 0, "ZD": 0}
            for t in tiles:
                ln_apply_tile(b, RS, t, st)
            if max(tiles) == 7:
                pt_dve_s[0] = st["pt"]
                last_z[0] = st["ZD"]
            return st

        if first:
            A["_whfwbk"] = [0, 0, 0, 0]
        whf_state = {"load": [0, 0, 0, 0], "free": [0, 0, 0, 0],
                     "wbk": A["_whfwbk"]}
        whfsem = ["whf0", "whf1", "whf2", "whf3"]
        whfbuf = [A["whf0"], A["whf1"], A["whf2"], A["whf3"]]

        def whf_load(ci, src):
            em.wait("scalar", "pe", whf_state["free"][ci])
            em.wait("scalar", "wbk", whf_state["wbk"][ci])
            whf_state["load"][ci] = em.dma("scalar", whfbuf[ci][:], src,
                                           whfsem[ci])

        def hid_prefetch(c_d):
            for j2 in range(2):
                for hc in range(2):
                    ci = (j2 * 2 + hc) % 4
                    whf_load(ci, c_d[j2][:, hc * 2048:(hc + 1) * 2048])

        def hid_layer(w_d, c_d, cache_write, src_sb, dst_sb, scol_off,
                      src_sem, src_cnt, ph1_p2_free=0, pre=False):
            pslot_free = [0, 0, 0]
            pregs = [H["P0"], H["P1"], H["P2"]]
            GE = 0
            for j2 in range(32):
                for hc in range(2):
                    ci = (j2 * 2 + hc) % 4
                    if pre and j2 < 2:
                        continue
                    if cache_write:
                        em.wait("gpsimd", "pe", whf_state["free"][ci])
                        em.wait("gpsimd", "wbk", whf_state["wbk"][ci])
                        whf_state["load"][ci] = em.dma(
                            "gpsimd", whfbuf[ci][:],
                            w_d[j2][:, hc * 2048:(hc + 1) * 2048], whfsem[ci])
                        em.wait("sync", whfsem[ci], whf_state["load"][ci])
                        whf_state["wbk"][ci] = em.dma(
                            "sync", c_d[j2][:, hc * 2048:(hc + 1) * 2048],
                            whfbuf[ci][:], "wbk")
                    else:
                        whf_load(ci, c_d[j2][:, hc * 2048:(hc + 1) * 2048])
                sp = j2 % 3
                Ph = pregs[sp]
                em.wait("tensor", "act", pslot_free[sp])
                em.wait("tensor", src_sem, src_cnt)
                em.wait("tensor", "dve", ph1_p2_free if j2 < 3 else 0)
                for half in range(2):
                    for i in range(32):
                        ci = (j2 * 2 + (i // 16)) % 4
                        if i in (0, 16):
                            em.wait("tensor", whfsem[ci], whf_state["load"][ci])
                        rhs = src_sb[:, i * 1024 + half * 512: i * 1024 + (half + 1) * 512]
                        MME = em.mm(Ph[:, half * 512:(half + 1) * 512],
                                    whfbuf[ci][:, (i % 16) * 128:((i % 16) + 1) * 128],
                                    rhs, start=(i == 0), stop=(i == 31),
                                    inc="pe" if (i == 31 or (half == 1 and i == 15))
                                    else None)
                        if half == 1 and i == 15:
                            whf_state["free"][(j2 * 2) % 4] = MME
                    if half == 1:
                        whf_state["free"][(j2 * 2 + 1) % 4] = MME
                em.wait("scalar", "pe", MME)
                dsth = dst_sb[:, j2 * 1024:(j2 + 1) * 1024]
                GE = em.act("activation", dsth, Ph[:], AF.Gelu,
                            scale=H["scols"][:, scol_off + j2:scol_off + j2 + 1],
                            inc="act")
                pslot_free[sp] = GE
            return GE

        stats_ready = ln_stats(0, 0)
        apply_ready = ln_apply(0, stats_ready)
        for b in range(NBLK):
            tok0 = b * TB
            xln_half0, XLN_OK = apply_ready["half0"], apply_ready["pt"]

            if not (first and b == 0):
                hid_prefetch(H["c_h1"])
            w12sem = ["w12a", "w12b"]
            w12buf = [A["w12a"], A["w12b"]]
            w12load = [0, 0]
            w12free = [0, 0]
            if first:
                A["_w12wbk"] = [0, 0]
            w12wbk = A["_w12wbk"]
            slot_free = [0, 0, 0]
            t1free = [0, 0, 0]
            SF = 0
            for j in range(32):
                sj = j % 2
                if first and b == 0:
                    em.wait("gpsimd", "pe", w12free[sj])
                    em.wait("gpsimd", "wbk", w12wbk[sj])
                    w12load[sj] = em.dma("gpsimd", w12buf[sj][:], H["d_w12"][j],
                                         w12sem[sj])
                    em.wait("sync", w12sem[sj], w12load[sj])
                    w12wbk[sj] = em.dma("sync", H["c_w12"][j], w12buf[sj][:], "wbk")
                else:
                    em.wait("scalar", "pe", w12free[sj])
                    em.wait("scalar", "wbk", w12wbk[sj])
                    w12load[sj] = em.dma("scalar", w12buf[sj][:], H["c_w12"][j],
                                         w12sem[sj])
                for half in range(2):
                    sp = (j * 2 + half) % 3
                    Ps = [H["P0"], H["P1"], H["P2"]][sp]
                    em.wait("tensor", w12sem[sj], w12load[sj])
                    em.wait("tensor", "dve",
                            max(xln_half0 if half == 0 else XLN_OK, slot_free[sp]))
                    for i in range(8):
                        rhs = A["xlnT"][:, i * 1024 + half * 512: i * 1024 + (half + 1) * 512]
                        em.mm(Ps[:, 0:512], w12buf[sj][:, i * 128:(i + 1) * 128], rhs,
                              start=(i == 0), stop=(i == 7))
                    for i in range(8):
                        rhs = A["xlnT"][:, i * 1024 + half * 512: i * 1024 + (half + 1) * 512]
                        MME = em.mm(Ps[:, 512:1024],
                                    w12buf[sj][:, 1024 + i * 128:1024 + (i + 1) * 128], rhs,
                                    start=(i == 0), stop=(i == 7),
                                    inc="pe" if i == 7 else None)
                    if half == 1:
                        w12free[sj] = MME
                    em.wait("scalar", "pe", MME)
                    em.wait("scalar", "dve", t1free[sp])
                    SIL = em.act("activation", A["t1"][sp][:], Ps[:, 0:512], AF.Silu,
                                 scale=H["scols"][:, j:j + 1], inc="act")
                    em.wait("vector", "act", SIL)
                    dsth = A["hA"][:, j * 1024 + half * 512: j * 1024 + (half + 1) * 512]
                    SF = em.dve("scalar_tensor_tensor", dsth, Ps[:, 512:1024],
                                H["scols"][:, 32 + j:32 + j + 1], A["t1"][sp][:],
                                ALU.mult, ALU.mult, inc="dve")
                    slot_free[sp] = SF
                    t1free[sp] = SF
            PH1_OK = SF

            PH2_OK = hid_layer(H["d_h1"], H["c_h1"], first and b == 0,
                               A["hA"], A["hB"], 64, "dve", PH1_OK,
                               ph1_p2_free=PH1_OK,
                               pre=not (first and b == 0))
            if b + 1 < NBLK:
                stats_ready = ln_stats(b + 1, last_z[0])
            PH3_OK = hid_layer(H["d_h2"], H["c_h2"], first and b == 0,
                               A["hB"], A["hA"], 96, "act", PH2_OK)

            w3sem = ["w3h0", "w3h1"]
            w3buf = [A["w3h0"], A["w3h1"]]
            w3load = [0, 0]
            w3free = [0, 0]
            if first:
                A["_w3wbk"] = [0, 0]
            w3wbk = A["_w3wbk"]
            ph4_slot_free = [0, 0, 0]
            p2_gate = 0
            for half in range(2):
                if half == 1 and b + 1 < NBLK:
                    ap_st = {"pt": pt_dve_s[0], "half0": 0, "ZD": 0}
                EV = 0
                for j3 in range(8):
                    if half == 1 and b + 1 < NBLK and j3 >= 1:
                        apply_ready = ln_apply(b + 1, stats_ready,
                                               tiles=[j3 - 1], st=ap_st)
                    for hc in range(2):
                        ci = (j3 * 2 + hc) % 2
                        if first and b == 0 and half == 0:
                            em.wait("gpsimd", "pe", w3free[ci])
                            em.wait("gpsimd", "wbk", w3wbk[ci])
                            w3load[ci] = em.dma(
                                "gpsimd", w3buf[ci][:],
                                H["d_w3"][j3][:, hc * 2048:(hc + 1) * 2048], w3sem[ci])
                            em.wait("sync", w3sem[ci], w3load[ci])
                            w3wbk[ci] = em.dma(
                                "sync", H["c_w3"][j3][:, hc * 2048:(hc + 1) * 2048],
                                w3buf[ci][:], "wbk")
                        else:
                            em.wait("scalar", "pe", w3free[ci])
                            em.wait("scalar", "wbk", w3wbk[ci])
                            w3load[ci] = em.dma(
                                "scalar", w3buf[ci][:],
                                H["c_w3"][j3][:, hc * 2048:(hc + 1) * 2048], w3sem[ci])
                    sp = j3 % 3
                    Pp = [H["P0"][:, 0:512], H["P1"][:, 0:512],
                          H["P0"][:, 512:1024]][sp]
                    em.wait("tensor", "dve", ph4_slot_free[sp])
                    em.wait("tensor", "act", PH3_OK)
                    for i in range(32):
                        ci = (j3 * 2 + (i // 16)) % 2
                        if i in (0, 16):
                            em.wait("tensor", w3sem[ci], w3load[ci])
                        rhs = A["hA"][:, i * 1024 + half * 512: i * 1024 + (half + 1) * 512]
                        MME = em.mm(Pp, w3buf[ci][:, (i % 16) * 128:((i % 16) + 1) * 128],
                                    rhs, start=(i == 0), stop=(i == 31),
                                    inc="pe" if (i == 31 or i == 15) else None)
                        if i == 15:
                            w3free[(j3 * 2) % 2] = MME
                    w3free[(j3 * 2 + 1) % 2] = MME
                    em.wait("vector", "pe", MME)
                    EV = em.dve("tensor_scalar", A["outT"][:, j3 * 512:(j3 + 1) * 512], Pp,
                                H["s3h"][:, j3:j3 + 1], None, ALU.mult, ALU.bypass,
                                inc="dve")
                    ph4_slot_free[sp] = EV
                if half == 1 and b + 1 < NBLK:
                    apply_ready = ln_apply(b + 1, stats_ready,
                                           tiles=[7], st=ap_st)
                OUTT_OK = EV
                for q in range(4):
                    t = half * 4 + q
                    s = t % 2
                    em.wait("sync", "dve", rfree[s])
                    rload[s] = em.dma("sync", rbuf[s][:],
                                      src_d[tok0 + t * 128: tok0 + (t + 1) * 128, :],
                                      rsem[s])
                    em.wait("tensor", "dve", max(OUTT_OK, p2_gate))
                    for j3 in range(8):
                        TPQ = em.tp(H["P2"][:, j3 * 128:(j3 + 1) * 128],
                                    A["outT"][:, j3 * 512 + q * 128: j3 * 512 + (q + 1) * 128],
                                    H["id32"][:], inc="pe" if j3 == 7 else None)
                    em.wait("vector", "pe", TPQ)
                    em.wait("vector", rsem[s], rload[s])
                    em.wait("vector", "outw", outw_gate)
                    AD = em.dve("tensor_tensor", A["outb"][:], rbuf[s][:], H["P2"][:],
                                ALU.add, inc="dve")
                    rfree[s] = AD
                    p2_gate = AD
                    em.wait("sync", "dve", AD)
                    outw_gate = em.dma("sync",
                                       dst_d[tok0 + t * 128: tok0 + (t + 1) * 128, :],
                                       A["outb"][:], "outw")
        return outw_gate

    def attn(src_d, dst_d):
        def stage_barrier():
            marks = dict(pe=c["pe"], act=c["act"], dve=c["dve"])
            for e in ("tensor", "scalar", "vector", "sync", "gpsimd"):
                em.wait(e, "pe", marks["pe"])
                em.wait(e, "act", marks["act"])
                em.wait(e, "dve", marks["dve"])
        A = H["attn"]
        rbuf = [A["rb0"], A["rb1"]]
        rsem = ["rin0", "rin1"]
        outw_gate = c["outw"]
        at_done = {(s, p): 0 for s in range(2) for p in range(3)}
        wob_n = 0
        em.wait("gpsimd", "pe", c["pe"])
        for f in range(8):
            wob_n = em.dma("gpsimd", A["wout"][f][:], H["d_wout"][f], "wob")

        for bi in range(NBATCH):
            tokb = bi * T
            rfree = [0, 0]
            rload = [0, 0]
            ps_act = [0, 0]
            ps_dve = [0, 0]
            pq_dve = 0
            pt_dve = 0

            em.wait("sync", "pe", c["pe"])
            em.dma("sync", A["gam"][:], H["d_gam"][:], "cst")
            GBL = em.dma("sync", A["bet"][:], H["d_bet"][:], "cst")
            for t in range(16):
                s = t % 2
                em.wait("sync", "dve", rfree[s])
                rload[s] = em.dma("sync", rbuf[s][:],
                                  src_d[tokb + t * 128: tokb + (t + 1) * 128, :], rsem[s])
                em.wait("vector", rsem[s], rload[s])
                em.dve("bn_stats", H["st6"][:, 0:6], rbuf[s][:, 0:512])
                em.dve("bn_stats", H["st6"][:, 6:12], rbuf[s][:, 512:1024])
                em.dve("drain")
                em.dve("bn_aggr", H["smv"][:, 2 * t:2 * t + 2], H["st6"][:])
                ST = em.dve("drain", inc="dve")
                em.wait("scalar", "dve", ST)
                em.act("activation", H["rstd"][:, t:t + 1], H["smv"][:, 2 * t + 1:2 * t + 2],
                       AF.Ln, bias=H["fc"][:, 0:1], scale=1.0)
                em.act("activation", H["rstd"][:, t:t + 1], H["rstd"][:, t:t + 1],
                       AF.Exp, scale=-0.5)
                em.act("activation", H["rstd2"][:, t:t + 1], H["smv"][:, 2 * t + 1:2 * t + 2],
                       AF.Ln, bias=H["fc"][:, 1:2], scale=float(1.0 + EPS))
                RS = em.act("activation", H["rstd2"][:, t:t + 1], H["rstd2"][:, t:t + 1],
                            AF.Exp, scale=-0.5, inc="act")
                em.wait("vector", "act", RS)
                em.wait("vector", "cst", GBL)
                em.dve("tensor_scalar", A["tf"][:], rbuf[s][:],
                       H["smv"][:, 2 * t:2 * t + 1], H["rstd2"][:, t:t + 1],
                       ALU.subtract, ALU.mult)
                em.dve("tensor_tensor", A["tf"][:], A["tf"][:], A["gam"][:], ALU.mult)
                HB = em.dve("tensor_tensor", A["hb"][:], A["tf"][:], A["bet"][:],
                            ALU.add, inc="dve")
                rfree[s] = HB
                if bi == 0 and t == 0:
                    em.tap("hb0", A["hb"][:], gates=[("dve", HB)])
                    em.tap("smv0", H["smv"][:], gates=[("dve", HB)])
                    em.tap("rstd20", H["rstd2"][:], gates=[("act", RS)])
                em.wait("tensor", "dve", max(HB, pt_dve))
                for q in range(8):
                    TPQ = em.tp(H["PT"][:, q * 128:(q + 1) * 128],
                                A["hb"][:, q * 128:(q + 1) * 128], H["idbf"][:],
                                inc="pe" if q == 7 else None)
                em.wait("vector", "pe", TPQ)
                dst = A["hT"].rearrange("p (i tt q) -> p i tt q", i=8, tt=16)[:, :, t, :]
                src = H["PT"][:].rearrange("p (i q) -> p i q", i=8)
                if t == 15:
                    em.dve("tensor_copy", dst, src)
                    pt_dve = em.dve("drain", inc="dve")
                else:
                    pt_dve = em.dve("tensor_copy", dst, src, inc="dve")
            if bi == 0:
                em.tap("hT0", A["hT"][:, 0:2048], gates=[("dve", pt_dve)])
            em.dve("tensor_scalar", H["zcA"][:], H["rstd"][:, 0:16], 1.0, None,
                   ALU.add, ALU.bypass)
            ZC = em.dve("tensor_tensor", H["zcB"][:],
                        H["smv"][:].rearrange("p (t two) -> p t two", two=2)[:, :, 0],
                        H["rstd"][:, 0:16], ALU.mult, inc="dve")

            v_done = ZC
            stage_barrier()
            for r in range(2):
                qj = [4 * r + a for a in range(4)]
                kj = [8 + 4 * r + a for a in range(4)]
                jsem = ["jb0", "jb1"]
                jbuf = [A["jb0"], A["jb1"]]
                jload = [0, 0]
                jfree = [0, 0]
                for idx, j in enumerate(qj + kj):
                    sj = idx % 2
                    em.wait("gpsimd", "pe", jfree[sj])
                    jload[sj] = em.dma("gpsimd", jbuf[sj][:], H["d_wqkv"][j], jsem[sj])
                    if TAPS_ON and bi == 0 and r == 1 and idx == 7:
                        em.tap("jb15", jbuf[sj][:], gates=[(jsem[sj], jload[sj])])
                    a = idx % 4
                    dst_sb = A["qT"] if idx < 4 else A["kT"]
                    for half in range(2):
                        sp = (idx * 2 + half) % 2
                        Ps = [H["P0"], H["P1"]][sp]
                        em.wait("tensor", jsem[sj], jload[sj])
                        em.wait("tensor", "dve", pt_dve)
                        em.wait("tensor", "act", ps_act[sp])
                        em.wait("tensor", "dve", ps_dve[sp])
                        for qtr in range(2):
                            for i in range(8):
                                rhs = A["hT"][:, i * 2048 + half * 1024 + qtr * 512:
                                              i * 2048 + half * 1024 + (qtr + 1) * 512]
                                MME = em.mm(Ps[:, qtr * 512:(qtr + 1) * 512],
                                            jbuf[sj][:, i * 128:(i + 1) * 128], rhs,
                                            start=(i == 0), stop=(i == 7),
                                            inc="pe" if (qtr == 1 and i == 7) else None)
                        if half == 1:
                            jfree[sj] = MME
                        if TAPS_ON and bi == 0 and r == 1 and idx == 7 and half == 1:
                            em.wait("vector", "pe", MME)
                            PQ_ = em.dve("tensor_copy", A["r2b"][:], Ps[:], inc="dve")
                            em.tap("k15psum", A["r2b"][:], gates=[("dve", PQ_)])
                            em.wait("scalar", "dve", PQ_)
                        em.wait("scalar", "pe", MME)
                        em.act("activation",
                               dst_sb[:, a * 2048 + half * 1024:
                                      a * 2048 + (half + 1) * 1024],
                               Ps[:], AF.Identity,
                               bias=H["scols"][:, 136 + j:136 + j + 1], scale=1.0)
                        QKE = em.act("drain", inc="act")
                        ps_act[sp] = QKE
                        if bi == 0 and idx == 0 and half == 1:
                            em.tap("qT0", A["qT"][:, 0:2048], gates=[("act", QKE)])
                        if bi == 0 and idx == 4 and half == 1:
                            em.tap("kT0", A["kT"][:, 0:2048], gates=[("act", QKE)])
                        if bi == 0 and r == 1 and idx == 3 and half == 1:
                            em.tap("qT3r1", A["qT"][:, 3 * 2048:4 * 2048], gates=[("act", QKE)])
                        if bi == 0 and r == 1 and idx == 7 and half == 1:
                            em.tap("kT3r1", A["kT"][:, 3 * 2048:4 * 2048], gates=[("act", QKE)])

                stage_barrier()
                if r == 0:
                    vload = 0
                    for i in range(8):
                        vload = em.dma("gpsimd", A["vb"][i][:], H["d_wv"][i], "vb")
                    va4 = A["vaug"].rearrange("p (t h cc) -> p t h cc", t=16, h=16)
                    VMS = em.dve("memset", va4[:, :, :, 64:65], 1.0, inc="dve")
                    for t in range(16):
                        sp = t % 2
                        Ps = [H["P0"], H["P1"]][sp]
                        em.wait("tensor", "vb", vload)
                        em.wait("tensor", "row", ROW_N)
                        em.wait("tensor", "act", ps_act[sp])
                        em.wait("tensor", "dve", ps_dve[sp])
                        for i in range(8):
                            for hh in range(2):
                                em.mm(Ps[:, hh * 512:(hh + 1) * 512],
                                      A["hT"][:, i * 2048 + t * 128: i * 2048 + (t + 1) * 128],
                                      A["vb"][i][:, hh * 512:(hh + 1) * 512],
                                      start=(i == 0), stop=False)
                        for hh in range(2):
                            MME = em.mm(Ps[:, hh * 512:(hh + 1) * 512],
                                        H["ones1"][:],
                                        H["brows"][0:1, hh * 512:(hh + 1) * 512],
                                        start=False, stop=True,
                                        inc="pe" if hh == 1 else None)
                        em.wait("vector", "pe", MME)
                        em.wait("vector", "dve", VMS)
                        src = Ps[:].rearrange("p (h cc) -> p h cc", h=16)
                        if bi == 0 and t == 0 and TAPS_ON:
                            PR = em.dve("tensor_copy", A["r2b"][:], Ps[:], inc="dve")
                            em.tap("vpsum0", A["r2b"][:], gates=[("dve", PR)])
                            em.tap("vb0", A["vb"][0][:], gates=[("dve", PR)])
                            em.tap("vb7", A["vb"][7][:], gates=[("dve", PR)])
                            em.tap("hTv0", A["hT"][:, 0:128], gates=[("dve", PR)])
                            em.tap("brows0", H["brows"][0:1, 0:2048], gates=[("dve", PR)])
                        em.dve("tensor_copy", va4[:, t, :, 0:64], src)
                        VE = em.dve("drain", inc="dve")
                        ps_dve[sp] = VE
                        v_done = VE
                        if bi == 0 and t == 0:
                            em.tap("vaug0", A["vaug"][:, 0:1040], gates=[("dve", VE)])

                stage_barrier()
                nsc = H["nsc"]
                for side in range(2):
                    src_sb = A["qT"] if side == 0 else A["kT"]
                    Mcol = nsc[0:8, 1 + 2 * side:2 + 2 * side]
                    regs = [H["P2"][0:8, 0:512], H["P2"][0:8, 512:1024],
                            H["P3"][0:8, 0:512], H["P0"][0:8, 0:512]]
                    sq_free = [0, 0]
                    for a in range(4):
                        sb = a % 2
                        em.wait("vector", "pe", sq_free[sb])
                        SQ = em.dve("tensor_tensor", A["at"][sb][:],
                                    src_sb[:, a * 2048:(a + 1) * 2048],
                                    src_sb[:, a * 2048:(a + 1) * 2048], ALU.mult,
                                    inc="dve")
                        em.wait("tensor", "dve", SQ)
                        if a == 0:
                            em.wait("tensor", "dve", pq_dve)
                            em.wait("tensor", "act", ps_act[0])
                            em.wait("tensor", "dve", ps_dve[0])
                        for ch in range(4):
                            MME = em.mm(regs[ch], H["sel4"][:, a * 8:(a + 1) * 8],
                                        A["at"][sb][:, ch * 512:(ch + 1) * 512],
                                        start=(a == 0), stop=(a == 3),
                                        inc="pe" if ch == 3 else None)
                        sq_free[sb] = MME
                    em.wait("vector", "pe", MME)
                    for ch in range(4):
                        em.dve("reduce_max", nsc[0:8, 0:1], regs[ch], axis=AX.X)
                        em.dve("drain")
                        if ch == 0:
                            em.dve("tensor_copy", Mcol, nsc[0:8, 0:1])
                        else:
                            em.dve("tensor_tensor", Mcol, Mcol, nsc[0:8, 0:1], ALU.max)
                        em.dve("drain")
                    NR = em.dve("drain", inc="dve")
                    pq_dve = NR
                    ps_dve[0] = NR
                em.dve("tensor_tensor", nsc[0:8, 5:6], nsc[0:8, 1:2], nsc[0:8, 3:4],
                       ALU.mult)
                NP = em.dve("drain", inc="dve")
                em.wait("scalar", "dve", NP)
                em.act("activation", nsc[0:8, 6:7], nsc[0:8, 5:6], AF.Ln,
                       bias=H["fc"][0:8, 2:3], scale=1.0)
                NB_ = em.act("activation", nsc[0:8, 6:7], nsc[0:8, 6:7], AF.Exp,
                             scale=0.5, inc="act")
                em.wait("vector", "act", NB_)
                em.dve("tensor_scalar", nsc[0:8, 7:8], nsc[0:8, 6:7], -0.125, None,
                       ALU.mult, ALU.bypass)
                em.dve("drain")
                em.dve("memset", A["nb32"][0:32, 0:64], 0.0)
                em.dve("tensor_copy", A["nb32"][0:8, 0:1], nsc[0:8, 7:8])
                em.dve("drain")
                em.dve("transpose", A["nb32"][0:32, 32:64], A["nb32"][0:32, 0:32])
                NBT = em.dve("drain", inc="dve")
                em.wait("tensor", "dve", NBT)
                em.wait("tensor", "dve", ps_dve[0])
                em.wait("tensor", "act", ps_act[0])
                BC = em.mm(H["P0"][:, 0:8], H["ones1"][:], A["nb32"][0:1, 32:40],
                           start=True, stop=True, inc="pe")
                em.wait("vector", "pe", BC)
                em.dve("tensor_copy", H["negb"][:], H["P0"][:, 0:8])
                NBB = em.dve("drain", inc="dve")
                ps_dve[0] = NBB
                if bi == 0 and r == 0:
                    em.tap("nsc0", H["nsc"][:], gates=[("dve", NBB)])
                    em.tap("negb0", H["negb"][:], gates=[("dve", NBB)])

                stage_barrier()
                for pa in range(4):
                    heads = [8 * r + 2 * pa, 8 * r + 2 * pa + 1]
                    for qhalf in range(2):
                        obase = [H["P2"], H["P3"]]

                        def qoff(s, qc):
                            if qc < 7:
                                return obase[s][:, qc * 65:(qc + 1) * 65]
                            return obase[s][:, 512:577]

                        def score(s, kt):
                            hh = 2 * pa + s
                            pb = (hh % 2) * 64
                            Ps = [H["P0"], H["P1"]][s]
                            em.wait("tensor", "act", ps_act[s])
                            em.wait("tensor", "dve", ps_dve[s])
                            for qtr in range(2):
                                MME = em.mm(
                                    Ps[:, qtr * 512:(qtr + 1) * 512],
                                    A["kT"][pb:pb + 64,
                                            pa * 2048 + kt * 128: pa * 2048 + (kt + 1) * 128],
                                    A["qT"][pb:pb + 64,
                                            pa * 2048 + qhalf * 1024 + qtr * 512:
                                            pa * 2048 + qhalf * 1024 + (qtr + 1) * 512],
                                    start=True, stop=True,
                                    inc="pe" if qtr == 1 else None)
                            return MME

                        def expf(s, kt):
                            hh = 2 * pa + s
                            em.wait("scalar", "pe", max(sc_mme[s], at_done[(s, kt % 3)]))
                            if pa == 0 and qhalf == 0 and kt == 0:
                                em.wait("scalar", "dve", NBB)
                            EX = em.act("activation", A["at6"][3 * s + kt % 3][:],
                                        [H["P0"], H["P1"]][s][:], AF.Exp,
                                        bias=H["negb"][:, hh:hh + 1], scale=0.125,
                                        inc="act")
                            ps_act[s] = EX

                        def attnv(s, kt2):
                            hg = 8 * r + 2 * pa + s
                            atb2 = A["at6"][3 * s + kt2 % 3]
                            for qc in range(8):
                                M2 = em.mm(qoff(s, qc), atb2[:, qc * 128:(qc + 1) * 128],
                                           A["vaug"][:, kt2 * 1040 + hg * 65:
                                                     kt2 * 1040 + (hg + 1) * 65],
                                           start=False, stop=(kt2 == 15),
                                           inc="pe" if qc == 7 else None)
                            at_done[(s, kt2 % 3)] = M2

                        sc_mme = [0, 0]
                        for kt in range(16):
                            sc_mme[0] = score(0, kt)
                            sc_mme[1] = score(1, kt)
                            expf(0, kt)
                            expf(1, kt)
                            if kt == 2:
                                em.wait("tensor", "dve", max(pq_dve, v_done))
                                for s in range(2):
                                    em.mm(obase[s][:, 0:455], H["ones1"][:],
                                          H["zrow"][0:1, 0:455], start=True, stop=False)
                                    em.mm(obase[s][:, 512:577], H["ones1"][:],
                                          H["zrow"][0:1, 0:65], start=True, stop=False)
                            if kt >= 2:
                                attnv(0, kt - 2)
                                attnv(1, kt - 2)
                        attnv(0, 14)
                        attnv(1, 14)
                        em.wait("tensor", "act", ps_act[0])
                        attnv(0, 15)
                        em.wait("tensor", "act", ps_act[1])
                        attnv(1, 15)
                        em.wait("vector", "pe", at_done[(1, 15 % 3)])
                        for s in range(2):
                            hg = 8 * r + 2 * pa + s
                            em.dve("reciprocal", H["rc"][:, s * 8:s * 8 + 7],
                                   obase[s][:, 64:512:65])
                            em.dve("reciprocal", H["rc"][:, s * 8 + 7:s * 8 + 8],
                                   obase[s][:, 576:577])
                            em.dve("drain")
                            for qc in range(8):
                                gq = qhalf * 8 + qc
                                em.dve("tensor_scalar",
                                       A["osb"][:, gq * 1024 + hg * 64:
                                                gq * 1024 + (hg + 1) * 64],
                                       qoff(s, qc)[:, 0:64],
                                       H["rc"][:, s * 8 + qc:s * 8 + qc + 1], None,
                                       ALU.mult, ALU.bypass)
                        ON = em.dve("drain", inc="dve")
                        pq_dve = ON

            stage_barrier()
            if TAPS_ON and bi == 0:
                em.tap("osbP", A["osb"][:, 0:1024], gates=[("dve", pq_dve)])
                em.tap("osbP8", A["osb"][:, 8 * 1024:9 * 1024], gates=[("dve", pq_dve)])
            otb_free = [0, 0]
            OTEs = [0] * 17

            def proj_tp(t2):
                ob2 = t2 % 2
                em.wait("tensor", "dve", max(pq_dve if t2 == 0 else 0, pt_dve_l[0]))
                for f in range(8):
                    TPQ = em.tp(H["PT"][:, f * 128:(f + 1) * 128],
                                A["osb"][:, t2 * 1024 + f * 128: t2 * 1024 + (f + 1) * 128],
                                H["idbf"][:], inc="pe" if f == 7 else None)
                em.wait("vector", "pe", max(TPQ, otb_free[ob2]))
                OTEs[t2] = em.dve("tensor_copy", A["otb"][ob2][:], H["PT"][:], inc="dve")
                pt_dve_l[0] = OTEs[t2]

            pt_dve_l = [pt_dve]
            proj_tp(0)
            for t in range(16):
                ob = t % 2
                if t + 1 < 16:
                    proj_tp(t + 1)
                OTE = OTEs[t]
                s = t % 2
                em.wait("sync", "dve", rfree[s])
                rload[s] = em.dma("sync", rbuf[s][:],
                                  src_d[tokb + t * 128: tokb + (t + 1) * 128, :], rsem[s])
                sp = t % 2
                Pp = [H["P0"], H["P1"]][sp]
                em.wait("tensor", "dve", max(OTE, ps_dve[sp]))
                em.wait("tensor", "act", ps_act[sp])
                em.wait("tensor", "wob", wob_n)
                for half in range(2):
                    for f in range(8):
                        em.mm(Pp[:, half * 512:(half + 1) * 512],
                              A["otb"][ob][:, f * 128:(f + 1) * 128],
                              A["wout"][f][:, half * 512:(half + 1) * 512],
                              start=(f == 0), stop=False)
                    MME = em.mm(Pp[:, half * 512:(half + 1) * 512], H["ones1"][:],
                                H["brows"][0:1, 1024 + half * 512:1024 + (half + 1) * 512],
                                start=False, stop=True, inc="pe" if half == 1 else None)
                otb_free[ob] = MME
                em.wait("vector", "pe", MME)
                em.wait("vector", rsem[s], rload[s])
                em.wait("vector", "dve", ZC if t == 0 else 0)
                em.dve("tensor_scalar", A["zr"][:], rbuf[s][:],
                       H["zcA"][:, t:t + 1], H["zcB"][:, t:t + 1], ALU.mult, ALU.subtract)
                em.wait("vector", "outw", outw_gate)
                em.dve("tensor_tensor", A["r2b"][:, 0:512], A["zr"][:, 0:512],
                       Pp[:, 0:512], ALU.add)
                AD = em.dve("tensor_tensor", A["r2b"][:, 512:1024], A["zr"][:, 512:1024],
                            Pp[:, 512:1024], ALU.add, inc="dve")
                rfree[s] = AD
                ps_dve[sp] = AD
                pt_dve = pt_dve_l[0]
                if TAPS_ON and bi == 0 and t < 2:
                    em.tap(f"otb{t}", A["otb"][ob][:], gates=[("dve", OTE)])
                    em.tap(f"zr{t}", A["zr"][:], gates=[("dve", AD)])
                    em.tap(f"r2b{t}", A["r2b"][:], gates=[("dve", AD)])
                em.wait("sync", "dve", AD)
                outw_gate = em.dma("sync", dst_d[tokb + t * 128: tokb + (t + 1) * 128, :],
                                   A["r2b"][:], "outw")
        return outw_gate

    def barrier():
        marks = dict(pe=c["pe"], act=c["act"], dve=c["dve"], outw=c["outw"])
        for e in ("tensor", "scalar", "vector", "sync", "gpsimd"):
            em.wait(e, "pe", marks["pe"])
            em.wait(e, "act", marks["act"])
            em.wait(e, "dve", marks["dve"])
        em.wait("sync", "outw", marks["outw"])

    if STAGES == "ffn1":
        ffn(H["d_x"], H["d_out"], first=True)
    elif STAGES == "attn":
        ffn(H["d_x"], H["d_R1"], first=True)
        barrier()
        attn(H["d_R1"], H["d_out"])
    else:
        ffn(H["d_x"], H["d_R1"], first=True)
        barrier()
        attn(H["d_R1"], H["d_R2"])
        barrier()
        ffn(H["d_R2"], H["d_out"])

    em.wait("sync", "outw", c["outw"])


def _build():
    nc = bass.Bass("TRN2", target_bir_lowering=False, debug=False)

    H = {}
    H["d_x"] = nc.dram_tensor("x", [TOK, D], F32, kind="ExternalInput").ap()
    H["d_w12"] = nc.dram_tensor("w12", [32, 128, 2048], F32, kind="ExternalInput").ap()
    H["d_h1"] = nc.dram_tensor("h1", [32, 128, 4096], F32, kind="ExternalInput").ap()
    H["d_h2"] = nc.dram_tensor("h2", [32, 128, 4096], F32, kind="ExternalInput").ap()
    H["d_w3"] = nc.dram_tensor("w3", [8, 128, 4096], F32, kind="ExternalInput").ap()
    H["d_wqkv"] = nc.dram_tensor("wqkv", [16, 128, 1024], F32, kind="ExternalInput").ap()
    H["d_wv"] = nc.dram_tensor("wv", [8, 128, 1024], F32, kind="ExternalInput").ap()
    H["d_wout"] = nc.dram_tensor("wout", [8, 128, 1024], F32, kind="ExternalInput").ap()
    H["d_scols"] = nc.dram_tensor("scols", [128, 160], F32, kind="ExternalInput").ap()
    H["d_gam"] = nc.dram_tensor("gam", [128, 1024], F32, kind="ExternalInput").ap()
    H["d_bet"] = nc.dram_tensor("bet", [128, 1024], F32, kind="ExternalInput").ap()
    H["d_brows"] = nc.dram_tensor("brows", [1, 2048], F32, kind="ExternalInput").ap()
    H["d_idbf"] = nc.dram_tensor("idbf", [128, 128], BF16, kind="ExternalInput").ap()
    H["d_id32"] = nc.dram_tensor("id32", [128, 128], F32, kind="ExternalInput").ap()
    H["d_ones1"] = nc.dram_tensor("ones1", [1, 128], BF16, kind="ExternalInput").ap()
    H["d_sel4"] = nc.dram_tensor("sel4", [128, 32], BF16, kind="ExternalInput").ap()
    H["d_zrow"] = nc.dram_tensor("zrow", [1, 512], BF16, kind="ExternalInput").ap()
    H["d_fc"] = nc.dram_tensor("fconst", [128, 4], F32, kind="ExternalInput").ap()
    H["d_out"] = nc.dram_tensor("out", [TOK, D], F32, kind="ExternalOutput").ap()
    H["d_R1"] = nc.dram_tensor("R1", [TOK, D], F32).ap()
    H["d_R2"] = nc.dram_tensor("R2", [TOK, D], F32).ap()
    H["c_w12"] = nc.dram_tensor("c_w12", [32, 128, 2048], BF16).ap()
    H["c_h1"] = nc.dram_tensor("c_h1", [32, 128, 4096], BF16).ap()
    H["c_h2"] = nc.dram_tensor("c_h2", [32, 128, 4096], BF16).ap()
    H["c_w3"] = nc.dram_tensor("c_w3", [8, 128, 4096], BF16).ap()

    es = ExitStack()
    ARENA_KB = 200
    AR = es.enter_context(nc.sbuf_tensor("AR", [128, ARENA_KB * 1024], mybir.dt.uint8))
    arf = AR[:].bitcast(F32)
    arb = AR[:].bitcast(BF16)

    class Cursor:
        def __init__(self, off=0):
            self.off = off

        def take(self, nbytes, align=64):
            self.off = (self.off + align - 1) // align * align
            o = self.off
            self.off += nbytes
            assert self.off <= ARENA_KB * 1024, f"arena overflow {self.off}"
            return o

    def f32(cur, n):
        o = cur.take(n * 4)
        return arf[:, o // 4: o // 4 + n]

    def bf16(cur, n):
        o = cur.take(n * 2)
        return arb[:, o // 2: o // 2 + n]

    cur0 = Cursor()
    rb0 = f32(cur0, 1024)
    rb1 = f32(cur0, 1024)
    base_off = cur0.off

    cur = Cursor(base_off)
    ffnA = dict(rb0=rb0, rb1=rb1)
    ffnA["zb"] = [bf16(cur, 1024), bf16(cur, 1024)]
    ffnA["t1"] = [f32(cur, 512), f32(cur, 512), f32(cur, 512)]
    ffnA["w12a"] = bf16(cur, 2048)
    ffnA["w12b"] = bf16(cur, 2048)
    ffnA["whf0"] = bf16(cur, 2048)
    ffnA["whf1"] = bf16(cur, 2048)
    ffnA["whf2"] = bf16(cur, 2048)
    ffnA["whf3"] = bf16(cur, 2048)
    ffnA["w3h0"] = bf16(cur, 2048)
    ffnA["w3h1"] = bf16(cur, 2048)
    ffnA["outb"] = f32(cur, 1024)
    ffnA["xlnT"] = bf16(cur, 8192)
    ffnA["hA"] = bf16(cur, 32768)
    hb_off = cur.off
    ffnA["hB"] = bf16(cur, 32768)
    cur2 = Cursor(hb_off)
    ffnA["outT"] = f32(cur2, 4096)
    ffn_end = cur.off

    cur = Cursor(base_off)
    atA = dict(rb0=rb0, rb1=rb1)
    atA["tf"] = f32(cur, 1024)
    atA["zr"] = atA["tf"]
    atA["hb"] = bf16(cur, 1024)
    atA["r2b"] = f32(cur, 1024)
    atA["jb0"] = bf16(cur, 1024)
    atA["jb1"] = bf16(cur, 1024)
    vb_off = cur.off
    atA["vb"] = [bf16(cur, 1024) for _ in range(8)]
    curg = Cursor(vb_off)
    atA["gam"] = f32(curg, 1024)
    atA["bet"] = f32(curg, 1024)
    atA["wout"] = [bf16(cur, 1024) for _ in range(8)]
    atA["nb32"] = bf16(cur, 64)
    atA["qT"] = bf16(cur, 8192)
    atA["kT"] = bf16(cur, 8192)
    atA["vaug"] = bf16(cur, 16 * 1040)
    atA["osb"] = bf16(cur, 16384)
    atA["at"] = [bf16(cur, 2048), bf16(cur, 2048)]
    at2 = bf16(cur, 2048)
    atA["at6"] = [atA["at"][0][:, 0:1024], atA["at"][0][:, 1024:2048],
                  at2[:, 0:1024],
                  atA["at"][1][:, 0:1024], atA["at"][1][:, 1024:2048],
                  at2[:, 1024:2048]]
    atA["otb"] = [bf16(cur, 1024), bf16(cur, 1024)]
    atA["hT"] = bf16(cur, 16384)
    attn_end = cur.off
    assert max(ffn_end, attn_end) <= ARENA_KB * 1024, (ffn_end, attn_end)

    def small(name, shape, dt):
        return es.enter_context(nc.sbuf_tensor(name, shape, dt))

    H["scols"] = small("s_scols", [128, 160], F32)
    H["smv"] = small("smv", [128, 32], F32)
    H["st6"] = small("st6", [128, 12], F32)
    H["rstd"] = small("rstd", [128, 16], F32)
    H["rstd2"] = small("rstd2", [128, 16], F32)
    H["zcA"] = small("zcA", [128, 16], F32)
    H["zcB"] = small("zcB", [128, 16], F32)
    H["negb"] = small("negb", [128, 8], F32)
    H["rc"] = small("rc", [128, 16], F32)
    H["nsc"] = small("nsc", [32, 8], F32)
    H["s3h"] = small("s3h", [128, 8], F32)
    H["idbf"] = small("s_idbf", [128, 128], BF16)
    H["id32"] = small("s_id32", [128, 128], F32)
    H["ones1"] = small("s_ones1", [1, 128], BF16)
    H["sel4"] = small("s_sel4", [128, 32], BF16)
    H["fc"] = small("s_fc", [128, 4], F32)
    H["brows_t"] = small("s_brows", [1, 2048], BF16)
    H["zrow"] = small("s_zrow", [1, 512], BF16)

    H["PST"] = es.enter_context(nc.psum_tensor("PPST", [128, 2048], F32))
    H["P0"] = H["PST"][:, 0:1024]
    H["P1"] = H["PST"][:, 1024:2048]
    H["P2"] = es.enter_context(nc.psum_tensor("PP2", [128, 1024], F32))
    H["P3"] = es.enter_context(nc.psum_tensor("PP3", [128, 1024], F32))
    H["PT"] = H["P3"][:].bitcast(BF16)[:, 0:1024]

    H["ffn"] = ffnA
    H["attn"] = atA
    H["brows"] = H["brows_t"][:]

    semmap = {s: es.enter_context(nc.semaphore(s)) for s in _SEMS}

    dry = Walk("dry", None, semmap)
    _schedule(dry, H)
    tapdict = {}
    for (name, shape, dt) in dry.taps:
        if name not in tapdict:
            tapdict[name] = nc.dram_tensor("tap_" + name, list(shape), dt,
                                           kind="ExternalOutput").ap()

    with es:
        with nc.Block() as block:
            counts = []

            def mkpass(which):
                def f(eng):
                    w = Walk(which, eng, semmap, tapdict)
                    _schedule(w, H)
                    counts.append(dict(w.cnt))
                return f

            block.sync(mkpass("sync"))
            block.gpsimd(mkpass("gpsimd"))
            block.tensor(mkpass("tensor"))
            block.scalar(mkpass("scalar"))
            block.vector(mkpass("vector"))

            for ci in counts[1:]:
                assert ci == counts[0], "pass count mismatch"

    return nc


def _host_prep(inp):
    f = np.float32
    x = np.asarray(inp["L0_attn_x"], f)
    gam = np.asarray(inp["L0_attn_ln_gamma"], f)
    bet = np.asarray(inp["L0_attn_ln_beta"], f)
    wqkv = np.asarray(inp["L0_attn_w_qkv"], f)
    bqkv = np.asarray(inp["L0_attn_b_qkv"], f)
    wout = np.asarray(inp["L0_attn_w_out"], f)
    bout = np.asarray(inp["L0_attn_b_out"], f)
    w1 = np.asarray(inp["L0_ffn_w1_w"], f)
    w1s = np.asarray(inp["L0_ffn_w1_s"], f)
    w2 = np.asarray(inp["L0_ffn_w2_w"], f)
    w2s = np.asarray(inp["L0_ffn_w2_s"], f)
    h1 = np.asarray(inp["L0_ffn_h1_w"], f)
    h1s = np.asarray(inp["L0_ffn_h1_s"], f)
    h2 = np.asarray(inp["L0_ffn_h2_w"], f)
    h2s = np.asarray(inp["L0_ffn_h2_s"], f)
    w3 = np.asarray(inp["L0_ffn_w3_w"], f)
    w3s = np.asarray(inp["L0_ffn_w3_s"], f)

    def phtile(w, njo, nji):
        return np.ascontiguousarray(
            w.reshape(njo, 128, nji, 128).transpose(0, 3, 2, 1).reshape(njo, 128, nji * 128))

    w1t = phtile(w1, 32, 8)
    w2t = phtile(w2, 32, 8)
    w12 = np.ascontiguousarray(np.concatenate([w1t, w2t], axis=2))
    h1t = phtile(h1, 32, 32)
    h2t = phtile(h2, 32, 32)
    w3t = phtile(w3, 8, 32)
    qk = wqkv[:, 0:2048]
    wqkv_t = np.ascontiguousarray(
        qk.reshape(8, 128, 16, 128).transpose(2, 1, 0, 3).reshape(16, 128, 1024))
    wv_t = np.ascontiguousarray(wqkv[:, 2048:3072].reshape(8, 128, 1024))
    wout_t = np.ascontiguousarray(wout.reshape(8, 128, 1024))

    scols = np.zeros((128, 160), f)
    scols[:, 0:32] = w1s.reshape(32, 128).T
    scols[:, 32:64] = w2s.reshape(32, 128).T
    scols[:, 64:96] = h1s.reshape(32, 128).T
    scols[:, 96:128] = h2s.reshape(32, 128).T
    scols[:, 128:136] = w3s.reshape(8, 128).T
    scols[:, 136:152] = bqkv[0:2048].reshape(16, 128).T

    brows = np.ascontiguousarray(np.concatenate([bqkv[2048:3072], bout]).reshape(1, 2048))
    gam_rep = np.ascontiguousarray(np.broadcast_to(gam, (128, 1024)))
    bet_rep = np.ascontiguousarray(np.broadcast_to(bet, (128, 1024)))

    idbf = np.eye(128, dtype=ml_dtypes.bfloat16)
    id32 = np.eye(128, dtype=f)
    ones1 = np.ones((1, 128), ml_dtypes.bfloat16)
    sel4 = np.zeros((128, 32), ml_dtypes.bfloat16)
    for a in range(4):
        sel4[0:64, a * 8 + 2 * a] = 1.0
        sel4[64:128, a * 8 + 2 * a + 1] = 1.0

    fconst = np.zeros((128, 4), f)
    fconst[:, 0] = EPS
    fconst[:, 1] = EPS * EPS
    fconst[:, 2] = 1e-18
    zrow = np.zeros((1, 512), ml_dtypes.bfloat16)
    common = dict(w12=w12, h1=h1t, h2=h2t, w3=w3t, wqkv=wqkv_t, wv=wv_t,
                  wout=wout_t, scols=scols, gam=gam_rep, bet=bet_rep,
                  brows=brows, idbf=idbf, id32=id32, ones1=ones1, sel4=sel4,
                  fconst=fconst, zrow=zrow)
    shards = [np.ascontiguousarray(x[2 * cid:2 * cid + 2].reshape(TOK, D))
              for cid in range(8)]
    return common, shards


def kernel(**inp):
    common, shards = _host_prep(inp)
    nc = _build()
    in_maps = [dict(common, x=shards[cid]) for cid in range(8)]
    res = run_bass_kernel_spmd(nc, in_maps, core_ids=list(range(8)))
    out = np.concatenate(
        [np.asarray(res.results[cid]["out"], np.float32).reshape(2, T, D)
         for cid in range(8)], axis=0)
    return out


# version 20
# speedup vs baseline: 1.1560x; 1.1560x over previous
import numpy as np
import ml_dtypes
import concourse.bass as bass
import concourse.mybir as mybir
from concourse.bass_utils import run_bass_kernel_spmd
from contextlib import ExitStack

F32 = mybir.dt.float32
BF16 = mybir.dt.bfloat16
AF = mybir.ActivationFunctionType
ALU = mybir.AluOpType
AX = mybir.AxisListType

D = 1024
HID = 4096
T = 2048
NBATCH = 2
TOK = NBATCH * T
TB = 1024
NBLK = TOK // TB
EPS = 1e-5

STAGES = "full"

_SEMS = [
    "pe", "act", "dve",
    "cst", "rin0", "rin1", "w12a", "w12b", "whf0", "whf1", "whf2", "whf3",
    "w3h0", "w3h1", "jb0", "jb1", "vb", "wob", "row", "outw", "wbk",
]


TAPS_ON = False


class Walk:

    def __init__(self, which, engine, semmap, tapdict=None):
        self.w = which
        self.eng = engine
        self.sems = semmap
        self.cnt = {k: 0 for k in semmap}
        self.taps = []
        self.tapdict = tapdict

    def wait(self, eng, sem, val):
        if val is None or val <= 0:
            return
        assert val <= self.cnt[sem], f"wait-ahead {eng}.{sem} {val}>{self.cnt[sem]}"
        if self.w == eng:
            self.eng.wait_ge(self.sems[sem], val)

    def _fin(self, ins, inc, k):
        if inc is not None and ins is not None:
            ins.then_inc(self.sems[inc], k)

    def _bump(self, inc, k):
        if inc is None:
            return None
        self.cnt[inc] += k
        return self.cnt[inc]

    def mm(self, out, lhsT, rhs, start, stop, inc=None, k=1):
        if self.w == "tensor":
            ins = self.eng.matmul(out, lhsT, rhs, start=start, stop=stop)
            self._fin(ins, inc, k)
        return self._bump(inc, k)

    def tp(self, out, in_, ident, inc=None, k=1):
        if self.w == "tensor":
            ins = self.eng.transpose(out, in_, ident)
            self._fin(ins, inc, k)
        return self._bump(inc, k)

    def act(self, fn, *a, inc=None, k=1, **kw):
        if self.w == "scalar":
            ins = getattr(self.eng, fn)(*a, **kw)
            self._fin(ins, inc, k)
        return self._bump(inc, k)

    def dve(self, fn, *a, inc=None, k=1, **kw):
        if self.w == "vector":
            ins = getattr(self.eng, fn)(*a, **kw)
            self._fin(ins, inc, k)
        return self._bump(inc, k)

    def dma(self, eng, out, in_, inc, k=16):
        if self.w == eng:
            ins = self.eng.dma_start(out=out, in_=in_)
            self._fin(ins, inc, k)
        return self._bump(inc, k)

    def tap(self, name, ap, gates=()):
        if not TAPS_ON:
            return
        self.taps.append((name, tuple(ap.shape), ap.dtype))
        for sem, val in gates:
            self.wait("sync", sem, val)
        if self.w == "sync" and self.tapdict is not None:
            ins = self.eng.dma_start(out=self.tapdict[name][:], in_=ap)
            ins.then_inc(self.sems["outw"], 16)
        self.cnt["outw"] += 16


def _schedule(em, H):
    c = em.cnt

    em.dma("sync", H["scols"][:], H["d_scols"][:], "cst")
    em.dma("sync", H["idbf"][:], H["d_idbf"][:], "cst")
    em.dma("sync", H["id32"][:], H["d_id32"][:], "cst")
    em.dma("sync", H["ones1"][:], H["d_ones1"][:], "cst")
    em.dma("sync", H["sel4"][:], H["d_sel4"][:], "cst")
    em.dma("sync", H["fc"][:], H["d_fc"][:], "cst")
    em.dma("sync", H["zrow"][:], H["d_zrow"][:], "cst")
    ROW_N = em.dma("gpsimd", H["brows"][0:1, 0:2048], H["d_brows"][:], "row")
    CST_N = c["cst"]
    for e in ("tensor", "scalar", "vector"):
        em.wait(e, "cst", CST_N)
    em.dve("tensor_scalar", H["s3h"][:], H["scols"][:, 128:136], 0.5, None,
           ALU.mult, ALU.bypass)

    def ffn(src_d, dst_d, first=False):
        A = H["ffn"]
        rbuf = [A["rb0"], A["rb1"]]
        rsem = ["rin0", "rin1"]
        rfree = [0, 0]
        rload = [0, 0]
        outw_gate = c["outw"]
        zb_free = [0, 0]
        pt_dve_s = [0]
        last_z = [0]

        def ln_stats(b, z_gate):
            tok0 = b * TB
            for t in range(8):
                s = t % 2
                em.wait("sync", "dve", rfree[s])
                rload[s] = em.dma("sync", rbuf[s][:],
                                  src_d[tok0 + t * 128: tok0 + (t + 1) * 128, :], rsem[s])
                em.wait("vector", rsem[s], rload[s])
                em.wait("vector", "dve", z_gate if t == 0 else 0)
                em.dve("bn_stats", H["st6"][:, 0:6], rbuf[s][:, 0:512])
                em.dve("bn_stats", H["st6"][:, 6:12], rbuf[s][:, 512:1024])
                em.dve("drain")
                em.dve("bn_aggr", H["smv"][:, 2 * t:2 * t + 2], H["st6"][:])
                ST = em.dve("drain", inc="dve")
                rfree[s] = ST
            em.wait("scalar", "dve", ST)
            em.wait("scalar", "dve", z_gate)
            smv_var = H["smv"][:, 0:16].rearrange("p (t two) -> p t two", two=2)[:, :, 1]
            em.act("activation", H["rstd"][:, 0:8], smv_var, AF.Ln,
                   bias=H["fc"][:, 0:1], scale=1.0)
            RS = em.act("activation", H["rstd"][:, 0:8], H["rstd"][:, 0:8],
                        AF.Exp, scale=-0.5, inc="act")
            return RS

        def ln_apply_tile(b, RS, t, st):
            tok0 = b * TB
            pt_dve = st["pt"]
            if True:
                s = t % 2
                em.wait("sync", "dve", rfree[s])
                rload[s] = em.dma("sync", rbuf[s][:],
                                  src_d[tok0 + t * 128: tok0 + (t + 1) * 128, :], rsem[s])
                em.wait("vector", rsem[s], rload[s])
                em.wait("vector", "act", RS if t == 0 else 0)
                em.wait("vector", "pe", zb_free[t % 2])
                ZD = em.dve("tensor_scalar", A["zb"][t % 2][:], rbuf[s][:],
                            H["smv"][:, 2 * t:2 * t + 1], H["rstd"][:, t:t + 1],
                            ALU.subtract, ALU.mult, inc="dve")
                rfree[s] = ZD
                em.wait("tensor", "dve", ZD)
                em.wait("tensor", "dve", pt_dve)
                for q in range(8):
                    TPQ = em.tp(H["PT"][:, q * 128:(q + 1) * 128],
                                A["zb"][t % 2][:, q * 128:(q + 1) * 128], H["idbf"][:],
                                inc="pe" if q == 7 else None)
                zb_free[t % 2] = TPQ
                em.wait("vector", "pe", TPQ)
                dst = A["xlnT"].rearrange("p (i tt q) -> p i tt q", i=8, tt=8)[:, :, t, :]
                src = H["PT"][:].rearrange("p (i q) -> p i q", i=8)
                if t in (3, 7):
                    em.dve("tensor_copy", dst, src)
                    pt_dve = em.dve("drain", inc="dve")
                else:
                    pt_dve = em.dve("tensor_copy", dst, src, inc="dve")
                if t == 3:
                    st["half0"] = pt_dve
            st["pt"] = pt_dve
            st["ZD"] = ZD

        def ln_apply(b, RS, tiles=range(8), st=None):
            if st is None:
                st = {"pt": pt_dve_s[0], "half0": 0, "ZD": 0}
            for t in tiles:
                ln_apply_tile(b, RS, t, st)
            if max(tiles) == 7:
                pt_dve_s[0] = st["pt"]
                last_z[0] = st["ZD"]
            return st

        if first:
            A["_whfwbk"] = [0, 0, 0, 0]
        whf_state = {"load": [0, 0, 0, 0], "free": [0, 0, 0, 0],
                     "wbk": A["_whfwbk"]}
        whfsem = ["whf0", "whf1", "whf2", "whf3"]
        whfbuf = [A["whf0"], A["whf1"], A["whf2"], A["whf3"]]

        def whf_load(ci, src):
            em.wait("scalar", "pe", whf_state["free"][ci])
            em.wait("scalar", "wbk", whf_state["wbk"][ci])
            whf_state["load"][ci] = em.dma("scalar", whfbuf[ci][:], src,
                                           whfsem[ci])

        def hid_prefetch(c_d):
            for j2 in range(2):
                for hc in range(2):
                    ci = (j2 * 2 + hc) % 4
                    whf_load(ci, c_d[j2][:, hc * 2048:(hc + 1) * 2048])

        def hid_layer(w_d, c_d, cache_write, src_sb, dst_sb, scol_off,
                      src_sem, src_cnt, ph1_p2_free=0, pre=False):
            pslot_free = [0, 0, 0]
            pregs = [H["P0"], H["P1"], H["P2"]]
            GE = 0
            for j2 in range(32):
                for hc in range(2):
                    ci = (j2 * 2 + hc) % 4
                    if pre and j2 < 2:
                        continue
                    if cache_write:
                        em.wait("gpsimd", "pe", whf_state["free"][ci])
                        em.wait("gpsimd", "wbk", whf_state["wbk"][ci])
                        whf_state["load"][ci] = em.dma(
                            "gpsimd", whfbuf[ci][:],
                            w_d[j2][:, hc * 2048:(hc + 1) * 2048], whfsem[ci])
                        em.wait("sync", whfsem[ci], whf_state["load"][ci])
                        whf_state["wbk"][ci] = em.dma(
                            "sync", c_d[j2][:, hc * 2048:(hc + 1) * 2048],
                            whfbuf[ci][:], "wbk")
                    else:
                        whf_load(ci, c_d[j2][:, hc * 2048:(hc + 1) * 2048])
                sp = j2 % 3
                Ph = pregs[sp]
                em.wait("tensor", "act", pslot_free[sp])
                em.wait("tensor", src_sem, src_cnt)
                em.wait("tensor", "dve", ph1_p2_free if j2 < 3 else 0)
                for half in range(2):
                    for i in range(32):
                        ci = (j2 * 2 + (i // 16)) % 4
                        if i in (0, 16):
                            em.wait("tensor", whfsem[ci], whf_state["load"][ci])
                        rhs = src_sb[:, i * 1024 + half * 512: i * 1024 + (half + 1) * 512]
                        MME = em.mm(Ph[:, half * 512:(half + 1) * 512],
                                    whfbuf[ci][:, (i % 16) * 128:((i % 16) + 1) * 128],
                                    rhs, start=(i == 0), stop=(i == 31),
                                    inc="pe" if (i == 31 or (half == 1 and i == 15))
                                    else None)
                        if half == 1 and i == 15:
                            whf_state["free"][(j2 * 2) % 4] = MME
                    if half == 1:
                        whf_state["free"][(j2 * 2 + 1) % 4] = MME
                em.wait("scalar", "pe", MME)
                dsth = dst_sb[:, j2 * 1024:(j2 + 1) * 1024]
                GE = em.act("activation", dsth, Ph[:], AF.Gelu,
                            scale=H["scols"][:, scol_off + j2:scol_off + j2 + 1],
                            inc="act")
                pslot_free[sp] = GE
            return GE

        stats_ready = ln_stats(0, 0)
        apply_ready = ln_apply(0, stats_ready)
        for b in range(NBLK):
            tok0 = b * TB
            xln_half0, XLN_OK = apply_ready["half0"], apply_ready["pt"]

            if not (first and b == 0):
                hid_prefetch(H["c_h1"])
            w12sem = ["w12a", "w12b"]
            w12buf = [A["w12a"], A["w12b"]]
            w12load = [0, 0]
            w12free = [0, 0]
            if first:
                A["_w12wbk"] = [0, 0]
            w12wbk = A["_w12wbk"]
            slot_free = [0, 0, 0]
            t1free = [0, 0, 0]
            SF = 0
            for j in range(32):
                sj = j % 2
                if first and b == 0:
                    em.wait("gpsimd", "pe", w12free[sj])
                    em.wait("gpsimd", "wbk", w12wbk[sj])
                    w12load[sj] = em.dma("gpsimd", w12buf[sj][:], H["d_w12"][j],
                                         w12sem[sj])
                    em.wait("sync", w12sem[sj], w12load[sj])
                    w12wbk[sj] = em.dma("sync", H["c_w12"][j], w12buf[sj][:], "wbk")
                else:
                    em.wait("scalar", "pe", w12free[sj])
                    em.wait("scalar", "wbk", w12wbk[sj])
                    w12load[sj] = em.dma("scalar", w12buf[sj][:], H["c_w12"][j],
                                         w12sem[sj])
                for half in range(2):
                    sp = (j * 2 + half) % 3
                    Ps = [H["P0"], H["P1"], H["P2"]][sp]
                    em.wait("tensor", w12sem[sj], w12load[sj])
                    em.wait("tensor", "dve",
                            max(xln_half0 if half == 0 else XLN_OK, slot_free[sp]))
                    for i in range(8):
                        rhs = A["xlnT"][:, i * 1024 + half * 512: i * 1024 + (half + 1) * 512]
                        em.mm(Ps[:, 0:512], w12buf[sj][:, i * 128:(i + 1) * 128], rhs,
                              start=(i == 0), stop=(i == 7))
                    for i in range(8):
                        rhs = A["xlnT"][:, i * 1024 + half * 512: i * 1024 + (half + 1) * 512]
                        MME = em.mm(Ps[:, 512:1024],
                                    w12buf[sj][:, 1024 + i * 128:1024 + (i + 1) * 128], rhs,
                                    start=(i == 0), stop=(i == 7),
                                    inc="pe" if i == 7 else None)
                    if half == 1:
                        w12free[sj] = MME
                    em.wait("scalar", "pe", MME)
                    em.wait("scalar", "dve", t1free[sp])
                    SIL = em.act("activation", A["t1"][sp][:], Ps[:, 0:512], AF.Silu,
                                 scale=H["scols"][:, j:j + 1], inc="act")
                    em.wait("vector", "act", SIL)
                    dsth = A["hA"][:, j * 1024 + half * 512: j * 1024 + (half + 1) * 512]
                    SF = em.dve("scalar_tensor_tensor", dsth, Ps[:, 512:1024],
                                H["scols"][:, 32 + j:32 + j + 1], A["t1"][sp][:],
                                ALU.mult, ALU.mult, inc="dve")
                    slot_free[sp] = SF
                    t1free[sp] = SF
            PH1_OK = SF

            PH2_OK = hid_layer(H["d_h1"], H["c_h1"], first and b == 0,
                               A["hA"], A["hB"], 64, "dve", PH1_OK,
                               ph1_p2_free=PH1_OK,
                               pre=not (first and b == 0))
            if b + 1 < NBLK:
                stats_ready = ln_stats(b + 1, last_z[0])
            PH3_OK = hid_layer(H["d_h2"], H["c_h2"], first and b == 0,
                               A["hB"], A["hA"], 96, "act", PH2_OK)

            w3sem = ["w3h0", "w3h1"]
            w3buf = [A["w3h0"], A["w3h1"]]
            w3load = [0, 0]
            w3free = [0, 0]
            if first:
                A["_w3wbk"] = [0, 0]
            w3wbk = A["_w3wbk"]
            ph4_slot_free = [0, 0, 0]
            p2_gate = 0
            for half in range(2):
                if half == 1 and b + 1 < NBLK:
                    ap_st = {"pt": pt_dve_s[0], "half0": 0, "ZD": 0}
                EV = 0
                for j3 in range(8):
                    if half == 1 and b + 1 < NBLK and j3 >= 1:
                        apply_ready = ln_apply(b + 1, stats_ready,
                                               tiles=[j3 - 1], st=ap_st)
                    for hc in range(2):
                        ci = (j3 * 2 + hc) % 2
                        if first and b == 0 and half == 0:
                            em.wait("gpsimd", "pe", w3free[ci])
                            em.wait("gpsimd", "wbk", w3wbk[ci])
                            w3load[ci] = em.dma(
                                "gpsimd", w3buf[ci][:],
                                H["d_w3"][j3][:, hc * 2048:(hc + 1) * 2048], w3sem[ci])
                            em.wait("sync", w3sem[ci], w3load[ci])
                            w3wbk[ci] = em.dma(
                                "sync", H["c_w3"][j3][:, hc * 2048:(hc + 1) * 2048],
                                w3buf[ci][:], "wbk")
                        else:
                            em.wait("scalar", "pe", w3free[ci])
                            em.wait("scalar", "wbk", w3wbk[ci])
                            w3load[ci] = em.dma(
                                "scalar", w3buf[ci][:],
                                H["c_w3"][j3][:, hc * 2048:(hc + 1) * 2048], w3sem[ci])
                    sp = j3 % 3
                    Pp = [H["P0"][:, 0:512], H["P1"][:, 0:512],
                          H["P0"][:, 512:1024]][sp]
                    em.wait("tensor", "dve", ph4_slot_free[sp])
                    em.wait("tensor", "act", PH3_OK)
                    for i in range(32):
                        ci = (j3 * 2 + (i // 16)) % 2
                        if i in (0, 16):
                            em.wait("tensor", w3sem[ci], w3load[ci])
                        rhs = A["hA"][:, i * 1024 + half * 512: i * 1024 + (half + 1) * 512]
                        MME = em.mm(Pp, w3buf[ci][:, (i % 16) * 128:((i % 16) + 1) * 128],
                                    rhs, start=(i == 0), stop=(i == 31),
                                    inc="pe" if (i == 31 or i == 15) else None)
                        if i == 15:
                            w3free[(j3 * 2) % 2] = MME
                    w3free[(j3 * 2 + 1) % 2] = MME
                    em.wait("vector", "pe", MME)
                    EV = em.dve("tensor_scalar", A["outT"][:, j3 * 512:(j3 + 1) * 512], Pp,
                                H["s3h"][:, j3:j3 + 1], None, ALU.mult, ALU.bypass,
                                inc="dve")
                    ph4_slot_free[sp] = EV
                if half == 1 and b + 1 < NBLK:
                    apply_ready = ln_apply(b + 1, stats_ready,
                                           tiles=[7], st=ap_st)
                OUTT_OK = EV
                for q in range(4):
                    t = half * 4 + q
                    s = t % 2
                    em.wait("sync", "dve", rfree[s])
                    rload[s] = em.dma("sync", rbuf[s][:],
                                      src_d[tok0 + t * 128: tok0 + (t + 1) * 128, :],
                                      rsem[s])
                    em.wait("tensor", "dve", max(OUTT_OK, p2_gate))
                    for j3 in range(8):
                        TPQ = em.tp(H["P2"][:, j3 * 128:(j3 + 1) * 128],
                                    A["outT"][:, j3 * 512 + q * 128: j3 * 512 + (q + 1) * 128],
                                    H["id32"][:], inc="pe" if j3 == 7 else None)
                    em.wait("vector", "pe", TPQ)
                    em.wait("vector", rsem[s], rload[s])
                    em.wait("vector", "outw", outw_gate)
                    AD = em.dve("tensor_tensor", A["outb"][:], rbuf[s][:], H["P2"][:],
                                ALU.add, inc="dve")
                    rfree[s] = AD
                    p2_gate = AD
                    em.wait("sync", "dve", AD)
                    outw_gate = em.dma("sync",
                                       dst_d[tok0 + t * 128: tok0 + (t + 1) * 128, :],
                                       A["outb"][:], "outw")
        return outw_gate

    def attn(src_d, dst_d):
        def stage_barrier():
            marks = dict(pe=c["pe"], act=c["act"], dve=c["dve"])
            for e in ("tensor", "scalar", "vector", "sync", "gpsimd"):
                em.wait(e, "pe", marks["pe"])
                em.wait(e, "act", marks["act"])
                em.wait(e, "dve", marks["dve"])
        A = H["attn"]
        rbuf = [A["rb0"], A["rb1"]]
        rsem = ["rin0", "rin1"]
        outw_gate = c["outw"]
        at_done = {(s, p): 0 for s in range(2) for p in range(3)}
        wob_n = 0
        em.wait("gpsimd", "pe", c["pe"])
        for f in range(8):
            wob_n = em.dma("gpsimd", A["wout"][f][:], H["d_wout"][f], "wob")
        em.dma("sync", A["gam"][:], H["d_gam"][:], "cst")
        GBL = em.dma("sync", A["bet"][:], H["d_bet"][:], "cst")
        vload = 0

        for bi in range(NBATCH):
            tokb = bi * T
            rfree = [0, 0]
            rload = [0, 0]
            ps_act = [0, 0]
            ps_dve = [0, 0]
            pq_dve = 0
            pt_dve = 0

            em.wait("gpsimd", "pe", c["pe"])
            for i in range(8):
                vload = em.dma("gpsimd", A["vb"][i][:], H["d_wv"][i], "vb")
            va4 = A["vaug"].rearrange("p (t h cc) -> p t h cc", t=16, h=16)
            em.wait("vector", "pe", c["pe"])
            VMS = em.dve("memset", va4[:, :, :, 64:65], 1.0, inc="dve")
            for t in range(16):
                s = t % 2
                em.wait("sync", "dve", rfree[s])
                rload[s] = em.dma("sync", rbuf[s][:],
                                  src_d[tokb + t * 128: tokb + (t + 1) * 128, :], rsem[s])
                em.wait("vector", rsem[s], rload[s])
                em.dve("bn_stats", H["st6"][:, 0:6], rbuf[s][:, 0:512])
                em.dve("bn_stats", H["st6"][:, 6:12], rbuf[s][:, 512:1024])
                em.dve("drain")
                em.dve("bn_aggr", H["smv"][:, 2 * t:2 * t + 2], H["st6"][:])
                ST = em.dve("drain", inc="dve")
                em.wait("scalar", "dve", ST)
                em.act("activation", H["rstd"][:, t:t + 1], H["smv"][:, 2 * t + 1:2 * t + 2],
                       AF.Ln, bias=H["fc"][:, 0:1], scale=1.0)
                em.act("activation", H["rstd"][:, t:t + 1], H["rstd"][:, t:t + 1],
                       AF.Exp, scale=-0.5)
                em.act("activation", H["rstd2"][:, t:t + 1], H["smv"][:, 2 * t + 1:2 * t + 2],
                       AF.Ln, bias=H["fc"][:, 1:2], scale=float(1.0 + EPS))
                RS = em.act("activation", H["rstd2"][:, t:t + 1], H["rstd2"][:, t:t + 1],
                            AF.Exp, scale=-0.5, inc="act")
                em.wait("vector", "act", RS)
                em.wait("vector", "cst", GBL)
                em.dve("tensor_scalar", A["tf"][:], rbuf[s][:],
                       H["smv"][:, 2 * t:2 * t + 1], H["rstd2"][:, t:t + 1],
                       ALU.subtract, ALU.mult)
                em.dve("tensor_tensor", A["tf"][:], A["tf"][:], A["gam"][:], ALU.mult)
                HB = em.dve("tensor_tensor", A["hb"][:], A["tf"][:], A["bet"][:],
                            ALU.add, inc="dve")
                rfree[s] = HB
                if bi == 0 and t == 0:
                    em.tap("hb0", A["hb"][:], gates=[("dve", HB)])
                    em.tap("smv0", H["smv"][:], gates=[("dve", HB)])
                    em.tap("rstd20", H["rstd2"][:], gates=[("act", RS)])
                em.wait("tensor", "dve", max(HB, pt_dve))
                for q in range(8):
                    TPQ = em.tp(H["PT"][:, q * 128:(q + 1) * 128],
                                A["hb"][:, q * 128:(q + 1) * 128], H["idbf"][:],
                                inc="pe" if q == 7 else None)
                em.wait("vector", "pe", TPQ)
                dst = A["hT"].rearrange("p (i tt q) -> p i tt q", i=8, tt=16)[:, :, t, :]
                src = H["PT"][:].rearrange("p (i q) -> p i q", i=8)
                em.dve("tensor_copy", dst, src)
                pt_dve = em.dve("drain", inc="dve")
                sp = t % 2
                Ps = [H["P0"], H["P1"]][sp]
                em.wait("tensor", "vb", vload)
                em.wait("tensor", "row", ROW_N)
                em.wait("tensor", "act", ps_act[sp])
                em.wait("tensor", "dve", max(ps_dve[sp], pt_dve))
                for i in range(8):
                    for hh in range(2):
                        em.mm(Ps[:, hh * 512:(hh + 1) * 512],
                              A["hT"][:, i * 2048 + t * 128: i * 2048 + (t + 1) * 128],
                              A["vb"][i][:, hh * 512:(hh + 1) * 512],
                              start=(i == 0), stop=False)
                for hh in range(2):
                    MME = em.mm(Ps[:, hh * 512:(hh + 1) * 512],
                                H["ones1"][:],
                                H["brows"][0:1, hh * 512:(hh + 1) * 512],
                                start=False, stop=True,
                                inc="pe" if hh == 1 else None)
                em.wait("vector", "pe", MME)
                em.wait("vector", "dve", VMS)
                vsrc = Ps[:].rearrange("p (h cc) -> p h cc", h=16)
                em.dve("tensor_copy", va4[:, t, :, 0:64], vsrc)
                VE = em.dve("drain", inc="dve")
                ps_dve[sp] = VE
            if bi == 0:
                em.tap("hT0", A["hT"][:, 0:2048], gates=[("dve", pt_dve)])
            em.dve("tensor_scalar", H["zcA"][:], H["rstd"][:, 0:16], 1.0, None,
                   ALU.add, ALU.bypass)
            ZC = em.dve("tensor_tensor", H["zcB"][:],
                        H["smv"][:].rearrange("p (t two) -> p t two", two=2)[:, :, 0],
                        H["rstd"][:, 0:16], ALU.mult, inc="dve")

            v_done = ZC
            stage_barrier()
            for r in range(2):
                qj = [4 * r + a for a in range(4)]
                kj = [8 + 4 * r + a for a in range(4)]
                jsem = ["jb0", "jb1"]
                jbuf = [A["jb0"], A["jb1"]]
                jload = [0, 0]
                jfree = [0, 0]
                for idx, j in enumerate(qj + kj):
                    sj = idx % 2
                    em.wait("gpsimd", "pe", jfree[sj])
                    jload[sj] = em.dma("gpsimd", jbuf[sj][:], H["d_wqkv"][j], jsem[sj])
                    if TAPS_ON and bi == 0 and r == 1 and idx == 7:
                        em.tap("jb15", jbuf[sj][:], gates=[(jsem[sj], jload[sj])])
                    a = idx % 4
                    dst_sb = A["qT"] if idx < 4 else A["kT"]
                    for half in range(2):
                        sp = (idx * 2 + half) % 2
                        Ps = [H["P0"], H["P1"]][sp]
                        em.wait("tensor", jsem[sj], jload[sj])
                        em.wait("tensor", "dve", pt_dve)
                        em.wait("tensor", "act", ps_act[sp])
                        em.wait("tensor", "dve", ps_dve[sp])
                        for qtr in range(2):
                            for i in range(8):
                                rhs = A["hT"][:, i * 2048 + half * 1024 + qtr * 512:
                                              i * 2048 + half * 1024 + (qtr + 1) * 512]
                                MME = em.mm(Ps[:, qtr * 512:(qtr + 1) * 512],
                                            jbuf[sj][:, i * 128:(i + 1) * 128], rhs,
                                            start=(i == 0), stop=(i == 7),
                                            inc="pe" if (qtr == 1 and i == 7) else None)
                        if half == 1:
                            jfree[sj] = MME
                        if TAPS_ON and bi == 0 and r == 1 and idx == 7 and half == 1:
                            em.wait("vector", "pe", MME)
                            PQ_ = em.dve("tensor_copy", A["r2b"][:], Ps[:], inc="dve")
                            em.tap("k15psum", A["r2b"][:], gates=[("dve", PQ_)])
                            em.wait("scalar", "dve", PQ_)
                        em.wait("scalar", "pe", MME)
                        em.act("activation",
                               dst_sb[:, a * 2048 + half * 1024:
                                      a * 2048 + (half + 1) * 1024],
                               Ps[:], AF.Identity,
                               bias=H["scols"][:, 136 + j:136 + j + 1], scale=1.0)
                        QKE = em.act("drain", inc="act")
                        ps_act[sp] = QKE
                        if bi == 0 and idx == 0 and half == 1:
                            em.tap("qT0", A["qT"][:, 0:2048], gates=[("act", QKE)])
                        if bi == 0 and idx == 4 and half == 1:
                            em.tap("kT0", A["kT"][:, 0:2048], gates=[("act", QKE)])
                        if bi == 0 and r == 1 and idx == 3 and half == 1:
                            em.tap("qT3r1", A["qT"][:, 3 * 2048:4 * 2048], gates=[("act", QKE)])
                        if bi == 0 and r == 1 and idx == 7 and half == 1:
                            em.tap("kT3r1", A["kT"][:, 3 * 2048:4 * 2048], gates=[("act", QKE)])

                stage_barrier()
                stage_barrier()
                nsc = H["nsc"]
                for side in range(2):
                    src_sb = A["qT"] if side == 0 else A["kT"]
                    Mcol = nsc[0:8, 1 + 2 * side:2 + 2 * side]
                    regs = [H["P2"][0:8, 0:512], H["P2"][0:8, 512:1024],
                            H["P3"][0:8, 0:512], H["P0"][0:8, 0:512]]
                    sq_free = [0, 0]
                    for a in range(4):
                        sb = a % 2
                        em.wait("vector", "pe", sq_free[sb])
                        SQ = em.dve("tensor_tensor", A["at"][sb][:],
                                    src_sb[:, a * 2048:(a + 1) * 2048],
                                    src_sb[:, a * 2048:(a + 1) * 2048], ALU.mult,
                                    inc="dve")
                        em.wait("tensor", "dve", SQ)
                        if a == 0:
                            em.wait("tensor", "dve", pq_dve)
                            em.wait("tensor", "act", ps_act[0])
                            em.wait("tensor", "dve", ps_dve[0])
                        for ch in range(4):
                            MME = em.mm(regs[ch], H["sel4"][:, a * 8:(a + 1) * 8],
                                        A["at"][sb][:, ch * 512:(ch + 1) * 512],
                                        start=(a == 0), stop=(a == 3),
                                        inc="pe" if ch == 3 else None)
                        sq_free[sb] = MME
                    em.wait("vector", "pe", MME)
                    for ch in range(4):
                        em.dve("reduce_max", nsc[0:8, 0:1], regs[ch], axis=AX.X)
                        em.dve("drain")
                        if ch == 0:
                            em.dve("tensor_copy", Mcol, nsc[0:8, 0:1])
                        else:
                            em.dve("tensor_tensor", Mcol, Mcol, nsc[0:8, 0:1], ALU.max)
                        em.dve("drain")
                    NR = em.dve("drain", inc="dve")
                    pq_dve = NR
                    ps_dve[0] = NR
                em.dve("tensor_tensor", nsc[0:8, 5:6], nsc[0:8, 1:2], nsc[0:8, 3:4],
                       ALU.mult)
                NP = em.dve("drain", inc="dve")
                em.wait("scalar", "dve", NP)
                em.act("activation", nsc[0:8, 6:7], nsc[0:8, 5:6], AF.Ln,
                       bias=H["fc"][0:8, 2:3], scale=1.0)
                NB_ = em.act("activation", nsc[0:8, 6:7], nsc[0:8, 6:7], AF.Exp,
                             scale=0.5, inc="act")
                em.wait("vector", "act", NB_)
                em.dve("tensor_scalar", nsc[0:8, 7:8], nsc[0:8, 6:7], -0.125, None,
                       ALU.mult, ALU.bypass)
                em.dve("drain")
                em.dve("memset", A["nb32"][0:32, 0:64], 0.0)
                em.dve("tensor_copy", A["nb32"][0:8, 0:1], nsc[0:8, 7:8])
                em.dve("drain")
                em.dve("transpose", A["nb32"][0:32, 32:64], A["nb32"][0:32, 0:32])
                NBT = em.dve("drain", inc="dve")
                em.wait("tensor", "dve", NBT)
                em.wait("tensor", "dve", ps_dve[0])
                em.wait("tensor", "act", ps_act[0])
                BC = em.mm(H["P0"][:, 0:8], H["ones1"][:], A["nb32"][0:1, 32:40],
                           start=True, stop=True, inc="pe")
                em.wait("vector", "pe", BC)
                em.dve("tensor_copy", H["negb"][:], H["P0"][:, 0:8])
                NBB = em.dve("drain", inc="dve")
                ps_dve[0] = NBB
                if bi == 0 and r == 0:
                    em.tap("nsc0", H["nsc"][:], gates=[("dve", NBB)])
                    em.tap("negb0", H["negb"][:], gates=[("dve", NBB)])

                stage_barrier()
                for pa in range(4):
                    heads = [8 * r + 2 * pa, 8 * r + 2 * pa + 1]
                    for qhalf in range(2):
                        obase = [H["P2"], H["P3"]]

                        def qoff(s, qc):
                            if qc < 7:
                                return obase[s][:, qc * 65:(qc + 1) * 65]
                            return obase[s][:, 512:577]

                        def score(s, kt):
                            hh = 2 * pa + s
                            pb = (hh % 2) * 64
                            Ps = [H["P0"], H["P1"]][s]
                            em.wait("tensor", "act", ps_act[s])
                            em.wait("tensor", "dve", ps_dve[s])
                            for qtr in range(2):
                                MME = em.mm(
                                    Ps[:, qtr * 512:(qtr + 1) * 512],
                                    A["kT"][pb:pb + 64,
                                            pa * 2048 + kt * 128: pa * 2048 + (kt + 1) * 128],
                                    A["qT"][pb:pb + 64,
                                            pa * 2048 + qhalf * 1024 + qtr * 512:
                                            pa * 2048 + qhalf * 1024 + (qtr + 1) * 512],
                                    start=True, stop=True,
                                    inc="pe" if qtr == 1 else None)
                            return MME

                        def expf(s, kt):
                            hh = 2 * pa + s
                            em.wait("scalar", "pe", max(sc_mme[s], at_done[(s, kt % 3)]))
                            if pa == 0 and qhalf == 0 and kt == 0:
                                em.wait("scalar", "dve", NBB)
                            EX = em.act("activation", A["at6"][3 * s + kt % 3][:],
                                        [H["P0"], H["P1"]][s][:], AF.Exp,
                                        bias=H["negb"][:, hh:hh + 1], scale=0.125,
                                        inc="act")
                            ps_act[s] = EX

                        def attnv(s, kt2):
                            hg = 8 * r + 2 * pa + s
                            atb2 = A["at6"][3 * s + kt2 % 3]
                            for qc in range(8):
                                M2 = em.mm(qoff(s, qc), atb2[:, qc * 128:(qc + 1) * 128],
                                           A["vaug"][:, kt2 * 1040 + hg * 65:
                                                     kt2 * 1040 + (hg + 1) * 65],
                                           start=False, stop=(kt2 == 15),
                                           inc="pe" if qc == 7 else None)
                            at_done[(s, kt2 % 3)] = M2

                        sc_mme = [0, 0]
                        for kt in range(16):
                            sc_mme[0] = score(0, kt)
                            sc_mme[1] = score(1, kt)
                            expf(0, kt)
                            expf(1, kt)
                            if kt == 2:
                                em.wait("tensor", "dve", max(pq_dve, v_done))
                                for s in range(2):
                                    em.mm(obase[s][:, 0:455], H["ones1"][:],
                                          H["zrow"][0:1, 0:455], start=True, stop=False)
                                    em.mm(obase[s][:, 512:577], H["ones1"][:],
                                          H["zrow"][0:1, 0:65], start=True, stop=False)
                            if kt >= 2:
                                attnv(0, kt - 2)
                                attnv(1, kt - 2)
                        attnv(0, 14)
                        attnv(1, 14)
                        em.wait("tensor", "act", ps_act[0])
                        attnv(0, 15)
                        em.wait("tensor", "act", ps_act[1])
                        attnv(1, 15)
                        em.wait("vector", "pe", at_done[(1, 15 % 3)])
                        for s in range(2):
                            hg = 8 * r + 2 * pa + s
                            em.dve("reciprocal", H["rc"][:, s * 8:s * 8 + 7],
                                   obase[s][:, 64:512:65])
                            em.dve("reciprocal", H["rc"][:, s * 8 + 7:s * 8 + 8],
                                   obase[s][:, 576:577])
                            em.dve("drain")
                            for qc in range(8):
                                gq = qhalf * 8 + qc
                                em.dve("tensor_scalar",
                                       A["osb"][:, gq * 1024 + hg * 64:
                                                gq * 1024 + (hg + 1) * 64],
                                       qoff(s, qc)[:, 0:64],
                                       H["rc"][:, s * 8 + qc:s * 8 + qc + 1], None,
                                       ALU.mult, ALU.bypass)
                        ON = em.dve("drain", inc="dve")
                        pq_dve = ON

            stage_barrier()
            if TAPS_ON and bi == 0:
                em.tap("osbP", A["osb"][:, 0:1024], gates=[("dve", pq_dve)])
                em.tap("osbP8", A["osb"][:, 8 * 1024:9 * 1024], gates=[("dve", pq_dve)])
            otb_free = [0, 0]
            OTEs = [0] * 17

            def proj_tp(t2):
                ob2 = t2 % 2
                em.wait("tensor", "dve", max(pq_dve if t2 == 0 else 0, pt_dve_l[0]))
                for f in range(8):
                    TPQ = em.tp(H["PT"][:, f * 128:(f + 1) * 128],
                                A["osb"][:, t2 * 1024 + f * 128: t2 * 1024 + (f + 1) * 128],
                                H["idbf"][:], inc="pe" if f == 7 else None)
                em.wait("vector", "pe", max(TPQ, otb_free[ob2]))
                OTEs[t2] = em.dve("tensor_copy", A["otb"][ob2][:], H["PT"][:], inc="dve")
                pt_dve_l[0] = OTEs[t2]

            pt_dve_l = [pt_dve]
            proj_tp(0)
            for t in range(16):
                ob = t % 2
                if t + 1 < 16:
                    proj_tp(t + 1)
                OTE = OTEs[t]
                s = t % 2
                em.wait("sync", "dve", rfree[s])
                rload[s] = em.dma("sync", rbuf[s][:],
                                  src_d[tokb + t * 128: tokb + (t + 1) * 128, :], rsem[s])
                sp = t % 2
                Pp = [H["P0"], H["P1"]][sp]
                em.wait("tensor", "dve", max(OTE, ps_dve[sp]))
                em.wait("tensor", "act", ps_act[sp])
                em.wait("tensor", "wob", wob_n)
                for half in range(2):
                    for f in range(8):
                        em.mm(Pp[:, half * 512:(half + 1) * 512],
                              A["otb"][ob][:, f * 128:(f + 1) * 128],
                              A["wout"][f][:, half * 512:(half + 1) * 512],
                              start=(f == 0), stop=False)
                    MME = em.mm(Pp[:, half * 512:(half + 1) * 512], H["ones1"][:],
                                H["brows"][0:1, 1024 + half * 512:1024 + (half + 1) * 512],
                                start=False, stop=True, inc="pe" if half == 1 else None)
                otb_free[ob] = MME
                em.wait("vector", "pe", MME)
                em.wait("vector", rsem[s], rload[s])
                em.wait("vector", "dve", ZC if t == 0 else 0)
                em.dve("tensor_scalar", A["zr"][:], rbuf[s][:],
                       H["zcA"][:, t:t + 1], H["zcB"][:, t:t + 1], ALU.mult, ALU.subtract)
                em.wait("vector", "outw", outw_gate)
                em.dve("tensor_tensor", A["r2b"][:, 0:512], A["zr"][:, 0:512],
                       Pp[:, 0:512], ALU.add)
                AD = em.dve("tensor_tensor", A["r2b"][:, 512:1024], A["zr"][:, 512:1024],
                            Pp[:, 512:1024], ALU.add, inc="dve")
                rfree[s] = AD
                ps_dve[sp] = AD
                pt_dve = pt_dve_l[0]
                if TAPS_ON and bi == 0 and t < 2:
                    em.tap(f"otb{t}", A["otb"][ob][:], gates=[("dve", OTE)])
                    em.tap(f"zr{t}", A["zr"][:], gates=[("dve", AD)])
                    em.tap(f"r2b{t}", A["r2b"][:], gates=[("dve", AD)])
                em.wait("sync", "dve", AD)
                outw_gate = em.dma("sync", dst_d[tokb + t * 128: tokb + (t + 1) * 128, :],
                                   A["r2b"][:], "outw")
        return outw_gate

    def barrier():
        marks = dict(pe=c["pe"], act=c["act"], dve=c["dve"], outw=c["outw"])
        for e in ("tensor", "scalar", "vector", "sync", "gpsimd"):
            em.wait(e, "pe", marks["pe"])
            em.wait(e, "act", marks["act"])
            em.wait(e, "dve", marks["dve"])
        em.wait("sync", "outw", marks["outw"])

    if STAGES == "ffn1":
        ffn(H["d_x"], H["d_out"], first=True)
    elif STAGES == "attn":
        ffn(H["d_x"], H["d_R1"], first=True)
        barrier()
        attn(H["d_R1"], H["d_out"])
    else:
        ffn(H["d_x"], H["d_R1"], first=True)
        barrier()
        attn(H["d_R1"], H["d_R2"])
        barrier()
        ffn(H["d_R2"], H["d_out"])

    em.wait("sync", "outw", c["outw"])


def _build():
    nc = bass.Bass("TRN2", target_bir_lowering=False, debug=False)

    H = {}
    H["d_x"] = nc.dram_tensor("x", [TOK, D], F32, kind="ExternalInput").ap()
    H["d_w12"] = nc.dram_tensor("w12", [32, 128, 2048], F32, kind="ExternalInput").ap()
    H["d_h1"] = nc.dram_tensor("h1", [32, 128, 4096], F32, kind="ExternalInput").ap()
    H["d_h2"] = nc.dram_tensor("h2", [32, 128, 4096], F32, kind="ExternalInput").ap()
    H["d_w3"] = nc.dram_tensor("w3", [8, 128, 4096], F32, kind="ExternalInput").ap()
    H["d_wqkv"] = nc.dram_tensor("wqkv", [16, 128, 1024], F32, kind="ExternalInput").ap()
    H["d_wv"] = nc.dram_tensor("wv", [8, 128, 1024], F32, kind="ExternalInput").ap()
    H["d_wout"] = nc.dram_tensor("wout", [8, 128, 1024], F32, kind="ExternalInput").ap()
    H["d_scols"] = nc.dram_tensor("scols", [128, 160], F32, kind="ExternalInput").ap()
    H["d_gam"] = nc.dram_tensor("gam", [128, 1024], F32, kind="ExternalInput").ap()
    H["d_bet"] = nc.dram_tensor("bet", [128, 1024], F32, kind="ExternalInput").ap()
    H["d_brows"] = nc.dram_tensor("brows", [1, 2048], F32, kind="ExternalInput").ap()
    H["d_idbf"] = nc.dram_tensor("idbf", [128, 128], BF16, kind="ExternalInput").ap()
    H["d_id32"] = nc.dram_tensor("id32", [128, 128], F32, kind="ExternalInput").ap()
    H["d_ones1"] = nc.dram_tensor("ones1", [1, 128], BF16, kind="ExternalInput").ap()
    H["d_sel4"] = nc.dram_tensor("sel4", [128, 32], BF16, kind="ExternalInput").ap()
    H["d_zrow"] = nc.dram_tensor("zrow", [1, 512], BF16, kind="ExternalInput").ap()
    H["d_fc"] = nc.dram_tensor("fconst", [128, 4], F32, kind="ExternalInput").ap()
    H["d_out"] = nc.dram_tensor("out", [TOK, D], F32, kind="ExternalOutput").ap()
    H["d_R1"] = nc.dram_tensor("R1", [TOK, D], F32).ap()
    H["d_R2"] = nc.dram_tensor("R2", [TOK, D], F32).ap()
    H["c_w12"] = nc.dram_tensor("c_w12", [32, 128, 2048], BF16).ap()
    H["c_h1"] = nc.dram_tensor("c_h1", [32, 128, 4096], BF16).ap()
    H["c_h2"] = nc.dram_tensor("c_h2", [32, 128, 4096], BF16).ap()
    H["c_w3"] = nc.dram_tensor("c_w3", [8, 128, 4096], BF16).ap()

    es = ExitStack()
    ARENA_KB = 200
    AR = es.enter_context(nc.sbuf_tensor("AR", [128, ARENA_KB * 1024], mybir.dt.uint8))
    arf = AR[:].bitcast(F32)
    arb = AR[:].bitcast(BF16)

    class Cursor:
        def __init__(self, off=0):
            self.off = off

        def take(self, nbytes, align=64):
            self.off = (self.off + align - 1) // align * align
            o = self.off
            self.off += nbytes
            assert self.off <= ARENA_KB * 1024, f"arena overflow {self.off}"
            return o

    def f32(cur, n):
        o = cur.take(n * 4)
        return arf[:, o // 4: o // 4 + n]

    def bf16(cur, n):
        o = cur.take(n * 2)
        return arb[:, o // 2: o // 2 + n]

    cur0 = Cursor()
    rb0 = f32(cur0, 1024)
    rb1 = f32(cur0, 1024)
    base_off = cur0.off

    cur = Cursor(base_off)
    ffnA = dict(rb0=rb0, rb1=rb1)
    ffnA["zb"] = [bf16(cur, 1024), bf16(cur, 1024)]
    ffnA["t1"] = [f32(cur, 512), f32(cur, 512), f32(cur, 512)]
    ffnA["w12a"] = bf16(cur, 2048)
    ffnA["w12b"] = bf16(cur, 2048)
    ffnA["whf0"] = bf16(cur, 2048)
    ffnA["whf1"] = bf16(cur, 2048)
    ffnA["whf2"] = bf16(cur, 2048)
    ffnA["whf3"] = bf16(cur, 2048)
    ffnA["w3h0"] = bf16(cur, 2048)
    ffnA["w3h1"] = bf16(cur, 2048)
    ffnA["outb"] = f32(cur, 1024)
    ffnA["xlnT"] = bf16(cur, 8192)
    ffnA["hA"] = bf16(cur, 32768)
    hb_off = cur.off
    ffnA["hB"] = bf16(cur, 32768)
    cur2 = Cursor(hb_off)
    ffnA["outT"] = f32(cur2, 4096)
    ffn_end = cur.off

    cur = Cursor(base_off)
    atA = dict(rb0=rb0, rb1=rb1)
    atA["tf"] = f32(cur, 1024)
    atA["zr"] = atA["tf"]
    atA["hb"] = bf16(cur, 1024)
    atA["r2b"] = f32(cur, 1024)
    atA["jb0"] = bf16(cur, 1024)
    atA["jb1"] = bf16(cur, 1024)
    atA["gam"] = f32(cur, 1024)
    atA["bet"] = f32(cur, 1024)
    atA["wout"] = [bf16(cur, 1024) for _ in range(8)]
    atA["nb32"] = bf16(cur, 64)
    atA["qT"] = bf16(cur, 8192)
    atA["kT"] = bf16(cur, 8192)
    atA["vaug"] = bf16(cur, 16 * 1040)
    atA["osb"] = bf16(cur, 16384)
    atA["vb"] = [atA["osb"][:, i * 1024:(i + 1) * 1024] for i in range(8)]
    atA["at"] = [bf16(cur, 2048), bf16(cur, 2048)]
    at2 = bf16(cur, 2048)
    atA["at6"] = [atA["at"][0][:, 0:1024], atA["at"][0][:, 1024:2048],
                  at2[:, 0:1024],
                  atA["at"][1][:, 0:1024], atA["at"][1][:, 1024:2048],
                  at2[:, 1024:2048]]
    atA["otb"] = [bf16(cur, 1024), bf16(cur, 1024)]
    atA["hT"] = bf16(cur, 16384)
    attn_end = cur.off
    assert max(ffn_end, attn_end) <= ARENA_KB * 1024, (ffn_end, attn_end)

    def small(name, shape, dt):
        return es.enter_context(nc.sbuf_tensor(name, shape, dt))

    H["scols"] = small("s_scols", [128, 160], F32)
    H["smv"] = small("smv", [128, 32], F32)
    H["st6"] = small("st6", [128, 12], F32)
    H["rstd"] = small("rstd", [128, 16], F32)
    H["rstd2"] = small("rstd2", [128, 16], F32)
    H["zcA"] = small("zcA", [128, 16], F32)
    H["zcB"] = small("zcB", [128, 16], F32)
    H["negb"] = small("negb", [128, 8], F32)
    H["rc"] = small("rc", [128, 16], F32)
    H["nsc"] = small("nsc", [32, 8], F32)
    H["s3h"] = small("s3h", [128, 8], F32)
    H["idbf"] = small("s_idbf", [128, 128], BF16)
    H["id32"] = small("s_id32", [128, 128], F32)
    H["ones1"] = small("s_ones1", [1, 128], BF16)
    H["sel4"] = small("s_sel4", [128, 32], BF16)
    H["fc"] = small("s_fc", [128, 4], F32)
    H["brows_t"] = small("s_brows", [1, 2048], BF16)
    H["zrow"] = small("s_zrow", [1, 512], BF16)

    H["PST"] = es.enter_context(nc.psum_tensor("PPST", [128, 2048], F32))
    H["P0"] = H["PST"][:, 0:1024]
    H["P1"] = H["PST"][:, 1024:2048]
    H["P2"] = es.enter_context(nc.psum_tensor("PP2", [128, 1024], F32))
    H["P3"] = es.enter_context(nc.psum_tensor("PP3", [128, 1024], F32))
    H["PT"] = H["P3"][:].bitcast(BF16)[:, 0:1024]

    H["ffn"] = ffnA
    H["attn"] = atA
    H["brows"] = H["brows_t"][:]

    semmap = {s: es.enter_context(nc.semaphore(s)) for s in _SEMS}

    dry = Walk("dry", None, semmap)
    _schedule(dry, H)
    tapdict = {}
    for (name, shape, dt) in dry.taps:
        if name not in tapdict:
            tapdict[name] = nc.dram_tensor("tap_" + name, list(shape), dt,
                                           kind="ExternalOutput").ap()

    with es:
        with nc.Block() as block:
            counts = []

            def mkpass(which):
                def f(eng):
                    w = Walk(which, eng, semmap, tapdict)
                    _schedule(w, H)
                    counts.append(dict(w.cnt))
                return f

            block.sync(mkpass("sync"))
            block.gpsimd(mkpass("gpsimd"))
            block.tensor(mkpass("tensor"))
            block.scalar(mkpass("scalar"))
            block.vector(mkpass("vector"))

            for ci in counts[1:]:
                assert ci == counts[0], "pass count mismatch"

    return nc


def _host_prep(inp):
    f = np.float32
    x = np.asarray(inp["L0_attn_x"], f)
    gam = np.asarray(inp["L0_attn_ln_gamma"], f)
    bet = np.asarray(inp["L0_attn_ln_beta"], f)
    wqkv = np.asarray(inp["L0_attn_w_qkv"], f)
    bqkv = np.asarray(inp["L0_attn_b_qkv"], f)
    wout = np.asarray(inp["L0_attn_w_out"], f)
    bout = np.asarray(inp["L0_attn_b_out"], f)
    w1 = np.asarray(inp["L0_ffn_w1_w"], f)
    w1s = np.asarray(inp["L0_ffn_w1_s"], f)
    w2 = np.asarray(inp["L0_ffn_w2_w"], f)
    w2s = np.asarray(inp["L0_ffn_w2_s"], f)
    h1 = np.asarray(inp["L0_ffn_h1_w"], f)
    h1s = np.asarray(inp["L0_ffn_h1_s"], f)
    h2 = np.asarray(inp["L0_ffn_h2_w"], f)
    h2s = np.asarray(inp["L0_ffn_h2_s"], f)
    w3 = np.asarray(inp["L0_ffn_w3_w"], f)
    w3s = np.asarray(inp["L0_ffn_w3_s"], f)

    def phtile(w, njo, nji):
        return np.ascontiguousarray(
            w.reshape(njo, 128, nji, 128).transpose(0, 3, 2, 1).reshape(njo, 128, nji * 128))

    w1t = phtile(w1, 32, 8)
    w2t = phtile(w2, 32, 8)
    w12 = np.ascontiguousarray(np.concatenate([w1t, w2t], axis=2))
    h1t = phtile(h1, 32, 32)
    h2t = phtile(h2, 32, 32)
    w3t = phtile(w3, 8, 32)
    qk = wqkv[:, 0:2048]
    wqkv_t = np.ascontiguousarray(
        qk.reshape(8, 128, 16, 128).transpose(2, 1, 0, 3).reshape(16, 128, 1024))
    wv_t = np.ascontiguousarray(wqkv[:, 2048:3072].reshape(8, 128, 1024))
    wout_t = np.ascontiguousarray(wout.reshape(8, 128, 1024))

    scols = np.zeros((128, 160), f)
    scols[:, 0:32] = w1s.reshape(32, 128).T
    scols[:, 32:64] = w2s.reshape(32, 128).T
    scols[:, 64:96] = h1s.reshape(32, 128).T
    scols[:, 96:128] = h2s.reshape(32, 128).T
    scols[:, 128:136] = w3s.reshape(8, 128).T
    scols[:, 136:152] = bqkv[0:2048].reshape(16, 128).T

    brows = np.ascontiguousarray(np.concatenate([bqkv[2048:3072], bout]).reshape(1, 2048))
    gam_rep = np.ascontiguousarray(np.broadcast_to(gam, (128, 1024)))
    bet_rep = np.ascontiguousarray(np.broadcast_to(bet, (128, 1024)))

    idbf = np.eye(128, dtype=ml_dtypes.bfloat16)
    id32 = np.eye(128, dtype=f)
    ones1 = np.ones((1, 128), ml_dtypes.bfloat16)
    sel4 = np.zeros((128, 32), ml_dtypes.bfloat16)
    for a in range(4):
        sel4[0:64, a * 8 + 2 * a] = 1.0
        sel4[64:128, a * 8 + 2 * a + 1] = 1.0

    fconst = np.zeros((128, 4), f)
    fconst[:, 0] = EPS
    fconst[:, 1] = EPS * EPS
    fconst[:, 2] = 1e-18
    zrow = np.zeros((1, 512), ml_dtypes.bfloat16)
    common = dict(w12=w12, h1=h1t, h2=h2t, w3=w3t, wqkv=wqkv_t, wv=wv_t,
                  wout=wout_t, scols=scols, gam=gam_rep, bet=bet_rep,
                  brows=brows, idbf=idbf, id32=id32, ones1=ones1, sel4=sel4,
                  fconst=fconst, zrow=zrow)
    shards = [np.ascontiguousarray(x[2 * cid:2 * cid + 2].reshape(TOK, D))
              for cid in range(8)]
    return common, shards


def kernel(**inp):
    common, shards = _host_prep(inp)
    nc = _build()
    in_maps = [dict(common, x=shards[cid]) for cid in range(8)]
    res = run_bass_kernel_spmd(nc, in_maps, core_ids=list(range(8)))
    out = np.concatenate(
        [np.asarray(res.results[cid]["out"], np.float32).reshape(2, T, D)
         for cid in range(8)], axis=0)
    return out


# version 21
# speedup vs baseline: 1.2504x; 1.0816x over previous
import numpy as np
import ml_dtypes
import concourse.bass as bass
import concourse.mybir as mybir
from concourse.bass_utils import run_bass_kernel_spmd
from contextlib import ExitStack

F32 = mybir.dt.float32
BF16 = mybir.dt.bfloat16
AF = mybir.ActivationFunctionType
ALU = mybir.AluOpType
AX = mybir.AxisListType

D = 1024
HID = 4096
T = 2048
NBATCH = 2
TOK = NBATCH * T
TB = 1024
NBLK = TOK // TB
EPS = 1e-5

STAGES = "full"

_SEMS = [
    "pe", "act", "dve",
    "cst", "rin0", "rin1", "w12a", "w12b", "w12c", "whf0", "whf1", "whf2", "whf3",
    "w3h0", "w3h1", "jb0", "jb1", "vb", "wob", "row", "outw", "wbk",
]


TAPS_ON = False


class Walk:

    def __init__(self, which, engine, semmap, tapdict=None):
        self.w = which
        self.eng = engine
        self.sems = semmap
        self.cnt = {k: 0 for k in semmap}
        self.taps = []
        self.tapdict = tapdict

    def wait(self, eng, sem, val):
        if val is None or val <= 0:
            return
        assert val <= self.cnt[sem], f"wait-ahead {eng}.{sem} {val}>{self.cnt[sem]}"
        if self.w == eng:
            self.eng.wait_ge(self.sems[sem], val)

    def _fin(self, ins, inc, k):
        if inc is not None and ins is not None:
            ins.then_inc(self.sems[inc], k)

    def _bump(self, inc, k):
        if inc is None:
            return None
        self.cnt[inc] += k
        return self.cnt[inc]

    def mm(self, out, lhsT, rhs, start, stop, inc=None, k=1):
        if self.w == "tensor":
            ins = self.eng.matmul(out, lhsT, rhs, start=start, stop=stop)
            self._fin(ins, inc, k)
        return self._bump(inc, k)

    def tp(self, out, in_, ident, inc=None, k=1):
        if self.w == "tensor":
            ins = self.eng.transpose(out, in_, ident)
            self._fin(ins, inc, k)
        return self._bump(inc, k)

    def act(self, fn, *a, inc=None, k=1, **kw):
        if self.w == "scalar":
            ins = getattr(self.eng, fn)(*a, **kw)
            self._fin(ins, inc, k)
        return self._bump(inc, k)

    def dve(self, fn, *a, inc=None, k=1, **kw):
        if self.w == "vector":
            ins = getattr(self.eng, fn)(*a, **kw)
            self._fin(ins, inc, k)
        return self._bump(inc, k)

    def dma(self, eng, out, in_, inc, k=16):
        if self.w == eng:
            ins = self.eng.dma_start(out=out, in_=in_)
            self._fin(ins, inc, k)
        return self._bump(inc, k)

    def tap(self, name, ap, gates=()):
        if not TAPS_ON:
            return
        self.taps.append((name, tuple(ap.shape), ap.dtype))
        for sem, val in gates:
            self.wait("sync", sem, val)
        if self.w == "sync" and self.tapdict is not None:
            ins = self.eng.dma_start(out=self.tapdict[name][:], in_=ap)
            ins.then_inc(self.sems["outw"], 16)
        self.cnt["outw"] += 16


def _schedule(em, H):
    c = em.cnt

    em.dma("sync", H["scols"][:], H["d_scols"][:], "cst")
    em.dma("sync", H["idbf"][:], H["d_idbf"][:], "cst")
    em.dma("sync", H["id32"][:], H["d_id32"][:], "cst")
    em.dma("sync", H["ones1"][:], H["d_ones1"][:], "cst")
    em.dma("sync", H["sel4"][:], H["d_sel4"][:], "cst")
    em.dma("sync", H["fc"][:], H["d_fc"][:], "cst")
    em.dma("sync", H["zrow"][:], H["d_zrow"][:], "cst")
    CST_N = c["cst"]
    for e in ("tensor", "scalar", "vector"):
        em.wait(e, "cst", CST_N)
    em.dve("tensor_scalar", H["s3h"][:], H["scols"][:, 128:136], 0.5, None,
           ALU.mult, ALU.bypass)

    def ffn(src_d, dst_d, first=False):
        A = H["ffn"]
        rbuf = [A["rb0"], A["rb1"]]
        rsem = ["rin0", "rin1"]
        rfree = [0, 0]
        rload = [0, 0]
        outw_gate = c["outw"]
        zb_free = [0, 0]
        pt_dve_s = [0]
        last_z = [0]

        def ln_stats(b, z_gate):
            tok0 = b * TB
            for t in range(8):
                s = t % 2
                em.wait("sync", "dve", rfree[s])
                rload[s] = em.dma("sync", rbuf[s][:],
                                  src_d[tok0 + t * 128: tok0 + (t + 1) * 128, :], rsem[s])
                em.wait("vector", rsem[s], rload[s])
                em.wait("vector", "dve", z_gate if t == 0 else 0)
                em.dve("bn_stats", H["st6"][:, 0:6], rbuf[s][:, 0:512])
                em.dve("bn_stats", H["st6"][:, 6:12], rbuf[s][:, 512:1024])
                em.dve("drain")
                em.dve("bn_aggr", H["smv"][:, 2 * t:2 * t + 2], H["st6"][:])
                ST = em.dve("drain", inc="dve")
                rfree[s] = ST
            em.wait("scalar", "dve", ST)
            em.wait("scalar", "dve", z_gate)
            smv_var = H["smv"][:, 0:16].rearrange("p (t two) -> p t two", two=2)[:, :, 1]
            em.act("activation", H["rstd"][:, 0:8], smv_var, AF.Ln,
                   bias=H["fc"][:, 0:1], scale=1.0)
            RS = em.act("activation", H["rstd"][:, 0:8], H["rstd"][:, 0:8],
                        AF.Exp, scale=-0.5, inc="act")
            return RS

        def ln_apply_tile(b, RS, t, st):
            tok0 = b * TB
            pt_dve = st["pt"]
            if True:
                s = t % 2
                em.wait("sync", "dve", rfree[s])
                rload[s] = em.dma("sync", rbuf[s][:],
                                  src_d[tok0 + t * 128: tok0 + (t + 1) * 128, :], rsem[s])
                em.wait("vector", rsem[s], rload[s])
                em.wait("vector", "act", RS if t == 0 else 0)
                em.wait("vector", "pe", zb_free[t % 2])
                ZD = em.dve("tensor_scalar", A["zb"][t % 2][:], rbuf[s][:],
                            H["smv"][:, 2 * t:2 * t + 1], H["rstd"][:, t:t + 1],
                            ALU.subtract, ALU.mult, inc="dve")
                rfree[s] = ZD
                em.wait("tensor", "dve", ZD)
                em.wait("tensor", "dve", pt_dve)
                for q in range(8):
                    TPQ = em.tp(H["PT"][:, q * 128:(q + 1) * 128],
                                A["zb"][t % 2][:, q * 128:(q + 1) * 128], H["idbf"][:],
                                inc="pe" if q == 7 else None)
                zb_free[t % 2] = TPQ
                em.wait("vector", "pe", TPQ)
                dst = A["xlnT"].rearrange("p (i tt q) -> p i tt q", i=8, tt=8)[:, :, t, :]
                src = H["PT"][:].rearrange("p (i q) -> p i q", i=8)
                if t in (3, 7):
                    em.dve("tensor_copy", dst, src)
                    pt_dve = em.dve("drain", inc="dve")
                else:
                    pt_dve = em.dve("tensor_copy", dst, src, inc="dve")
                if t == 3:
                    st["half0"] = pt_dve
            st["pt"] = pt_dve
            st["ZD"] = ZD

        def ln_apply(b, RS, tiles=range(8), st=None):
            if st is None:
                st = {"pt": pt_dve_s[0], "half0": 0, "ZD": 0}
            for t in tiles:
                ln_apply_tile(b, RS, t, st)
            if max(tiles) == 7:
                pt_dve_s[0] = st["pt"]
                last_z[0] = st["ZD"]
            return st

        if first:
            A["_whfwbk"] = [0, 0, 0, 0]
        whf_state = {"load": [0, 0, 0, 0], "free": [0, 0, 0, 0],
                     "wbk": A["_whfwbk"]}
        w12_state = {"load": [0, 0, 0], "free": [0, 0, 0]}
        whfsem = ["whf0", "whf1", "whf2", "whf3"]
        whfbuf = [A["whf0"], A["whf1"], A["whf2"], A["whf3"]]

        def whf_load(ci, src):
            em.wait("scalar", "pe", whf_state["free"][ci])
            em.wait("scalar", "wbk", whf_state["wbk"][ci])
            whf_state["load"][ci] = em.dma("scalar", whfbuf[ci][:], src,
                                           whfsem[ci])

        def hid_prefetch(c_d):
            for j2 in range(2):
                for hc in range(2):
                    ci = (j2 * 2 + hc) % 4
                    whf_load(ci, c_d[j2][:, hc * 2048:(hc + 1) * 2048])

        def hid_layer(w_d, c_d, cache_write, src_sb, dst_sb, scol_off,
                      src_sem, src_cnt, ph1_p2_free=0, pre=False):
            pslot_free = [0, 0, 0]
            pregs = [H["P0"], H["P1"], H["P2"]]
            GE = 0
            for j2 in range(32):
                for hc in range(2):
                    ci = (j2 * 2 + hc) % 4
                    if pre and j2 < 2:
                        continue
                    if cache_write:
                        em.wait("gpsimd", "pe", whf_state["free"][ci])
                        em.wait("gpsimd", "wbk", whf_state["wbk"][ci])
                        whf_state["load"][ci] = em.dma(
                            "gpsimd", whfbuf[ci][:],
                            w_d[j2][:, hc * 2048:(hc + 1) * 2048], whfsem[ci])
                        em.wait("sync", whfsem[ci], whf_state["load"][ci])
                        whf_state["wbk"][ci] = em.dma(
                            "sync", c_d[j2][:, hc * 2048:(hc + 1) * 2048],
                            whfbuf[ci][:], "wbk")
                    else:
                        whf_load(ci, c_d[j2][:, hc * 2048:(hc + 1) * 2048])
                sp = j2 % 3
                Ph = pregs[sp]
                em.wait("tensor", "act", pslot_free[sp])
                em.wait("tensor", src_sem, src_cnt)
                em.wait("tensor", "dve", ph1_p2_free if j2 < 3 else 0)
                for half in range(2):
                    for i in range(32):
                        ci = (j2 * 2 + (i // 16)) % 4
                        if i in (0, 16):
                            em.wait("tensor", whfsem[ci], whf_state["load"][ci])
                        rhs = src_sb[:, i * 1024 + half * 512: i * 1024 + (half + 1) * 512]
                        MME = em.mm(Ph[:, half * 512:(half + 1) * 512],
                                    whfbuf[ci][:, (i % 16) * 128:((i % 16) + 1) * 128],
                                    rhs, start=(i == 0), stop=(i == 31),
                                    inc="pe" if (i == 31 or (half == 1 and i == 15))
                                    else None)
                        if half == 1 and i == 15:
                            whf_state["free"][(j2 * 2) % 4] = MME
                    if half == 1:
                        whf_state["free"][(j2 * 2 + 1) % 4] = MME
                em.wait("scalar", "pe", MME)
                dsth = dst_sb[:, j2 * 1024:(j2 + 1) * 1024]
                GE = em.act("activation", dsth, Ph[:], AF.Gelu,
                            scale=H["scols"][:, scol_off + j2:scol_off + j2 + 1],
                            inc="act")
                pslot_free[sp] = GE
            return GE

        stats_ready = ln_stats(0, 0)
        apply_ready = ln_apply(0, stats_ready)
        for b in range(NBLK):
            tok0 = b * TB
            xln_half0, XLN_OK = apply_ready["half0"], apply_ready["pt"]

            if not (first and b == 0):
                hid_prefetch(H["c_h1"])
            w12sem = ["w12a", "w12b", "w12c"]
            w12buf = [A["w12a"], A["w12b"], A["w12c"]]
            w12load = w12_state["load"]
            w12free = w12_state["free"]
            if first:
                A["_w12wbk"] = [0, 0, 0]
            w12wbk = A["_w12wbk"]

            def w12_issue(jj):
                sj = jj % 3
                if first and b == 0:
                    em.wait("gpsimd", "pe", w12free[sj])
                    em.wait("gpsimd", "wbk", w12wbk[sj])
                    w12load[sj] = em.dma("gpsimd", w12buf[sj][:], H["d_w12"][jj],
                                         w12sem[sj])
                    em.wait("sync", w12sem[sj], w12load[sj])
                    w12wbk[sj] = em.dma("sync", H["c_w12"][jj], w12buf[sj][:], "wbk")
                else:
                    em.wait("scalar", "pe", w12free[sj])
                    em.wait("scalar", "wbk", w12wbk[sj])
                    w12load[sj] = em.dma("scalar", w12buf[sj][:], H["c_w12"][jj],
                                         w12sem[sj])

            slot_free = [0, 0, 0]
            t1free = [0, 0, 0]
            SF = 0
            w12_issue(0)
            w12_issue(1)
            for j in range(32):
                sj = j % 3
                if j + 2 < 32:
                    w12_issue(j + 2)
                for half in range(2):
                    sp = (j * 2 + half) % 3
                    Ps = [H["P0"], H["P1"], H["P2"]][sp]
                    em.wait("tensor", w12sem[sj], w12load[sj])
                    em.wait("tensor", "dve",
                            max(xln_half0 if half == 0 else XLN_OK, slot_free[sp]))
                    for i in range(8):
                        rhs = A["xlnT"][:, i * 1024 + half * 512: i * 1024 + (half + 1) * 512]
                        em.mm(Ps[:, 0:512], w12buf[sj][:, i * 128:(i + 1) * 128], rhs,
                              start=(i == 0), stop=(i == 7))
                    for i in range(8):
                        rhs = A["xlnT"][:, i * 1024 + half * 512: i * 1024 + (half + 1) * 512]
                        MME = em.mm(Ps[:, 512:1024],
                                    w12buf[sj][:, 1024 + i * 128:1024 + (i + 1) * 128], rhs,
                                    start=(i == 0), stop=(i == 7),
                                    inc="pe" if i == 7 else None)
                    if half == 1:
                        w12free[sj] = MME
                    em.wait("scalar", "pe", MME)
                    em.wait("scalar", "dve", t1free[sp])
                    SIL = em.act("activation", A["t1"][sp][:], Ps[:, 0:512], AF.Silu,
                                 scale=H["scols"][:, j:j + 1], inc="act")
                    em.wait("vector", "act", SIL)
                    dsth = A["hA"][:, j * 1024 + half * 512: j * 1024 + (half + 1) * 512]
                    SF = em.dve("scalar_tensor_tensor", dsth, Ps[:, 512:1024],
                                H["scols"][:, 32 + j:32 + j + 1], A["t1"][sp][:],
                                ALU.mult, ALU.mult, inc="dve")
                    slot_free[sp] = SF
                    t1free[sp] = SF
            PH1_OK = SF

            PH2_OK = hid_layer(H["d_h1"], H["c_h1"], first and b == 0,
                               A["hA"], A["hB"], 64, "dve", PH1_OK,
                               ph1_p2_free=PH1_OK,
                               pre=not (first and b == 0))
            if b + 1 < NBLK:
                stats_ready = ln_stats(b + 1, last_z[0])
            PH3_OK = hid_layer(H["d_h2"], H["c_h2"], first and b == 0,
                               A["hB"], A["hA"], 96, "act", PH2_OK)

            w3sem = ["w3h0", "w3h1"]
            w3buf = [A["w3h0"], A["w3h1"]]
            w3load = [0, 0]
            w3free = [0, 0]
            if first:
                A["_w3wbk"] = [0, 0]
            w3wbk = A["_w3wbk"]
            ph4_slot_free = [0, 0, 0]
            p2_gate = 0
            for half in range(2):
                if half == 1 and b + 1 < NBLK:
                    ap_st = {"pt": pt_dve_s[0], "half0": 0, "ZD": 0}
                EV = 0
                for j3 in range(8):
                    if half == 1 and b + 1 < NBLK and j3 >= 1:
                        apply_ready = ln_apply(b + 1, stats_ready,
                                               tiles=[j3 - 1], st=ap_st)
                    for hc in range(2):
                        ci = (j3 * 2 + hc) % 2
                        if first and b == 0 and half == 0:
                            em.wait("gpsimd", "pe", w3free[ci])
                            em.wait("gpsimd", "wbk", w3wbk[ci])
                            w3load[ci] = em.dma(
                                "gpsimd", w3buf[ci][:],
                                H["d_w3"][j3][:, hc * 2048:(hc + 1) * 2048], w3sem[ci])
                            em.wait("sync", w3sem[ci], w3load[ci])
                            w3wbk[ci] = em.dma(
                                "sync", H["c_w3"][j3][:, hc * 2048:(hc + 1) * 2048],
                                w3buf[ci][:], "wbk")
                        else:
                            em.wait("scalar", "pe", w3free[ci])
                            em.wait("scalar", "wbk", w3wbk[ci])
                            w3load[ci] = em.dma(
                                "scalar", w3buf[ci][:],
                                H["c_w3"][j3][:, hc * 2048:(hc + 1) * 2048], w3sem[ci])
                    sp = j3 % 3
                    Pp = [H["P0"][:, 0:512], H["P1"][:, 0:512],
                          H["P0"][:, 512:1024]][sp]
                    em.wait("tensor", "dve", ph4_slot_free[sp])
                    em.wait("tensor", "act", PH3_OK)
                    for i in range(32):
                        ci = (j3 * 2 + (i // 16)) % 2
                        if i in (0, 16):
                            em.wait("tensor", w3sem[ci], w3load[ci])
                        rhs = A["hA"][:, i * 1024 + half * 512: i * 1024 + (half + 1) * 512]
                        MME = em.mm(Pp, w3buf[ci][:, (i % 16) * 128:((i % 16) + 1) * 128],
                                    rhs, start=(i == 0), stop=(i == 31),
                                    inc="pe" if (i == 31 or i == 15) else None)
                        if i == 15:
                            w3free[(j3 * 2) % 2] = MME
                    w3free[(j3 * 2 + 1) % 2] = MME
                    em.wait("vector", "pe", MME)
                    EV = em.dve("tensor_scalar", A["outT"][:, j3 * 512:(j3 + 1) * 512], Pp,
                                H["s3h"][:, j3:j3 + 1], None, ALU.mult, ALU.bypass,
                                inc="dve")
                    ph4_slot_free[sp] = EV
                if half == 1 and b + 1 < NBLK:
                    apply_ready = ln_apply(b + 1, stats_ready,
                                           tiles=[7], st=ap_st)
                OUTT_OK = EV
                for q in range(4):
                    t = half * 4 + q
                    s = t % 2
                    em.wait("sync", "dve", rfree[s])
                    rload[s] = em.dma("sync", rbuf[s][:],
                                      src_d[tok0 + t * 128: tok0 + (t + 1) * 128, :],
                                      rsem[s])
                    em.wait("tensor", "dve", max(OUTT_OK, p2_gate))
                    for j3 in range(8):
                        TPQ = em.tp(H["P2"][:, j3 * 128:(j3 + 1) * 128],
                                    A["outT"][:, j3 * 512 + q * 128: j3 * 512 + (q + 1) * 128],
                                    H["id32"][:], inc="pe" if j3 == 7 else None)
                    em.wait("vector", "pe", TPQ)
                    em.wait("vector", rsem[s], rload[s])
                    em.wait("vector", "outw", outw_gate)
                    AD = em.dve("tensor_tensor", A["outb"][:], rbuf[s][:], H["P2"][:],
                                ALU.add, inc="dve")
                    rfree[s] = AD
                    p2_gate = AD
                    em.wait("sync", "dve", AD)
                    outw_gate = em.dma("sync",
                                       dst_d[tok0 + t * 128: tok0 + (t + 1) * 128, :],
                                       A["outb"][:], "outw")
        return outw_gate

    def attn(src_d, dst_d):
        def stage_barrier():
            marks = dict(pe=c["pe"], act=c["act"], dve=c["dve"])
            for e in ("tensor", "scalar", "vector", "sync", "gpsimd"):
                em.wait(e, "pe", marks["pe"])
                em.wait(e, "act", marks["act"])
                em.wait(e, "dve", marks["dve"])
        A = H["attn"]
        rbuf = [A["rb0"], A["rb1"]]
        rsem = ["rin0", "rin1"]
        outw_gate = c["outw"]
        at_done = {(s, p): 0 for s in range(2) for p in range(3)}
        wob_n = 0
        em.wait("gpsimd", "pe", c["pe"])
        for f in range(8):
            wob_n = em.dma("gpsimd", A["wout"][f][:], H["d_wout"][f], "wob")
        ROW_N = em.dma("gpsimd", H["brows"][0:1, 0:2048], H["d_brows"][:], "row")
        em.dma("sync", A["gam"][:], H["d_gam"][:], "cst")
        GBL = em.dma("sync", A["bet"][:], H["d_bet"][:], "cst")
        vload = 0

        for bi in range(NBATCH):
            tokb = bi * T
            rfree = [0, 0]
            rload = [0, 0]
            ps_act = [0, 0]
            ps_dve = [0, 0]
            pq_dve = 0
            pt_dve = 0

            em.wait("gpsimd", "pe", c["pe"])
            for i in range(8):
                vload = em.dma("gpsimd", A["vb"][i][:], H["d_wv"][i], "vb")
            va4 = A["vaug"].rearrange("p (t h cc) -> p t h cc", t=16, h=16)
            em.wait("vector", "pe", c["pe"])
            VMS = em.dve("memset", va4[:, :, :, 64:65], 1.0, inc="dve")
            for t in range(16):
                s = t % 2
                em.wait("sync", "dve", rfree[s])
                rload[s] = em.dma("sync", rbuf[s][:],
                                  src_d[tokb + t * 128: tokb + (t + 1) * 128, :], rsem[s])
                em.wait("vector", rsem[s], rload[s])
                em.dve("bn_stats", H["st6"][:, 0:6], rbuf[s][:, 0:512])
                em.dve("bn_stats", H["st6"][:, 6:12], rbuf[s][:, 512:1024])
                em.dve("drain")
                em.dve("bn_aggr", H["smv"][:, 2 * t:2 * t + 2], H["st6"][:])
                ST = em.dve("drain", inc="dve")
                em.wait("scalar", "dve", ST)
                em.act("activation", H["rstd"][:, t:t + 1], H["smv"][:, 2 * t + 1:2 * t + 2],
                       AF.Ln, bias=H["fc"][:, 0:1], scale=1.0)
                em.act("activation", H["rstd"][:, t:t + 1], H["rstd"][:, t:t + 1],
                       AF.Exp, scale=-0.5)
                em.act("activation", H["rstd2"][:, t:t + 1], H["smv"][:, 2 * t + 1:2 * t + 2],
                       AF.Ln, bias=H["fc"][:, 1:2], scale=float(1.0 + EPS))
                RS = em.act("activation", H["rstd2"][:, t:t + 1], H["rstd2"][:, t:t + 1],
                            AF.Exp, scale=-0.5, inc="act")
                em.wait("vector", "act", RS)
                em.wait("vector", "cst", GBL)
                em.dve("tensor_scalar", A["tf"][:], rbuf[s][:],
                       H["smv"][:, 2 * t:2 * t + 1], H["rstd2"][:, t:t + 1],
                       ALU.subtract, ALU.mult)
                em.dve("tensor_tensor", A["tf"][:], A["tf"][:], A["gam"][:], ALU.mult)
                HB = em.dve("tensor_tensor", A["hb"][:], A["tf"][:], A["bet"][:],
                            ALU.add, inc="dve")
                rfree[s] = HB
                if bi == 0 and t == 0:
                    em.tap("hb0", A["hb"][:], gates=[("dve", HB)])
                    em.tap("smv0", H["smv"][:], gates=[("dve", HB)])
                    em.tap("rstd20", H["rstd2"][:], gates=[("act", RS)])
                em.wait("tensor", "dve", max(HB, pt_dve))
                for q in range(8):
                    TPQ = em.tp(H["PT"][:, q * 128:(q + 1) * 128],
                                A["hb"][:, q * 128:(q + 1) * 128], H["idbf"][:],
                                inc="pe" if q == 7 else None)
                em.wait("vector", "pe", TPQ)
                dst = A["hT"].rearrange("p (i tt q) -> p i tt q", i=8, tt=16)[:, :, t, :]
                src = H["PT"][:].rearrange("p (i q) -> p i q", i=8)
                em.dve("tensor_copy", dst, src)
                pt_dve = em.dve("drain", inc="dve")
                sp = t % 2
                Ps = [H["P0"], H["P1"]][sp]
                em.wait("tensor", "vb", vload)
                em.wait("tensor", "row", ROW_N)
                em.wait("tensor", "act", ps_act[sp])
                em.wait("tensor", "dve", max(ps_dve[sp], pt_dve))
                for i in range(8):
                    for hh in range(2):
                        em.mm(Ps[:, hh * 512:(hh + 1) * 512],
                              A["hT"][:, i * 2048 + t * 128: i * 2048 + (t + 1) * 128],
                              A["vb"][i][:, hh * 512:(hh + 1) * 512],
                              start=(i == 0), stop=False)
                for hh in range(2):
                    MME = em.mm(Ps[:, hh * 512:(hh + 1) * 512],
                                H["ones1"][:],
                                H["brows"][0:1, hh * 512:(hh + 1) * 512],
                                start=False, stop=True,
                                inc="pe" if hh == 1 else None)
                em.wait("vector", "pe", MME)
                em.wait("vector", "dve", VMS)
                vsrc = Ps[:].rearrange("p (h cc) -> p h cc", h=16)
                em.dve("tensor_copy", va4[:, t, :, 0:64], vsrc)
                VE = em.dve("drain", inc="dve")
                ps_dve[sp] = VE
            if bi == 0:
                em.tap("hT0", A["hT"][:, 0:2048], gates=[("dve", pt_dve)])
            em.dve("tensor_scalar", H["zcA"][:], H["rstd"][:, 0:16], 1.0, None,
                   ALU.add, ALU.bypass)
            ZC = em.dve("tensor_tensor", H["zcB"][:],
                        H["smv"][:].rearrange("p (t two) -> p t two", two=2)[:, :, 0],
                        H["rstd"][:, 0:16], ALU.mult, inc="dve")

            v_done = ZC
            stage_barrier()
            for r in range(2):
                qj = [4 * r + a for a in range(4)]
                kj = [8 + 4 * r + a for a in range(4)]
                jsem = ["jb0", "jb1"]
                jbuf = [A["jb0"], A["jb1"]]
                jload = [0, 0]
                jfree = [0, 0]
                for idx, j in enumerate(qj + kj):
                    sj = idx % 2
                    em.wait("gpsimd", "pe", jfree[sj])
                    jload[sj] = em.dma("gpsimd", jbuf[sj][:], H["d_wqkv"][j], jsem[sj])
                    if TAPS_ON and bi == 0 and r == 1 and idx == 7:
                        em.tap("jb15", jbuf[sj][:], gates=[(jsem[sj], jload[sj])])
                    a = idx % 4
                    dst_sb = A["qT"] if idx < 4 else A["kT"]
                    for half in range(2):
                        sp = (idx * 2 + half) % 2
                        Ps = [H["P0"], H["P1"]][sp]
                        em.wait("tensor", jsem[sj], jload[sj])
                        em.wait("tensor", "dve", pt_dve)
                        em.wait("tensor", "act", ps_act[sp])
                        em.wait("tensor", "dve", ps_dve[sp])
                        for qtr in range(2):
                            for i in range(8):
                                rhs = A["hT"][:, i * 2048 + half * 1024 + qtr * 512:
                                              i * 2048 + half * 1024 + (qtr + 1) * 512]
                                MME = em.mm(Ps[:, qtr * 512:(qtr + 1) * 512],
                                            jbuf[sj][:, i * 128:(i + 1) * 128], rhs,
                                            start=(i == 0), stop=(i == 7),
                                            inc="pe" if (qtr == 1 and i == 7) else None)
                        if half == 1:
                            jfree[sj] = MME
                        if TAPS_ON and bi == 0 and r == 1 and idx == 7 and half == 1:
                            em.wait("vector", "pe", MME)
                            PQ_ = em.dve("tensor_copy", A["r2b"][:], Ps[:], inc="dve")
                            em.tap("k15psum", A["r2b"][:], gates=[("dve", PQ_)])
                            em.wait("scalar", "dve", PQ_)
                        em.wait("scalar", "pe", MME)
                        em.act("activation",
                               dst_sb[:, a * 2048 + half * 1024:
                                      a * 2048 + (half + 1) * 1024],
                               Ps[:], AF.Identity,
                               bias=H["scols"][:, 136 + j:136 + j + 1], scale=1.0)
                        QKE = em.act("drain", inc="act")
                        ps_act[sp] = QKE
                        if bi == 0 and idx == 0 and half == 1:
                            em.tap("qT0", A["qT"][:, 0:2048], gates=[("act", QKE)])
                        if bi == 0 and idx == 4 and half == 1:
                            em.tap("kT0", A["kT"][:, 0:2048], gates=[("act", QKE)])
                        if bi == 0 and r == 1 and idx == 3 and half == 1:
                            em.tap("qT3r1", A["qT"][:, 3 * 2048:4 * 2048], gates=[("act", QKE)])
                        if bi == 0 and r == 1 and idx == 7 and half == 1:
                            em.tap("kT3r1", A["kT"][:, 3 * 2048:4 * 2048], gates=[("act", QKE)])

                stage_barrier()
                nsc = H["nsc"]
                for side in range(2):
                    src_sb = A["qT"] if side == 0 else A["kT"]
                    Mcol = nsc[0:8, 1 + 2 * side:2 + 2 * side]
                    regs = [H["P2"][0:8, 0:512], H["P2"][0:8, 512:1024],
                            H["P3"][0:8, 0:512], H["P0"][0:8, 0:512]]
                    sq_free = [0, 0]
                    for a in range(4):
                        sb = a % 2
                        em.wait("vector", "pe", sq_free[sb])
                        SQ = em.dve("tensor_tensor", A["at"][sb][:],
                                    src_sb[:, a * 2048:(a + 1) * 2048],
                                    src_sb[:, a * 2048:(a + 1) * 2048], ALU.mult,
                                    inc="dve")
                        em.wait("tensor", "dve", SQ)
                        if a == 0:
                            em.wait("tensor", "dve", pq_dve)
                            em.wait("tensor", "act", ps_act[0])
                            em.wait("tensor", "dve", ps_dve[0])
                        for ch in range(4):
                            MME = em.mm(regs[ch], H["sel4"][:, a * 8:(a + 1) * 8],
                                        A["at"][sb][:, ch * 512:(ch + 1) * 512],
                                        start=(a == 0), stop=(a == 3),
                                        inc="pe" if ch == 3 else None)
                        sq_free[sb] = MME
                    em.wait("vector", "pe", MME)
                    for ch in range(4):
                        em.dve("reduce_max", nsc[0:8, 0:1], regs[ch], axis=AX.X)
                        em.dve("drain")
                        if ch == 0:
                            em.dve("tensor_copy", Mcol, nsc[0:8, 0:1])
                        else:
                            em.dve("tensor_tensor", Mcol, Mcol, nsc[0:8, 0:1], ALU.max)
                        em.dve("drain")
                    NR = em.dve("drain", inc="dve")
                    pq_dve = NR
                    ps_dve[0] = NR
                em.dve("tensor_tensor", nsc[0:8, 5:6], nsc[0:8, 1:2], nsc[0:8, 3:4],
                       ALU.mult)
                NP = em.dve("drain", inc="dve")
                em.wait("scalar", "dve", NP)
                em.act("activation", nsc[0:8, 6:7], nsc[0:8, 5:6], AF.Ln,
                       bias=H["fc"][0:8, 2:3], scale=1.0)
                NB_ = em.act("activation", nsc[0:8, 6:7], nsc[0:8, 6:7], AF.Exp,
                             scale=0.5, inc="act")
                em.wait("vector", "act", NB_)
                em.dve("tensor_scalar", nsc[0:8, 7:8], nsc[0:8, 6:7], -0.125, None,
                       ALU.mult, ALU.bypass)
                em.dve("drain")
                em.dve("memset", A["nb32"][0:32, 0:64], 0.0)
                em.dve("tensor_copy", A["nb32"][0:8, 0:1], nsc[0:8, 7:8])
                em.dve("drain")
                em.dve("transpose", A["nb32"][0:32, 32:64], A["nb32"][0:32, 0:32])
                NBT = em.dve("drain", inc="dve")
                em.wait("tensor", "dve", NBT)
                em.wait("tensor", "dve", ps_dve[0])
                em.wait("tensor", "act", ps_act[0])
                BC = em.mm(H["P0"][:, 0:8], H["ones1"][:], A["nb32"][0:1, 32:40],
                           start=True, stop=True, inc="pe")
                em.wait("vector", "pe", BC)
                em.dve("tensor_copy", H["negb"][:], H["P0"][:, 0:8])
                NBB = em.dve("drain", inc="dve")
                ps_dve[0] = NBB
                if bi == 0 and r == 0:
                    em.tap("nsc0", H["nsc"][:], gates=[("dve", NBB)])
                    em.tap("negb0", H["negb"][:], gates=[("dve", NBB)])

                stage_barrier()
                for pa in range(4):
                    heads = [8 * r + 2 * pa, 8 * r + 2 * pa + 1]
                    for qhalf in range(2):
                        obase = [H["P2"], H["P3"]]

                        def qoff(s, qc):
                            if qc < 7:
                                return obase[s][:, qc * 65:(qc + 1) * 65]
                            return obase[s][:, 512:577]

                        def score(s, kt):
                            hh = 2 * pa + s
                            pb = (hh % 2) * 64
                            Ps = [H["P0"], H["P1"]][s]
                            em.wait("tensor", "act", ps_act[s])
                            em.wait("tensor", "dve", ps_dve[s])
                            for qtr in range(2):
                                MME = em.mm(
                                    Ps[:, qtr * 512:(qtr + 1) * 512],
                                    A["kT"][pb:pb + 64,
                                            pa * 2048 + kt * 128: pa * 2048 + (kt + 1) * 128],
                                    A["qT"][pb:pb + 64,
                                            pa * 2048 + qhalf * 1024 + qtr * 512:
                                            pa * 2048 + qhalf * 1024 + (qtr + 1) * 512],
                                    start=True, stop=True,
                                    inc="pe" if qtr == 1 else None)
                            return MME

                        def expf(s, kt):
                            hh = 2 * pa + s
                            em.wait("scalar", "pe", max(sc_mme[s], at_done[(s, kt % 3)]))
                            if pa == 0 and qhalf == 0 and kt == 0:
                                em.wait("scalar", "dve", NBB)
                            EX = em.act("activation", A["at6"][3 * s + kt % 3][:],
                                        [H["P0"], H["P1"]][s][:], AF.Exp,
                                        bias=H["negb"][:, hh:hh + 1], scale=0.125,
                                        inc="act")
                            ps_act[s] = EX

                        def attnv(s, kt2):
                            hg = 8 * r + 2 * pa + s
                            atb2 = A["at6"][3 * s + kt2 % 3]
                            for qc in range(8):
                                M2 = em.mm(qoff(s, qc), atb2[:, qc * 128:(qc + 1) * 128],
                                           A["vaug"][:, kt2 * 1040 + hg * 65:
                                                     kt2 * 1040 + (hg + 1) * 65],
                                           start=False, stop=(kt2 == 15),
                                           inc="pe" if qc == 7 else None)
                            at_done[(s, kt2 % 3)] = M2

                        sc_mme = [0, 0]
                        for kt in range(16):
                            sc_mme[0] = score(0, kt)
                            sc_mme[1] = score(1, kt)
                            expf(0, kt)
                            expf(1, kt)
                            if kt == 2:
                                em.wait("tensor", "dve", max(pq_dve, v_done))
                                for s in range(2):
                                    em.mm(obase[s][:, 0:455], H["ones1"][:],
                                          H["zrow"][0:1, 0:455], start=True, stop=False)
                                    em.mm(obase[s][:, 512:577], H["ones1"][:],
                                          H["zrow"][0:1, 0:65], start=True, stop=False)
                            if kt >= 2:
                                attnv(0, kt - 2)
                                attnv(1, kt - 2)
                        attnv(0, 14)
                        attnv(1, 14)
                        em.wait("tensor", "act", ps_act[0])
                        attnv(0, 15)
                        em.wait("tensor", "act", ps_act[1])
                        attnv(1, 15)
                        em.wait("vector", "pe", at_done[(1, 15 % 3)])
                        for s in range(2):
                            hg = 8 * r + 2 * pa + s
                            em.dve("reciprocal", H["rc"][:, s * 8:s * 8 + 7],
                                   obase[s][:, 64:512:65])
                            em.dve("reciprocal", H["rc"][:, s * 8 + 7:s * 8 + 8],
                                   obase[s][:, 576:577])
                            em.dve("drain")
                            for qc in range(8):
                                gq = qhalf * 8 + qc
                                em.dve("tensor_scalar",
                                       A["osb"][:, gq * 1024 + hg * 64:
                                                gq * 1024 + (hg + 1) * 64],
                                       qoff(s, qc)[:, 0:64],
                                       H["rc"][:, s * 8 + qc:s * 8 + qc + 1], None,
                                       ALU.mult, ALU.bypass)
                        ON = em.dve("drain", inc="dve")
                        pq_dve = ON

            stage_barrier()
            if TAPS_ON and bi == 0:
                em.tap("osbP", A["osb"][:, 0:1024], gates=[("dve", pq_dve)])
                em.tap("osbP8", A["osb"][:, 8 * 1024:9 * 1024], gates=[("dve", pq_dve)])
            otb_free = [0, 0]
            OTEs = [0] * 17

            def proj_tp(t2):
                ob2 = t2 % 2
                em.wait("tensor", "dve", max(pq_dve if t2 == 0 else 0, pt_dve_l[0]))
                for f in range(8):
                    TPQ = em.tp(H["PT"][:, f * 128:(f + 1) * 128],
                                A["osb"][:, t2 * 1024 + f * 128: t2 * 1024 + (f + 1) * 128],
                                H["idbf"][:], inc="pe" if f == 7 else None)
                em.wait("vector", "pe", max(TPQ, otb_free[ob2]))
                OTEs[t2] = em.dve("tensor_copy", A["otb"][ob2][:], H["PT"][:], inc="dve")
                pt_dve_l[0] = OTEs[t2]

            pt_dve_l = [pt_dve]
            proj_tp(0)
            for t in range(16):
                ob = t % 2
                if t + 1 < 16:
                    proj_tp(t + 1)
                OTE = OTEs[t]
                s = t % 2
                em.wait("sync", "dve", rfree[s])
                rload[s] = em.dma("sync", rbuf[s][:],
                                  src_d[tokb + t * 128: tokb + (t + 1) * 128, :], rsem[s])
                sp = t % 2
                Pp = [H["P0"], H["P1"]][sp]
                em.wait("tensor", "dve", max(OTE, ps_dve[sp]))
                em.wait("tensor", "act", ps_act[sp])
                em.wait("tensor", "wob", wob_n)
                for half in range(2):
                    for f in range(8):
                        em.mm(Pp[:, half * 512:(half + 1) * 512],
                              A["otb"][ob][:, f * 128:(f + 1) * 128],
                              A["wout"][f][:, half * 512:(half + 1) * 512],
                              start=(f == 0), stop=False)
                    MME = em.mm(Pp[:, half * 512:(half + 1) * 512], H["ones1"][:],
                                H["brows"][0:1, 1024 + half * 512:1024 + (half + 1) * 512],
                                start=False, stop=True, inc="pe" if half == 1 else None)
                otb_free[ob] = MME
                em.wait("vector", "pe", MME)
                em.wait("vector", rsem[s], rload[s])
                em.wait("vector", "dve", ZC if t == 0 else 0)
                em.dve("tensor_scalar", A["zr"][:], rbuf[s][:],
                       H["zcA"][:, t:t + 1], H["zcB"][:, t:t + 1], ALU.mult, ALU.subtract)
                em.wait("vector", "outw", outw_gate)
                em.dve("tensor_tensor", A["r2b"][:, 0:512], A["zr"][:, 0:512],
                       Pp[:, 0:512], ALU.add)
                AD = em.dve("tensor_tensor", A["r2b"][:, 512:1024], A["zr"][:, 512:1024],
                            Pp[:, 512:1024], ALU.add, inc="dve")
                rfree[s] = AD
                ps_dve[sp] = AD
                pt_dve = pt_dve_l[0]
                if TAPS_ON and bi == 0 and t < 2:
                    em.tap(f"otb{t}", A["otb"][ob][:], gates=[("dve", OTE)])
                    em.tap(f"zr{t}", A["zr"][:], gates=[("dve", AD)])
                    em.tap(f"r2b{t}", A["r2b"][:], gates=[("dve", AD)])
                em.wait("sync", "dve", AD)
                outw_gate = em.dma("sync", dst_d[tokb + t * 128: tokb + (t + 1) * 128, :],
                                   A["r2b"][:], "outw")
        return outw_gate

    def barrier():
        marks = dict(pe=c["pe"], act=c["act"], dve=c["dve"], outw=c["outw"])
        for e in ("tensor", "scalar", "vector", "sync", "gpsimd"):
            em.wait(e, "pe", marks["pe"])
            em.wait(e, "act", marks["act"])
            em.wait(e, "dve", marks["dve"])
        em.wait("sync", "outw", marks["outw"])

    if STAGES == "ffn1":
        ffn(H["d_x"], H["d_out"], first=True)
    elif STAGES == "attn":
        ffn(H["d_x"], H["d_R1"], first=True)
        barrier()
        attn(H["d_R1"], H["d_out"])
    else:
        ffn(H["d_x"], H["d_R1"], first=True)
        barrier()
        attn(H["d_R1"], H["d_R2"])
        barrier()
        ffn(H["d_R2"], H["d_out"])

    em.wait("sync", "outw", c["outw"])


def _build():
    nc = bass.Bass("TRN2", target_bir_lowering=False, debug=False)

    H = {}
    H["d_x"] = nc.dram_tensor("x", [TOK, D], F32, kind="ExternalInput").ap()
    H["d_w12"] = nc.dram_tensor("w12", [32, 128, 2048], F32, kind="ExternalInput").ap()
    H["d_h1"] = nc.dram_tensor("h1", [32, 128, 4096], F32, kind="ExternalInput").ap()
    H["d_h2"] = nc.dram_tensor("h2", [32, 128, 4096], F32, kind="ExternalInput").ap()
    H["d_w3"] = nc.dram_tensor("w3", [8, 128, 4096], F32, kind="ExternalInput").ap()
    H["d_wqkv"] = nc.dram_tensor("wqkv", [16, 128, 1024], F32, kind="ExternalInput").ap()
    H["d_wv"] = nc.dram_tensor("wv", [8, 128, 1024], F32, kind="ExternalInput").ap()
    H["d_wout"] = nc.dram_tensor("wout", [8, 128, 1024], F32, kind="ExternalInput").ap()
    H["d_scols"] = nc.dram_tensor("scols", [128, 160], F32, kind="ExternalInput").ap()
    H["d_gam"] = nc.dram_tensor("gam", [128, 1024], F32, kind="ExternalInput").ap()
    H["d_bet"] = nc.dram_tensor("bet", [128, 1024], F32, kind="ExternalInput").ap()
    H["d_brows"] = nc.dram_tensor("brows", [1, 2048], F32, kind="ExternalInput").ap()
    H["d_idbf"] = nc.dram_tensor("idbf", [128, 128], BF16, kind="ExternalInput").ap()
    H["d_id32"] = nc.dram_tensor("id32", [128, 128], F32, kind="ExternalInput").ap()
    H["d_ones1"] = nc.dram_tensor("ones1", [1, 128], BF16, kind="ExternalInput").ap()
    H["d_sel4"] = nc.dram_tensor("sel4", [128, 32], BF16, kind="ExternalInput").ap()
    H["d_zrow"] = nc.dram_tensor("zrow", [1, 512], BF16, kind="ExternalInput").ap()
    H["d_fc"] = nc.dram_tensor("fconst", [128, 4], F32, kind="ExternalInput").ap()
    H["d_out"] = nc.dram_tensor("out", [TOK, D], F32, kind="ExternalOutput").ap()
    H["d_R1"] = nc.dram_tensor("R1", [TOK, D], F32).ap()
    H["d_R2"] = nc.dram_tensor("R2", [TOK, D], F32).ap()
    H["c_w12"] = nc.dram_tensor("c_w12", [32, 128, 2048], BF16).ap()
    H["c_h1"] = nc.dram_tensor("c_h1", [32, 128, 4096], BF16).ap()
    H["c_h2"] = nc.dram_tensor("c_h2", [32, 128, 4096], BF16).ap()
    H["c_w3"] = nc.dram_tensor("c_w3", [8, 128, 4096], BF16).ap()

    es = ExitStack()
    ARENA_KB = 202
    AR = es.enter_context(nc.sbuf_tensor("AR", [128, ARENA_KB * 1024], mybir.dt.uint8))
    arf = AR[:].bitcast(F32)
    arb = AR[:].bitcast(BF16)

    class Cursor:
        def __init__(self, off=0):
            self.off = off

        def take(self, nbytes, align=64):
            self.off = (self.off + align - 1) // align * align
            o = self.off
            self.off += nbytes
            assert self.off <= ARENA_KB * 1024, f"arena overflow {self.off}"
            return o

    def f32(cur, n):
        o = cur.take(n * 4)
        return arf[:, o // 4: o // 4 + n]

    def bf16(cur, n):
        o = cur.take(n * 2)
        return arb[:, o // 2: o // 2 + n]

    cur0 = Cursor()
    rb0 = f32(cur0, 1024)
    rb1 = f32(cur0, 1024)
    base_off = cur0.off

    cur = Cursor(base_off)
    ffnA = dict(rb0=rb0, rb1=rb1)
    ffnA["zb"] = [bf16(cur, 1024), bf16(cur, 1024)]
    ffnA["t1"] = [f32(cur, 512), f32(cur, 512), f32(cur, 512)]
    ffnA["w12a"] = bf16(cur, 2048)
    ffnA["w12b"] = bf16(cur, 2048)
    ffnA["w12c"] = bf16(cur, 2048)
    ffnA["whf0"] = bf16(cur, 2048)
    ffnA["whf1"] = bf16(cur, 2048)
    ffnA["whf2"] = bf16(cur, 2048)
    ffnA["whf3"] = bf16(cur, 2048)
    ffnA["w3h0"] = bf16(cur, 2048)
    ffnA["w3h1"] = bf16(cur, 2048)
    ffnA["outb"] = f32(cur, 1024)
    ffnA["xlnT"] = bf16(cur, 8192)
    ffnA["hA"] = bf16(cur, 32768)
    hb_off = cur.off
    ffnA["hB"] = bf16(cur, 32768)
    cur2 = Cursor(hb_off)
    ffnA["outT"] = f32(cur2, 4096)
    ffn_end = cur.off

    cur = Cursor(base_off)
    atA = dict(rb0=rb0, rb1=rb1)
    atA["tf"] = f32(cur, 1024)
    atA["zr"] = atA["tf"]
    atA["hb"] = bf16(cur, 1024)
    atA["r2b"] = f32(cur, 1024)
    atA["jb0"] = bf16(cur, 1024)
    atA["jb1"] = bf16(cur, 1024)
    atA["gam"] = f32(cur, 1024)
    atA["bet"] = f32(cur, 1024)
    atA["wout"] = [bf16(cur, 1024) for _ in range(8)]
    atA["nb32"] = bf16(cur, 64)
    atA["qT"] = bf16(cur, 8192)
    atA["kT"] = bf16(cur, 8192)
    atA["vaug"] = bf16(cur, 16 * 1040)
    atA["osb"] = bf16(cur, 16384)
    atA["vb"] = [atA["osb"][:, i * 1024:(i + 1) * 1024] for i in range(8)]
    atA["at"] = [bf16(cur, 2048), bf16(cur, 2048)]
    at2 = bf16(cur, 2048)
    browsA = bf16(cur, 2048)
    atA["at6"] = [atA["at"][0][:, 0:1024], atA["at"][0][:, 1024:2048],
                  at2[:, 0:1024],
                  atA["at"][1][:, 0:1024], atA["at"][1][:, 1024:2048],
                  at2[:, 1024:2048]]
    atA["otb"] = [bf16(cur, 1024), bf16(cur, 1024)]
    atA["hT"] = bf16(cur, 16384)
    attn_end = cur.off
    assert max(ffn_end, attn_end) <= ARENA_KB * 1024, (ffn_end, attn_end)

    def small(name, shape, dt):
        return es.enter_context(nc.sbuf_tensor(name, shape, dt))

    H["scols"] = small("s_scols", [128, 160], F32)
    H["smv"] = small("smv", [128, 32], F32)
    H["st6"] = small("st6", [128, 12], F32)
    H["rstd"] = small("rstd", [128, 16], F32)
    H["rstd2"] = small("rstd2", [128, 16], F32)
    H["zcA"] = small("zcA", [128, 16], F32)
    H["zcB"] = small("zcB", [128, 16], F32)
    H["negb"] = small("negb", [128, 8], F32)
    H["rc"] = small("rc", [128, 16], F32)
    H["nsc"] = small("nsc", [32, 8], F32)
    H["s3h"] = small("s3h", [128, 8], F32)
    H["idbf"] = small("s_idbf", [128, 128], BF16)
    H["id32"] = small("s_id32", [128, 128], F32)
    H["ones1"] = small("s_ones1", [1, 128], BF16)
    H["sel4"] = small("s_sel4", [128, 32], BF16)
    H["fc"] = small("s_fc", [128, 4], F32)
    H["zrow"] = small("s_zrow", [1, 512], BF16)

    H["PST"] = es.enter_context(nc.psum_tensor("PPST", [128, 2048], F32))
    H["P0"] = H["PST"][:, 0:1024]
    H["P1"] = H["PST"][:, 1024:2048]
    H["P2"] = es.enter_context(nc.psum_tensor("PP2", [128, 1024], F32))
    H["P3"] = es.enter_context(nc.psum_tensor("PP3", [128, 1024], F32))
    H["PT"] = H["P3"][:].bitcast(BF16)[:, 0:1024]

    H["ffn"] = ffnA
    H["attn"] = atA
    H["brows"] = browsA

    semmap = {s: es.enter_context(nc.semaphore(s)) for s in _SEMS}

    dry = Walk("dry", None, semmap)
    _schedule(dry, H)
    tapdict = {}
    for (name, shape, dt) in dry.taps:
        if name not in tapdict:
            tapdict[name] = nc.dram_tensor("tap_" + name, list(shape), dt,
                                           kind="ExternalOutput").ap()

    with es:
        with nc.Block() as block:
            counts = []

            def mkpass(which):
                def f(eng):
                    w = Walk(which, eng, semmap, tapdict)
                    _schedule(w, H)
                    counts.append(dict(w.cnt))
                return f

            block.sync(mkpass("sync"))
            block.gpsimd(mkpass("gpsimd"))
            block.tensor(mkpass("tensor"))
            block.scalar(mkpass("scalar"))
            block.vector(mkpass("vector"))

            for ci in counts[1:]:
                assert ci == counts[0], "pass count mismatch"

    return nc


def _host_prep(inp):
    f = np.float32
    x = np.asarray(inp["L0_attn_x"], f)
    gam = np.asarray(inp["L0_attn_ln_gamma"], f)
    bet = np.asarray(inp["L0_attn_ln_beta"], f)
    wqkv = np.asarray(inp["L0_attn_w_qkv"], f)
    bqkv = np.asarray(inp["L0_attn_b_qkv"], f)
    wout = np.asarray(inp["L0_attn_w_out"], f)
    bout = np.asarray(inp["L0_attn_b_out"], f)
    w1 = np.asarray(inp["L0_ffn_w1_w"], f)
    w1s = np.asarray(inp["L0_ffn_w1_s"], f)
    w2 = np.asarray(inp["L0_ffn_w2_w"], f)
    w2s = np.asarray(inp["L0_ffn_w2_s"], f)
    h1 = np.asarray(inp["L0_ffn_h1_w"], f)
    h1s = np.asarray(inp["L0_ffn_h1_s"], f)
    h2 = np.asarray(inp["L0_ffn_h2_w"], f)
    h2s = np.asarray(inp["L0_ffn_h2_s"], f)
    w3 = np.asarray(inp["L0_ffn_w3_w"], f)
    w3s = np.asarray(inp["L0_ffn_w3_s"], f)

    def phtile(w, njo, nji):
        return np.ascontiguousarray(
            w.reshape(njo, 128, nji, 128).transpose(0, 3, 2, 1).reshape(njo, 128, nji * 128))

    w1t = phtile(w1, 32, 8)
    w2t = phtile(w2, 32, 8)
    w12 = np.ascontiguousarray(np.concatenate([w1t, w2t], axis=2))
    h1t = phtile(h1, 32, 32)
    h2t = phtile(h2, 32, 32)
    w3t = phtile(w3, 8, 32)
    qk = wqkv[:, 0:2048]
    wqkv_t = np.ascontiguousarray(
        qk.reshape(8, 128, 16, 128).transpose(2, 1, 0, 3).reshape(16, 128, 1024))
    wv_t = np.ascontiguousarray(wqkv[:, 2048:3072].reshape(8, 128, 1024))
    wout_t = np.ascontiguousarray(wout.reshape(8, 128, 1024))

    scols = np.zeros((128, 160), f)
    scols[:, 0:32] = w1s.reshape(32, 128).T
    scols[:, 32:64] = w2s.reshape(32, 128).T
    scols[:, 64:96] = h1s.reshape(32, 128).T
    scols[:, 96:128] = h2s.reshape(32, 128).T
    scols[:, 128:136] = w3s.reshape(8, 128).T
    scols[:, 136:152] = bqkv[0:2048].reshape(16, 128).T

    brows = np.ascontiguousarray(np.concatenate([bqkv[2048:3072], bout]).reshape(1, 2048))
    gam_rep = np.ascontiguousarray(np.broadcast_to(gam, (128, 1024)))
    bet_rep = np.ascontiguousarray(np.broadcast_to(bet, (128, 1024)))

    idbf = np.eye(128, dtype=ml_dtypes.bfloat16)
    id32 = np.eye(128, dtype=f)
    ones1 = np.ones((1, 128), ml_dtypes.bfloat16)
    sel4 = np.zeros((128, 32), ml_dtypes.bfloat16)
    for a in range(4):
        sel4[0:64, a * 8 + 2 * a] = 1.0
        sel4[64:128, a * 8 + 2 * a + 1] = 1.0

    fconst = np.zeros((128, 4), f)
    fconst[:, 0] = EPS
    fconst[:, 1] = EPS * EPS
    fconst[:, 2] = 1e-18
    zrow = np.zeros((1, 512), ml_dtypes.bfloat16)
    common = dict(w12=w12, h1=h1t, h2=h2t, w3=w3t, wqkv=wqkv_t, wv=wv_t,
                  wout=wout_t, scols=scols, gam=gam_rep, bet=bet_rep,
                  brows=brows, idbf=idbf, id32=id32, ones1=ones1, sel4=sel4,
                  fconst=fconst, zrow=zrow)
    shards = [np.ascontiguousarray(x[2 * cid:2 * cid + 2].reshape(TOK, D))
              for cid in range(8)]
    return common, shards


def kernel(**inp):
    common, shards = _host_prep(inp)
    nc = _build()
    in_maps = [dict(common, x=shards[cid]) for cid in range(8)]
    res = run_bass_kernel_spmd(nc, in_maps, core_ids=list(range(8)))
    out = np.concatenate(
        [np.asarray(res.results[cid]["out"], np.float32).reshape(2, T, D)
         for cid in range(8)], axis=0)
    return out


# version 23
# speedup vs baseline: 1.5774x; 1.2615x over previous
import numpy as np
import ml_dtypes
import concourse.bass as bass
import concourse.mybir as mybir
from concourse.bass_utils import run_bass_kernel_spmd
from contextlib import ExitStack

F32 = mybir.dt.float32
BF16 = mybir.dt.bfloat16
AF = mybir.ActivationFunctionType
ALU = mybir.AluOpType
AX = mybir.AxisListType

D = 1024
HID = 4096
T = 2048
NBATCH = 2
TOK = NBATCH * T
TB = 1024
NBLK = TOK // TB
EPS = 1e-5

STAGES = "full"

_SEMS = [
    "pe", "act", "dve",
    "cst", "rin0", "rin1", "w12a", "w12b", "w12c", "whf0", "whf1", "whf2", "whf3",
    "w3h0", "w3h1", "jb0", "jb1", "vb", "wob", "row", "outw", "wbk",
]


TAPS_ON = False


class Walk:

    def __init__(self, which, engine, semmap, tapdict=None):
        self.w = which
        self.eng = engine
        self.sems = semmap
        self.cnt = {k: 0 for k in semmap}
        self.taps = []
        self.tapdict = tapdict

    def wait(self, eng, sem, val):
        if val is None or val <= 0:
            return
        assert val <= self.cnt[sem], f"wait-ahead {eng}.{sem} {val}>{self.cnt[sem]}"
        if self.w == eng:
            self.eng.wait_ge(self.sems[sem], val)

    def _fin(self, ins, inc, k):
        if inc is not None and ins is not None:
            ins.then_inc(self.sems[inc], k)

    def _bump(self, inc, k):
        if inc is None:
            return None
        self.cnt[inc] += k
        return self.cnt[inc]

    def mm(self, out, lhsT, rhs, start, stop, inc=None, k=1):
        if self.w == "tensor":
            ins = self.eng.matmul(out, lhsT, rhs, start=start, stop=stop)
            self._fin(ins, inc, k)
        return self._bump(inc, k)

    def tp(self, out, in_, ident, inc=None, k=1):
        if self.w == "tensor":
            ins = self.eng.transpose(out, in_, ident)
            self._fin(ins, inc, k)
        return self._bump(inc, k)

    def act(self, fn, *a, inc=None, k=1, **kw):
        if self.w == "scalar":
            ins = getattr(self.eng, fn)(*a, **kw)
            self._fin(ins, inc, k)
        return self._bump(inc, k)

    def dve(self, fn, *a, inc=None, k=1, **kw):
        if self.w == "vector":
            ins = getattr(self.eng, fn)(*a, **kw)
            self._fin(ins, inc, k)
        return self._bump(inc, k)

    def dma(self, eng, out, in_, inc, k=16):
        if self.w == eng:
            ins = self.eng.dma_start(out=out, in_=in_)
            self._fin(ins, inc, k)
        return self._bump(inc, k)

    def tap(self, name, ap, gates=()):
        if not TAPS_ON:
            return
        self.taps.append((name, tuple(ap.shape), ap.dtype))
        for sem, val in gates:
            self.wait("sync", sem, val)
        if self.w == "sync" and self.tapdict is not None:
            ins = self.eng.dma_start(out=self.tapdict[name][:], in_=ap)
            ins.then_inc(self.sems["outw"], 16)
        self.cnt["outw"] += 16


def _schedule(em, H):
    c = em.cnt

    em.dma("sync", H["scols"][:], H["d_scols"][:], "cst")
    em.dma("sync", H["idbf"][:], H["d_idbf"][:], "cst")
    em.dma("sync", H["id32"][:], H["d_id32"][:], "cst")
    em.dma("sync", H["ones1"][:], H["d_ones1"][:], "cst")
    em.dma("sync", H["sel4"][:], H["d_sel4"][:], "cst")
    em.dma("sync", H["fc"][:], H["d_fc"][:], "cst")
    em.dma("sync", H["zrow"][:], H["d_zrow"][:], "cst")
    CST_N = c["cst"]
    for e in ("tensor", "scalar", "vector"):
        em.wait(e, "cst", CST_N)
    em.dve("tensor_scalar", H["s3h"][:], H["scols"][:, 128:136], 0.5, None,
           ALU.mult, ALU.bypass)

    def ffn(src_d, dst_d, first=False):
        A = H["ffn"]
        rbuf = [A["rb0"], A["rb1"]]
        rsem = ["rin0", "rin1"]
        rfree = [0, 0]
        rload = [0, 0]
        outw_gate = c["outw"]
        zb_free = [0, 0]
        pt_dve_s = [0]
        last_z = [0]

        def ln_stats(b, z_gate):
            tok0 = b * TB
            for t in range(8):
                s = t % 2
                em.wait("sync", "dve", rfree[s])
                rload[s] = em.dma("sync", rbuf[s][:],
                                  src_d[tok0 + t * 128: tok0 + (t + 1) * 128, :], rsem[s])
                em.wait("vector", rsem[s], rload[s])
                em.wait("vector", "dve", z_gate if t == 0 else 0)
                em.dve("bn_stats", H["st6"][:, 0:6], rbuf[s][:, 0:512])
                em.dve("bn_stats", H["st6"][:, 6:12], rbuf[s][:, 512:1024])
                em.dve("drain")
                em.dve("bn_aggr", H["smv"][:, 2 * t:2 * t + 2], H["st6"][:])
                ST = em.dve("drain", inc="dve")
                rfree[s] = ST
            em.wait("scalar", "dve", ST)
            em.wait("scalar", "dve", z_gate)
            smv_var = H["smv"][:, 0:16].rearrange("p (t two) -> p t two", two=2)[:, :, 1]
            em.act("activation", H["rstd"][:, 0:8], smv_var, AF.Ln,
                   bias=H["fc"][:, 0:1], scale=1.0)
            RS = em.act("activation", H["rstd"][:, 0:8], H["rstd"][:, 0:8],
                        AF.Exp, scale=-0.5, inc="act")
            return RS

        def ln_apply_tile(b, RS, t, st):
            tok0 = b * TB
            pt_dve = st["pt"]
            if True:
                s = t % 2
                em.wait("sync", "dve", rfree[s])
                rload[s] = em.dma("sync", rbuf[s][:],
                                  src_d[tok0 + t * 128: tok0 + (t + 1) * 128, :], rsem[s])
                em.wait("vector", rsem[s], rload[s])
                em.wait("vector", "act", RS if t == 0 else 0)
                em.wait("vector", "pe", zb_free[t % 2])
                ZD = em.dve("tensor_scalar", A["zb"][t % 2][:], rbuf[s][:],
                            H["smv"][:, 2 * t:2 * t + 1], H["rstd"][:, t:t + 1],
                            ALU.subtract, ALU.mult, inc="dve")
                rfree[s] = ZD
                em.wait("tensor", "dve", ZD)
                em.wait("tensor", "dve", pt_dve)
                for q in range(8):
                    TPQ = em.tp(H["PT"][:, q * 128:(q + 1) * 128],
                                A["zb"][t % 2][:, q * 128:(q + 1) * 128], H["idbf"][:],
                                inc="pe" if q == 7 else None)
                zb_free[t % 2] = TPQ
                em.wait("vector", "pe", TPQ)
                dst = A["xlnT"].rearrange("p (i tt q) -> p i tt q", i=8, tt=8)[:, :, t, :]
                src = H["PT"][:].rearrange("p (i q) -> p i q", i=8)
                if t in (3, 7):
                    em.dve("tensor_copy", dst, src)
                    pt_dve = em.dve("drain", inc="dve")
                else:
                    pt_dve = em.dve("tensor_copy", dst, src, inc="dve")
                if t == 3:
                    st["half0"] = pt_dve
            st["pt"] = pt_dve
            st["ZD"] = ZD

        def ln_apply(b, RS, tiles=range(8), st=None):
            if st is None:
                st = {"pt": pt_dve_s[0], "half0": 0, "ZD": 0}
            for t in tiles:
                ln_apply_tile(b, RS, t, st)
            if max(tiles) == 7:
                pt_dve_s[0] = st["pt"]
                last_z[0] = st["ZD"]
            return st

        if first:
            A["_whfwbk"] = [0, 0, 0, 0]
        whf_state = {"load": [0, 0, 0, 0], "free": [0, 0, 0, 0],
                     "wbk": A["_whfwbk"]}
        w12_state = {"load": [0, 0, 0], "free": [0, 0, 0]}
        whfsem = ["whf0", "whf1", "whf2", "whf3"]
        whfbuf = [A["whf0"], A["whf1"], A["whf2"], A["whf3"]]

        def whf_load(ci, src):
            em.wait("scalar", "pe", whf_state["free"][ci])
            em.wait("scalar", "wbk", whf_state["wbk"][ci])
            whf_state["load"][ci] = em.dma("scalar", whfbuf[ci][:], src,
                                           whfsem[ci])

        def hid_prefetch(c_d):
            for j2 in range(2):
                for hc in range(2):
                    ci = (j2 * 2 + hc) % 4
                    whf_load(ci, c_d[j2][:, hc * 2048:(hc + 1) * 2048])

        def hid_layer(w_d, c_d, cache_write, src_sb, dst_sb, scol_off,
                      src_sem, src_cnt, ph1_p2_free=0, pre=False):
            pslot_free = [0, 0, 0]
            pregs = [H["P0"], H["P1"], H["P2"]]
            GE = 0
            if not cache_write and not pre:
                hid_prefetch(c_d)
            for j2 in range(32):
                if cache_write:
                    for hc in range(2):
                        ci = (j2 * 2 + hc) % 4
                        em.wait("gpsimd", "pe", whf_state["free"][ci])
                        em.wait("gpsimd", "wbk", whf_state["wbk"][ci])
                        whf_state["load"][ci] = em.dma(
                            "gpsimd", whfbuf[ci][:],
                            w_d[j2][:, hc * 2048:(hc + 1) * 2048], whfsem[ci])
                        em.wait("sync", whfsem[ci], whf_state["load"][ci])
                        whf_state["wbk"][ci] = em.dma(
                            "sync", c_d[j2][:, hc * 2048:(hc + 1) * 2048],
                            whfbuf[ci][:], "wbk")
                elif j2 >= 1 and j2 + 1 < 32:
                    for hc in range(2):
                        ci = ((j2 + 1) * 2 + hc) % 4
                        whf_load(ci, c_d[j2 + 1][:, hc * 2048:(hc + 1) * 2048])
                sp = j2 % 3
                Ph = pregs[sp]
                em.wait("tensor", "act", pslot_free[sp])
                em.wait("tensor", src_sem, src_cnt)
                em.wait("tensor", "dve", ph1_p2_free if j2 < 3 else 0)
                for half in range(2):
                    for i in range(32):
                        ci = (j2 * 2 + (i // 16)) % 4
                        if i in (0, 16):
                            em.wait("tensor", whfsem[ci], whf_state["load"][ci])
                        rhs = src_sb[:, i * 1024 + half * 512: i * 1024 + (half + 1) * 512]
                        MME = em.mm(Ph[:, half * 512:(half + 1) * 512],
                                    whfbuf[ci][:, (i % 16) * 128:((i % 16) + 1) * 128],
                                    rhs, start=(i == 0), stop=(i == 31),
                                    inc="pe" if (i == 31 or (half == 1 and i == 15))
                                    else None)
                        if half == 1 and i == 15:
                            whf_state["free"][(j2 * 2) % 4] = MME
                    if half == 1:
                        whf_state["free"][(j2 * 2 + 1) % 4] = MME
                em.wait("scalar", "pe", MME)
                dsth = dst_sb[:, j2 * 1024:(j2 + 1) * 1024]
                GE = em.act("activation", dsth, Ph[:], AF.Gelu,
                            scale=H["scols"][:, scol_off + j2:scol_off + j2 + 1],
                            inc="act")
                pslot_free[sp] = GE
            return GE

        stats_ready = ln_stats(0, 0)
        apply_ready = ln_apply(0, stats_ready)
        for b in range(NBLK):
            tok0 = b * TB
            xln_half0, XLN_OK = apply_ready["half0"], apply_ready["pt"]

            if not (first and b == 0):
                hid_prefetch(H["c_h1"])
            w12sem = ["w12a", "w12b", "w12c"]
            w12buf = [A["w12a"], A["w12b"], A["w12c"]]
            w12load = w12_state["load"]
            w12free = w12_state["free"]
            if first:
                A["_w12wbk"] = [0, 0, 0]
            w12wbk = A["_w12wbk"]

            def w12_issue(jj):
                sj = jj % 3
                if first and b == 0:
                    em.wait("gpsimd", "pe", w12free[sj])
                    em.wait("gpsimd", "wbk", w12wbk[sj])
                    w12load[sj] = em.dma("gpsimd", w12buf[sj][:], H["d_w12"][jj],
                                         w12sem[sj])
                    em.wait("sync", w12sem[sj], w12load[sj])
                    w12wbk[sj] = em.dma("sync", H["c_w12"][jj], w12buf[sj][:], "wbk")
                else:
                    em.wait("scalar", "pe", w12free[sj])
                    em.wait("scalar", "wbk", w12wbk[sj])
                    w12load[sj] = em.dma("scalar", w12buf[sj][:], H["c_w12"][jj],
                                         w12sem[sj])

            slot_free = [0, 0, 0]
            t1free = [0, 0, 0]
            SF = 0
            w12_issue(0)
            w12_issue(1)
            for j in range(32):
                sj = j % 3
                if j + 2 < 32:
                    w12_issue(j + 2)
                for half in range(2):
                    sp = (j * 2 + half) % 3
                    Ps = [H["P0"], H["P1"], H["P2"]][sp]
                    em.wait("tensor", w12sem[sj], w12load[sj])
                    em.wait("tensor", "dve",
                            max(xln_half0 if half == 0 else XLN_OK, slot_free[sp]))
                    for i in range(8):
                        rhs = A["xlnT"][:, i * 1024 + half * 512: i * 1024 + (half + 1) * 512]
                        em.mm(Ps[:, 0:512], w12buf[sj][:, i * 128:(i + 1) * 128], rhs,
                              start=(i == 0), stop=(i == 7))
                    for i in range(8):
                        rhs = A["xlnT"][:, i * 1024 + half * 512: i * 1024 + (half + 1) * 512]
                        MME = em.mm(Ps[:, 512:1024],
                                    w12buf[sj][:, 1024 + i * 128:1024 + (i + 1) * 128], rhs,
                                    start=(i == 0), stop=(i == 7),
                                    inc="pe" if i == 7 else None)
                    if half == 1:
                        w12free[sj] = MME
                    em.wait("scalar", "pe", MME)
                    em.wait("scalar", "dve", t1free[sp])
                    SIL = em.act("activation", A["t1"][sp][:], Ps[:, 0:512], AF.Silu,
                                 scale=H["scols"][:, j:j + 1], inc="act")
                    em.wait("vector", "act", SIL)
                    dsth = A["hA"][:, j * 1024 + half * 512: j * 1024 + (half + 1) * 512]
                    SF = em.dve("scalar_tensor_tensor", dsth, Ps[:, 512:1024],
                                H["scols"][:, 32 + j:32 + j + 1], A["t1"][sp][:],
                                ALU.mult, ALU.mult, inc="dve")
                    slot_free[sp] = SF
                    t1free[sp] = SF
            PH1_OK = SF

            PH2_OK = hid_layer(H["d_h1"], H["c_h1"], first and b == 0,
                               A["hA"], A["hB"], 64, "dve", PH1_OK,
                               ph1_p2_free=PH1_OK,
                               pre=not (first and b == 0))
            if b + 1 < NBLK:
                stats_ready = ln_stats(b + 1, last_z[0])
            PH3_OK = hid_layer(H["d_h2"], H["c_h2"], first and b == 0,
                               A["hB"], A["hA"], 96, "act", PH2_OK)

            w3sem = ["w3h0", "w3h1", "w12a", "w12b"]
            w3buf = [A["w3h0"], A["w3h1"], A["w12a"], A["w12b"]]
            if first:
                A["_w3wbk"] = [0, 0]
                A["_w3free4"] = [0, 0]
            w3wbk = A["_w3wbk"]
            w3load = [0, 0, 0, 0]
            w3free = [A["_w3free4"][0], A["_w3free4"][1],
                      w12_state["free"][0], w12_state["free"][1]]

            def w3_issue(half2, t):
                ci = t % 4
                em.wait("scalar", "pe", w3free[ci])
                em.wait("scalar", "wbk", max(w3wbk))
                w3load[ci] = em.dma(
                    "scalar", w3buf[ci][:],
                    H["c_w3"][t // 2][:, (t % 2) * 2048:(t % 2 + 1) * 2048],
                    w3sem[ci])
            ph4_slot_free = [0, 0, 0]
            p2_gate = 0
            for half in range(2):
                if half == 1 and b + 1 < NBLK:
                    ap_st = {"pt": pt_dve_s[0], "half0": 0, "ZD": 0}
                cache3 = first and b == 0 and half == 0
                cmod = 2 if cache3 else 4
                EV = 0
                if not cache3:
                    for t3 in range(4):
                        w3_issue(half, t3)
                for j3 in range(8):
                    if half == 1 and b + 1 < NBLK and j3 >= 1:
                        apply_ready = ln_apply(b + 1, stats_ready,
                                               tiles=[j3 - 1], st=ap_st)
                    if cache3:
                        for hc in range(2):
                            ci = (j3 * 2 + hc) % 2
                            em.wait("gpsimd", "pe", w3free[ci])
                            em.wait("gpsimd", "wbk", w3wbk[ci])
                            w3load[ci] = em.dma(
                                "gpsimd", w3buf[ci][:],
                                H["d_w3"][j3][:, hc * 2048:(hc + 1) * 2048], w3sem[ci])
                            em.wait("sync", w3sem[ci], w3load[ci])
                            w3wbk[ci] = em.dma(
                                "sync", H["c_w3"][j3][:, hc * 2048:(hc + 1) * 2048],
                                w3buf[ci][:], "wbk")
                    else:
                        if j3 >= 1:
                            for t3 in (j3 * 2 + 2, j3 * 2 + 3):
                                if t3 < 16:
                                    w3_issue(half, t3)
                    sp = j3 % 3
                    Pp = [H["P0"][:, 0:512], H["P1"][:, 0:512],
                          H["P0"][:, 512:1024]][sp]
                    em.wait("tensor", "dve", ph4_slot_free[sp])
                    em.wait("tensor", "act", PH3_OK)
                    for i in range(32):
                        ci = (j3 * 2 + (i // 16)) % cmod
                        if i in (0, 16):
                            em.wait("tensor", w3sem[ci], w3load[ci])
                        rhs = A["hA"][:, i * 1024 + half * 512: i * 1024 + (half + 1) * 512]
                        MME = em.mm(Pp, w3buf[ci][:, (i % 16) * 128:((i % 16) + 1) * 128],
                                    rhs, start=(i == 0), stop=(i == 31),
                                    inc="pe" if (i == 31 or i == 15) else None)
                        if i == 15:
                            w3free[(j3 * 2) % cmod] = MME
                    w3free[(j3 * 2 + 1) % cmod] = MME
                    em.wait("vector", "pe", MME)
                    EV = em.dve("tensor_scalar", A["outT"][:, j3 * 512:(j3 + 1) * 512], Pp,
                                H["s3h"][:, j3:j3 + 1], None, ALU.mult, ALU.bypass,
                                inc="dve")
                    ph4_slot_free[sp] = EV
                if half == 1 and b + 1 < NBLK:
                    apply_ready = ln_apply(b + 1, stats_ready,
                                           tiles=[7], st=ap_st)
                OUTT_OK = EV
                for q in range(4):
                    t = half * 4 + q
                    s = t % 2
                    em.wait("sync", "dve", rfree[s])
                    rload[s] = em.dma("sync", rbuf[s][:],
                                      src_d[tok0 + t * 128: tok0 + (t + 1) * 128, :],
                                      rsem[s])
                    em.wait("tensor", "dve", max(OUTT_OK, p2_gate))
                    for j3 in range(8):
                        TPQ = em.tp(H["P2"][:, j3 * 128:(j3 + 1) * 128],
                                    A["outT"][:, j3 * 512 + q * 128: j3 * 512 + (q + 1) * 128],
                                    H["id32"][:], inc="pe" if j3 == 7 else None)
                    em.wait("vector", "pe", TPQ)
                    em.wait("vector", rsem[s], rload[s])
                    em.wait("vector", "outw", outw_gate)
                    AD = em.dve("tensor_tensor", A["outb"][:], rbuf[s][:], H["P2"][:],
                                ALU.add, inc="dve")
                    rfree[s] = AD
                    p2_gate = AD
                    em.wait("sync", "dve", AD)
                    outw_gate = em.dma("sync",
                                       dst_d[tok0 + t * 128: tok0 + (t + 1) * 128, :],
                                       A["outb"][:], "outw")
            A["_w3free4"][0] = w3free[0]
            A["_w3free4"][1] = w3free[1]
            w12_state["free"][0] = max(w12_state["free"][0], w3free[2])
            w12_state["free"][1] = max(w12_state["free"][1], w3free[3])
        return outw_gate

    def attn(src_d, dst_d):
        def stage_barrier():
            marks = dict(pe=c["pe"], act=c["act"], dve=c["dve"])
            for e in ("tensor", "scalar", "vector", "sync", "gpsimd"):
                em.wait(e, "pe", marks["pe"])
                em.wait(e, "act", marks["act"])
                em.wait(e, "dve", marks["dve"])
        A = H["attn"]
        rbuf = [A["rb0"], A["rb1"]]
        rsem = ["rin0", "rin1"]
        outw_gate = c["outw"]
        at_done = {(s, p): 0 for s in range(2) for p in range(3)}
        wob_n = 0
        em.wait("gpsimd", "pe", c["pe"])
        for f in range(8):
            wob_n = em.dma("gpsimd", A["wout"][f][:], H["d_wout"][f], "wob")
        ROW_N = em.dma("gpsimd", H["brows"][0:1, 0:2048], H["d_brows"][:], "row")
        em.dma("sync", A["gam"][:], H["d_gam"][:], "cst")
        GBL = em.dma("sync", A["bet"][:], H["d_bet"][:], "cst")
        vload = 0

        for bi in range(NBATCH):
            tokb = bi * T
            rfree = [0, 0]
            rload = [0, 0]
            ps_act = [0, 0]
            ps_dve = [0, 0]
            pq_dve = 0
            pt_dve = 0

            em.wait("gpsimd", "pe", c["pe"])
            for i in range(8):
                vload = em.dma("gpsimd", A["vb"][i][:], H["d_wv"][i], "vb")
            va4 = A["vaug"].rearrange("p (t h cc) -> p t h cc", t=16, h=16)
            em.wait("vector", "pe", c["pe"])
            VMS = em.dve("memset", va4[:, :, :, 64:65], 1.0, inc="dve")
            for t in range(16):
                s = t % 2
                em.wait("sync", "dve", rfree[s])
                rload[s] = em.dma("sync", rbuf[s][:],
                                  src_d[tokb + t * 128: tokb + (t + 1) * 128, :], rsem[s])
                em.wait("vector", rsem[s], rload[s])
                em.dve("bn_stats", H["st6"][:, 0:6], rbuf[s][:, 0:512])
                em.dve("bn_stats", H["st6"][:, 6:12], rbuf[s][:, 512:1024])
                em.dve("drain")
                em.dve("bn_aggr", H["smv"][:, 2 * t:2 * t + 2], H["st6"][:])
                ST = em.dve("drain", inc="dve")
                em.wait("scalar", "dve", ST)
                em.act("activation", H["rstd"][:, t:t + 1], H["smv"][:, 2 * t + 1:2 * t + 2],
                       AF.Ln, bias=H["fc"][:, 0:1], scale=1.0)
                em.act("activation", H["rstd"][:, t:t + 1], H["rstd"][:, t:t + 1],
                       AF.Exp, scale=-0.5)
                em.act("activation", H["rstd2"][:, t:t + 1], H["smv"][:, 2 * t + 1:2 * t + 2],
                       AF.Ln, bias=H["fc"][:, 1:2], scale=float(1.0 + EPS))
                RS = em.act("activation", H["rstd2"][:, t:t + 1], H["rstd2"][:, t:t + 1],
                            AF.Exp, scale=-0.5, inc="act")
                em.wait("vector", "act", RS)
                em.wait("vector", "cst", GBL)
                em.dve("tensor_scalar", A["tf"][:], rbuf[s][:],
                       H["smv"][:, 2 * t:2 * t + 1], H["rstd2"][:, t:t + 1],
                       ALU.subtract, ALU.mult)
                em.dve("tensor_tensor", A["tf"][:], A["tf"][:], A["gam"][:], ALU.mult)
                HB = em.dve("tensor_tensor", A["hb"][:], A["tf"][:], A["bet"][:],
                            ALU.add, inc="dve")
                rfree[s] = HB
                if bi == 0 and t == 0:
                    em.tap("hb0", A["hb"][:], gates=[("dve", HB)])
                    em.tap("smv0", H["smv"][:], gates=[("dve", HB)])
                    em.tap("rstd20", H["rstd2"][:], gates=[("act", RS)])
                em.wait("tensor", "dve", max(HB, pt_dve))
                for q in range(8):
                    TPQ = em.tp(H["PT"][:, q * 128:(q + 1) * 128],
                                A["hb"][:, q * 128:(q + 1) * 128], H["idbf"][:],
                                inc="pe" if q == 7 else None)
                em.wait("vector", "pe", TPQ)
                dst = A["hT"].rearrange("p (i tt q) -> p i tt q", i=8, tt=16)[:, :, t, :]
                src = H["PT"][:].rearrange("p (i q) -> p i q", i=8)
                em.dve("tensor_copy", dst, src)
                pt_dve = em.dve("drain", inc="dve")
                sp = t % 2
                Ps = [H["P0"], H["P1"]][sp]
                em.wait("tensor", "vb", vload)
                em.wait("tensor", "row", ROW_N)
                em.wait("tensor", "act", ps_act[sp])
                em.wait("tensor", "dve", max(ps_dve[sp], pt_dve))
                for i in range(8):
                    for hh in range(2):
                        em.mm(Ps[:, hh * 512:(hh + 1) * 512],
                              A["hT"][:, i * 2048 + t * 128: i * 2048 + (t + 1) * 128],
                              A["vb"][i][:, hh * 512:(hh + 1) * 512],
                              start=(i == 0), stop=False)
                for hh in range(2):
                    MME = em.mm(Ps[:, hh * 512:(hh + 1) * 512],
                                H["ones1"][:],
                                H["brows"][0:1, hh * 512:(hh + 1) * 512],
                                start=False, stop=True,
                                inc="pe" if hh == 1 else None)
                em.wait("vector", "pe", MME)
                em.wait("vector", "dve", VMS)
                vsrc = Ps[:].rearrange("p (h cc) -> p h cc", h=16)
                em.dve("tensor_copy", va4[:, t, :, 0:64], vsrc)
                VE = em.dve("drain", inc="dve")
                ps_dve[sp] = VE
            if bi == 0:
                em.tap("hT0", A["hT"][:, 0:2048], gates=[("dve", pt_dve)])
            em.dve("tensor_scalar", H["zcA"][:], H["rstd"][:, 0:16], 1.0, None,
                   ALU.add, ALU.bypass)
            ZC = em.dve("tensor_tensor", H["zcB"][:],
                        H["smv"][:].rearrange("p (t two) -> p t two", two=2)[:, :, 0],
                        H["rstd"][:, 0:16], ALU.mult, inc="dve")

            v_done = ZC
            stage_barrier()
            for r in range(2):
                qj = [4 * r + a for a in range(4)]
                kj = [8 + 4 * r + a for a in range(4)]
                jsem = ["jb0", "jb1"]
                jbuf = [A["jb0"], A["jb1"]]
                jload = [0, 0]
                jfree = [0, 0]
                for idx, j in enumerate(qj + kj):
                    sj = idx % 2
                    em.wait("gpsimd", "pe", jfree[sj])
                    jload[sj] = em.dma("gpsimd", jbuf[sj][:], H["d_wqkv"][j], jsem[sj])
                    if TAPS_ON and bi == 0 and r == 1 and idx == 7:
                        em.tap("jb15", jbuf[sj][:], gates=[(jsem[sj], jload[sj])])
                    a = idx % 4
                    dst_sb = A["qT"] if idx < 4 else A["kT"]
                    for half in range(2):
                        sp = (idx * 2 + half) % 2
                        Ps = [H["P0"], H["P1"]][sp]
                        em.wait("tensor", jsem[sj], jload[sj])
                        em.wait("tensor", "dve", pt_dve)
                        em.wait("tensor", "act", ps_act[sp])
                        em.wait("tensor", "dve", ps_dve[sp])
                        for qtr in range(2):
                            for i in range(8):
                                rhs = A["hT"][:, i * 2048 + half * 1024 + qtr * 512:
                                              i * 2048 + half * 1024 + (qtr + 1) * 512]
                                MME = em.mm(Ps[:, qtr * 512:(qtr + 1) * 512],
                                            jbuf[sj][:, i * 128:(i + 1) * 128], rhs,
                                            start=(i == 0), stop=(i == 7),
                                            inc="pe" if (qtr == 1 and i == 7) else None)
                        if half == 1:
                            jfree[sj] = MME
                        if TAPS_ON and bi == 0 and r == 1 and idx == 7 and half == 1:
                            em.wait("vector", "pe", MME)
                            PQ_ = em.dve("tensor_copy", A["r2b"][:], Ps[:], inc="dve")
                            em.tap("k15psum", A["r2b"][:], gates=[("dve", PQ_)])
                            em.wait("scalar", "dve", PQ_)
                        em.wait("scalar", "pe", MME)
                        em.act("activation",
                               dst_sb[:, a * 2048 + half * 1024:
                                      a * 2048 + (half + 1) * 1024],
                               Ps[:], AF.Identity,
                               bias=H["scols"][:, 136 + j:136 + j + 1], scale=1.0)
                        QKE = em.act("drain", inc="act")
                        ps_act[sp] = QKE
                        if bi == 0 and idx == 0 and half == 1:
                            em.tap("qT0", A["qT"][:, 0:2048], gates=[("act", QKE)])
                        if bi == 0 and idx == 4 and half == 1:
                            em.tap("kT0", A["kT"][:, 0:2048], gates=[("act", QKE)])
                        if bi == 0 and r == 1 and idx == 3 and half == 1:
                            em.tap("qT3r1", A["qT"][:, 3 * 2048:4 * 2048], gates=[("act", QKE)])
                        if bi == 0 and r == 1 and idx == 7 and half == 1:
                            em.tap("kT3r1", A["kT"][:, 3 * 2048:4 * 2048], gates=[("act", QKE)])

                stage_barrier()
                nsc = H["nsc"]
                for side in range(2):
                    src_sb = A["qT"] if side == 0 else A["kT"]
                    Mcol = nsc[0:8, 1 + 2 * side:2 + 2 * side]
                    regs = [H["P2"][0:8, 0:512], H["P2"][0:8, 512:1024],
                            H["P3"][0:8, 0:512], H["P0"][0:8, 0:512]]
                    sq_free = [0, 0]
                    for a in range(4):
                        sb = a % 2
                        em.wait("vector", "pe", sq_free[sb])
                        SQ = em.dve("tensor_tensor", A["at"][sb][:],
                                    src_sb[:, a * 2048:(a + 1) * 2048],
                                    src_sb[:, a * 2048:(a + 1) * 2048], ALU.mult,
                                    inc="dve")
                        em.wait("tensor", "dve", SQ)
                        if a == 0:
                            em.wait("tensor", "dve", pq_dve)
                            em.wait("tensor", "act", ps_act[0])
                            em.wait("tensor", "dve", ps_dve[0])
                        for ch in range(4):
                            MME = em.mm(regs[ch], H["sel4"][:, a * 8:(a + 1) * 8],
                                        A["at"][sb][:, ch * 512:(ch + 1) * 512],
                                        start=(a == 0), stop=(a == 3),
                                        inc="pe" if ch == 3 else None)
                        sq_free[sb] = MME
                    em.wait("vector", "pe", MME)
                    for ch in range(4):
                        em.dve("reduce_max", nsc[0:8, 0:1], regs[ch], axis=AX.X)
                        em.dve("drain")
                        if ch == 0:
                            em.dve("tensor_copy", Mcol, nsc[0:8, 0:1])
                        else:
                            em.dve("tensor_tensor", Mcol, Mcol, nsc[0:8, 0:1], ALU.max)
                        em.dve("drain")
                    NR = em.dve("drain", inc="dve")
                    pq_dve = NR
                    ps_dve[0] = NR
                em.dve("tensor_tensor", nsc[0:8, 5:6], nsc[0:8, 1:2], nsc[0:8, 3:4],
                       ALU.mult)
                NP = em.dve("drain", inc="dve")
                em.wait("scalar", "dve", NP)
                em.act("activation", nsc[0:8, 6:7], nsc[0:8, 5:6], AF.Ln,
                       bias=H["fc"][0:8, 2:3], scale=1.0)
                NB_ = em.act("activation", nsc[0:8, 6:7], nsc[0:8, 6:7], AF.Exp,
                             scale=0.5, inc="act")
                em.wait("vector", "act", NB_)
                em.dve("tensor_scalar", nsc[0:8, 7:8], nsc[0:8, 6:7], -0.125, None,
                       ALU.mult, ALU.bypass)
                em.dve("drain")
                em.dve("memset", A["nb32"][0:32, 0:64], 0.0)
                em.dve("tensor_copy", A["nb32"][0:8, 0:1], nsc[0:8, 7:8])
                em.dve("drain")
                em.dve("transpose", A["nb32"][0:32, 32:64], A["nb32"][0:32, 0:32])
                NBT = em.dve("drain", inc="dve")
                em.wait("tensor", "dve", NBT)
                em.wait("tensor", "dve", ps_dve[0])
                em.wait("tensor", "act", ps_act[0])
                BC = em.mm(H["P0"][:, 0:8], H["ones1"][:], A["nb32"][0:1, 32:40],
                           start=True, stop=True, inc="pe")
                em.wait("vector", "pe", BC)
                em.dve("tensor_copy", H["negb"][:], H["P0"][:, 0:8])
                NBB = em.dve("drain", inc="dve")
                ps_dve[0] = NBB
                if bi == 0 and r == 0:
                    em.tap("nsc0", H["nsc"][:], gates=[("dve", NBB)])
                    em.tap("negb0", H["negb"][:], gates=[("dve", NBB)])

                stage_barrier()
                for pa in range(4):
                    heads = [8 * r + 2 * pa, 8 * r + 2 * pa + 1]
                    for qhalf in range(2):
                        obase = [H["P2"], H["P3"]]

                        def qoff(s, qc):
                            if qc < 7:
                                return obase[s][:, qc * 65:(qc + 1) * 65]
                            return obase[s][:, 512:577]

                        def score(s, kt):
                            hh = 2 * pa + s
                            pb = (hh % 2) * 64
                            Ps = [H["P0"], H["P1"]][s]
                            em.wait("tensor", "act", ps_act[s])
                            em.wait("tensor", "dve", ps_dve[s])
                            for qtr in range(2):
                                MME = em.mm(
                                    Ps[:, qtr * 512:(qtr + 1) * 512],
                                    A["kT"][pb:pb + 64,
                                            pa * 2048 + kt * 128: pa * 2048 + (kt + 1) * 128],
                                    A["qT"][pb:pb + 64,
                                            pa * 2048 + qhalf * 1024 + qtr * 512:
                                            pa * 2048 + qhalf * 1024 + (qtr + 1) * 512],
                                    start=True, stop=True,
                                    inc="pe" if qtr == 1 else None)
                            return MME

                        def expf(s, kt):
                            hh = 2 * pa + s
                            em.wait("scalar", "pe", max(sc_mme[s], at_done[(s, kt % 3)]))
                            if pa == 0 and qhalf == 0 and kt == 0:
                                em.wait("scalar", "dve", NBB)
                            EX = em.act("activation", A["at6"][3 * s + kt % 3][:],
                                        [H["P0"], H["P1"]][s][:], AF.Exp,
                                        bias=H["negb"][:, hh:hh + 1], scale=0.125,
                                        inc="act")
                            ps_act[s] = EX

                        def attnv(s, kt2):
                            hg = 8 * r + 2 * pa + s
                            atb2 = A["at6"][3 * s + kt2 % 3]
                            for qc in range(8):
                                M2 = em.mm(qoff(s, qc), atb2[:, qc * 128:(qc + 1) * 128],
                                           A["vaug"][:, kt2 * 1040 + hg * 65:
                                                     kt2 * 1040 + (hg + 1) * 65],
                                           start=False, stop=(kt2 == 15),
                                           inc="pe" if qc == 7 else None)
                            at_done[(s, kt2 % 3)] = M2

                        sc_mme = [0, 0]
                        for kt in range(16):
                            sc_mme[0] = score(0, kt)
                            sc_mme[1] = score(1, kt)
                            expf(0, kt)
                            expf(1, kt)
                            if kt == 2:
                                em.wait("tensor", "dve", max(pq_dve, v_done))
                                for s in range(2):
                                    em.mm(obase[s][:, 0:455], H["ones1"][:],
                                          H["zrow"][0:1, 0:455], start=True, stop=False)
                                    em.mm(obase[s][:, 512:577], H["ones1"][:],
                                          H["zrow"][0:1, 0:65], start=True, stop=False)
                            if kt >= 2:
                                attnv(0, kt - 2)
                                attnv(1, kt - 2)
                        attnv(0, 14)
                        attnv(1, 14)
                        em.wait("tensor", "act", ps_act[0])
                        attnv(0, 15)
                        em.wait("tensor", "act", ps_act[1])
                        attnv(1, 15)
                        em.wait("vector", "pe", at_done[(1, 15 % 3)])
                        for s in range(2):
                            hg = 8 * r + 2 * pa + s
                            em.dve("reciprocal", H["rc"][:, s * 8:s * 8 + 7],
                                   obase[s][:, 64:512:65])
                            em.dve("reciprocal", H["rc"][:, s * 8 + 7:s * 8 + 8],
                                   obase[s][:, 576:577])
                            em.dve("drain")
                            for qc in range(8):
                                gq = qhalf * 8 + qc
                                em.dve("tensor_scalar",
                                       A["osb"][:, gq * 1024 + hg * 64:
                                                gq * 1024 + (hg + 1) * 64],
                                       qoff(s, qc)[:, 0:64],
                                       H["rc"][:, s * 8 + qc:s * 8 + qc + 1], None,
                                       ALU.mult, ALU.bypass)
                        ON = em.dve("drain", inc="dve")
                        pq_dve = ON

            stage_barrier()
            if TAPS_ON and bi == 0:
                em.tap("osbP", A["osb"][:, 0:1024], gates=[("dve", pq_dve)])
                em.tap("osbP8", A["osb"][:, 8 * 1024:9 * 1024], gates=[("dve", pq_dve)])
            otb_free = [0, 0]
            OTEs = [0] * 17

            def proj_tp(t2):
                ob2 = t2 % 2
                em.wait("tensor", "dve", max(pq_dve if t2 == 0 else 0, pt_dve_l[0]))
                for f in range(8):
                    TPQ = em.tp(H["PT"][:, f * 128:(f + 1) * 128],
                                A["osb"][:, t2 * 1024 + f * 128: t2 * 1024 + (f + 1) * 128],
                                H["idbf"][:], inc="pe" if f == 7 else None)
                em.wait("vector", "pe", max(TPQ, otb_free[ob2]))
                OTEs[t2] = em.dve("tensor_copy", A["otb"][ob2][:], H["PT"][:], inc="dve")
                pt_dve_l[0] = OTEs[t2]

            pt_dve_l = [pt_dve]
            proj_tp(0)
            for t in range(16):
                ob = t % 2
                if t + 1 < 16:
                    proj_tp(t + 1)
                OTE = OTEs[t]
                s = t % 2
                em.wait("sync", "dve", rfree[s])
                rload[s] = em.dma("sync", rbuf[s][:],
                                  src_d[tokb + t * 128: tokb + (t + 1) * 128, :], rsem[s])
                sp = t % 2
                Pp = [H["P0"], H["P1"]][sp]
                em.wait("tensor", "dve", max(OTE, ps_dve[sp]))
                em.wait("tensor", "act", ps_act[sp])
                em.wait("tensor", "wob", wob_n)
                for half in range(2):
                    for f in range(8):
                        em.mm(Pp[:, half * 512:(half + 1) * 512],
                              A["otb"][ob][:, f * 128:(f + 1) * 128],
                              A["wout"][f][:, half * 512:(half + 1) * 512],
                              start=(f == 0), stop=False)
                    MME = em.mm(Pp[:, half * 512:(half + 1) * 512], H["ones1"][:],
                                H["brows"][0:1, 1024 + half * 512:1024 + (half + 1) * 512],
                                start=False, stop=True, inc="pe" if half == 1 else None)
                otb_free[ob] = MME
                em.wait("vector", "pe", MME)
                em.wait("vector", rsem[s], rload[s])
                em.wait("vector", "dve", ZC if t == 0 else 0)
                em.dve("tensor_scalar", A["zr"][:], rbuf[s][:],
                       H["zcA"][:, t:t + 1], H["zcB"][:, t:t + 1], ALU.mult, ALU.subtract)
                em.wait("vector", "outw", outw_gate)
                em.dve("tensor_tensor", A["r2b"][:, 0:512], A["zr"][:, 0:512],
                       Pp[:, 0:512], ALU.add)
                AD = em.dve("tensor_tensor", A["r2b"][:, 512:1024], A["zr"][:, 512:1024],
                            Pp[:, 512:1024], ALU.add, inc="dve")
                rfree[s] = AD
                ps_dve[sp] = AD
                pt_dve = pt_dve_l[0]
                if TAPS_ON and bi == 0 and t < 2:
                    em.tap(f"otb{t}", A["otb"][ob][:], gates=[("dve", OTE)])
                    em.tap(f"zr{t}", A["zr"][:], gates=[("dve", AD)])
                    em.tap(f"r2b{t}", A["r2b"][:], gates=[("dve", AD)])
                em.wait("sync", "dve", AD)
                outw_gate = em.dma("sync", dst_d[tokb + t * 128: tokb + (t + 1) * 128, :],
                                   A["r2b"][:], "outw")
        return outw_gate

    def barrier():
        marks = dict(pe=c["pe"], act=c["act"], dve=c["dve"], outw=c["outw"])
        for e in ("tensor", "scalar", "vector", "sync", "gpsimd"):
            em.wait(e, "pe", marks["pe"])
            em.wait(e, "act", marks["act"])
            em.wait(e, "dve", marks["dve"])
        em.wait("sync", "outw", marks["outw"])

    if STAGES == "ffn1":
        ffn(H["d_x"], H["d_out"], first=True)
    elif STAGES == "attn":
        ffn(H["d_x"], H["d_R1"], first=True)
        barrier()
        attn(H["d_R1"], H["d_out"])
    else:
        ffn(H["d_x"], H["d_R1"], first=True)
        barrier()
        attn(H["d_R1"], H["d_R2"])
        barrier()
        ffn(H["d_R2"], H["d_out"])

    em.wait("sync", "outw", c["outw"])


def _build():
    nc = bass.Bass("TRN2", target_bir_lowering=False, debug=False)

    H = {}
    H["d_x"] = nc.dram_tensor("x", [TOK, D], F32, kind="ExternalInput").ap()
    H["d_w12"] = nc.dram_tensor("w12", [32, 128, 2048], F32, kind="ExternalInput").ap()
    H["d_h1"] = nc.dram_tensor("h1", [32, 128, 4096], F32, kind="ExternalInput").ap()
    H["d_h2"] = nc.dram_tensor("h2", [32, 128, 4096], F32, kind="ExternalInput").ap()
    H["d_w3"] = nc.dram_tensor("w3", [8, 128, 4096], F32, kind="ExternalInput").ap()
    H["d_wqkv"] = nc.dram_tensor("wqkv", [16, 128, 1024], F32, kind="ExternalInput").ap()
    H["d_wv"] = nc.dram_tensor("wv", [8, 128, 1024], F32, kind="ExternalInput").ap()
    H["d_wout"] = nc.dram_tensor("wout", [8, 128, 1024], F32, kind="ExternalInput").ap()
    H["d_scols"] = nc.dram_tensor("scols", [128, 160], F32, kind="ExternalInput").ap()
    H["d_gam"] = nc.dram_tensor("gam", [128, 1024], F32, kind="ExternalInput").ap()
    H["d_bet"] = nc.dram_tensor("bet", [128, 1024], F32, kind="ExternalInput").ap()
    H["d_brows"] = nc.dram_tensor("brows", [1, 2048], F32, kind="ExternalInput").ap()
    H["d_idbf"] = nc.dram_tensor("idbf", [128, 128], BF16, kind="ExternalInput").ap()
    H["d_id32"] = nc.dram_tensor("id32", [128, 128], F32, kind="ExternalInput").ap()
    H["d_ones1"] = nc.dram_tensor("ones1", [1, 128], BF16, kind="ExternalInput").ap()
    H["d_sel4"] = nc.dram_tensor("sel4", [128, 32], BF16, kind="ExternalInput").ap()
    H["d_zrow"] = nc.dram_tensor("zrow", [1, 512], BF16, kind="ExternalInput").ap()
    H["d_fc"] = nc.dram_tensor("fconst", [128, 4], F32, kind="ExternalInput").ap()
    H["d_out"] = nc.dram_tensor("out", [TOK, D], F32, kind="ExternalOutput").ap()
    H["d_R1"] = nc.dram_tensor("R1", [TOK, D], F32).ap()
    H["d_R2"] = nc.dram_tensor("R2", [TOK, D], F32).ap()
    H["c_w12"] = nc.dram_tensor("c_w12", [32, 128, 2048], BF16).ap()
    H["c_h1"] = nc.dram_tensor("c_h1", [32, 128, 4096], BF16).ap()
    H["c_h2"] = nc.dram_tensor("c_h2", [32, 128, 4096], BF16).ap()
    H["c_w3"] = nc.dram_tensor("c_w3", [8, 128, 4096], BF16).ap()

    es = ExitStack()
    ARENA_KB = 202
    AR = es.enter_context(nc.sbuf_tensor("AR", [128, ARENA_KB * 1024], mybir.dt.uint8))
    arf = AR[:].bitcast(F32)
    arb = AR[:].bitcast(BF16)

    class Cursor:
        def __init__(self, off=0):
            self.off = off

        def take(self, nbytes, align=64):
            self.off = (self.off + align - 1) // align * align
            o = self.off
            self.off += nbytes
            assert self.off <= ARENA_KB * 1024, f"arena overflow {self.off}"
            return o

    def f32(cur, n):
        o = cur.take(n * 4)
        return arf[:, o // 4: o // 4 + n]

    def bf16(cur, n):
        o = cur.take(n * 2)
        return arb[:, o // 2: o // 2 + n]

    cur0 = Cursor()
    rb0 = f32(cur0, 1024)
    rb1 = f32(cur0, 1024)
    base_off = cur0.off

    cur = Cursor(base_off)
    ffnA = dict(rb0=rb0, rb1=rb1)
    ffnA["zb"] = [bf16(cur, 1024), bf16(cur, 1024)]
    ffnA["t1"] = [f32(cur, 512), f32(cur, 512), f32(cur, 512)]
    ffnA["w12a"] = bf16(cur, 2048)
    ffnA["w12b"] = bf16(cur, 2048)
    ffnA["w12c"] = bf16(cur, 2048)
    ffnA["whf0"] = bf16(cur, 2048)
    ffnA["whf1"] = bf16(cur, 2048)
    ffnA["whf2"] = bf16(cur, 2048)
    ffnA["whf3"] = bf16(cur, 2048)
    ffnA["w3h0"] = bf16(cur, 2048)
    ffnA["w3h1"] = bf16(cur, 2048)
    ffnA["outb"] = f32(cur, 1024)
    ffnA["xlnT"] = bf16(cur, 8192)
    ffnA["hA"] = bf16(cur, 32768)
    hb_off = cur.off
    ffnA["hB"] = bf16(cur, 32768)
    cur2 = Cursor(hb_off)
    ffnA["outT"] = f32(cur2, 4096)
    ffn_end = cur.off

    cur = Cursor(base_off)
    atA = dict(rb0=rb0, rb1=rb1)
    atA["tf"] = f32(cur, 1024)
    atA["zr"] = atA["tf"]
    atA["hb"] = bf16(cur, 1024)
    atA["r2b"] = f32(cur, 1024)
    atA["jb0"] = bf16(cur, 1024)
    atA["jb1"] = bf16(cur, 1024)
    atA["gam"] = f32(cur, 1024)
    atA["bet"] = f32(cur, 1024)
    atA["wout"] = [bf16(cur, 1024) for _ in range(8)]
    atA["nb32"] = bf16(cur, 64)
    atA["qT"] = bf16(cur, 8192)
    atA["kT"] = bf16(cur, 8192)
    atA["vaug"] = bf16(cur, 16 * 1040)
    atA["osb"] = bf16(cur, 16384)
    atA["vb"] = [atA["osb"][:, i * 1024:(i + 1) * 1024] for i in range(8)]
    atA["at"] = [bf16(cur, 2048), bf16(cur, 2048)]
    at2 = bf16(cur, 2048)
    browsA = bf16(cur, 2048)
    atA["at6"] = [atA["at"][0][:, 0:1024], atA["at"][0][:, 1024:2048],
                  at2[:, 0:1024],
                  atA["at"][1][:, 0:1024], atA["at"][1][:, 1024:2048],
                  at2[:, 1024:2048]]
    atA["otb"] = [bf16(cur, 1024), bf16(cur, 1024)]
    atA["hT"] = bf16(cur, 16384)
    attn_end = cur.off
    assert max(ffn_end, attn_end) <= ARENA_KB * 1024, (ffn_end, attn_end)

    def small(name, shape, dt):
        return es.enter_context(nc.sbuf_tensor(name, shape, dt))

    H["scols"] = small("s_scols", [128, 160], F32)
    H["smv"] = small("smv", [128, 32], F32)
    H["st6"] = small("st6", [128, 12], F32)
    H["rstd"] = small("rstd", [128, 16], F32)
    H["rstd2"] = small("rstd2", [128, 16], F32)
    H["zcA"] = small("zcA", [128, 16], F32)
    H["zcB"] = small("zcB", [128, 16], F32)
    H["negb"] = small("negb", [128, 8], F32)
    H["rc"] = small("rc", [128, 16], F32)
    H["nsc"] = small("nsc", [32, 8], F32)
    H["s3h"] = small("s3h", [128, 8], F32)
    H["idbf"] = small("s_idbf", [128, 128], BF16)
    H["id32"] = small("s_id32", [128, 128], F32)
    H["ones1"] = small("s_ones1", [1, 128], BF16)
    H["sel4"] = small("s_sel4", [128, 32], BF16)
    H["fc"] = small("s_fc", [128, 4], F32)
    H["zrow"] = small("s_zrow", [1, 512], BF16)

    H["PST"] = es.enter_context(nc.psum_tensor("PPST", [128, 2048], F32))
    H["P0"] = H["PST"][:, 0:1024]
    H["P1"] = H["PST"][:, 1024:2048]
    H["P2"] = es.enter_context(nc.psum_tensor("PP2", [128, 1024], F32))
    H["P3"] = es.enter_context(nc.psum_tensor("PP3", [128, 1024], F32))
    H["PT"] = H["P3"][:].bitcast(BF16)[:, 0:1024]

    H["ffn"] = ffnA
    H["attn"] = atA
    H["brows"] = browsA

    semmap = {s: es.enter_context(nc.semaphore(s)) for s in _SEMS}

    dry = Walk("dry", None, semmap)
    _schedule(dry, H)
    tapdict = {}
    for (name, shape, dt) in dry.taps:
        if name not in tapdict:
            tapdict[name] = nc.dram_tensor("tap_" + name, list(shape), dt,
                                           kind="ExternalOutput").ap()

    with es:
        with nc.Block() as block:
            counts = []

            def mkpass(which):
                def f(eng):
                    w = Walk(which, eng, semmap, tapdict)
                    _schedule(w, H)
                    counts.append(dict(w.cnt))
                return f

            block.sync(mkpass("sync"))
            block.gpsimd(mkpass("gpsimd"))
            block.tensor(mkpass("tensor"))
            block.scalar(mkpass("scalar"))
            block.vector(mkpass("vector"))

            for ci in counts[1:]:
                assert ci == counts[0], "pass count mismatch"

    return nc


def _host_prep(inp):
    f = np.float32
    x = np.asarray(inp["L0_attn_x"], f)
    gam = np.asarray(inp["L0_attn_ln_gamma"], f)
    bet = np.asarray(inp["L0_attn_ln_beta"], f)
    wqkv = np.asarray(inp["L0_attn_w_qkv"], f)
    bqkv = np.asarray(inp["L0_attn_b_qkv"], f)
    wout = np.asarray(inp["L0_attn_w_out"], f)
    bout = np.asarray(inp["L0_attn_b_out"], f)
    w1 = np.asarray(inp["L0_ffn_w1_w"], f)
    w1s = np.asarray(inp["L0_ffn_w1_s"], f)
    w2 = np.asarray(inp["L0_ffn_w2_w"], f)
    w2s = np.asarray(inp["L0_ffn_w2_s"], f)
    h1 = np.asarray(inp["L0_ffn_h1_w"], f)
    h1s = np.asarray(inp["L0_ffn_h1_s"], f)
    h2 = np.asarray(inp["L0_ffn_h2_w"], f)
    h2s = np.asarray(inp["L0_ffn_h2_s"], f)
    w3 = np.asarray(inp["L0_ffn_w3_w"], f)
    w3s = np.asarray(inp["L0_ffn_w3_s"], f)

    def phtile(w, njo, nji):
        return np.ascontiguousarray(
            w.reshape(njo, 128, nji, 128).transpose(0, 3, 2, 1).reshape(njo, 128, nji * 128))

    w1t = phtile(w1, 32, 8)
    w2t = phtile(w2, 32, 8)
    w12 = np.ascontiguousarray(np.concatenate([w1t, w2t], axis=2))
    h1t = phtile(h1, 32, 32)
    h2t = phtile(h2, 32, 32)
    w3t = phtile(w3, 8, 32)
    qk = wqkv[:, 0:2048]
    wqkv_t = np.ascontiguousarray(
        qk.reshape(8, 128, 16, 128).transpose(2, 1, 0, 3).reshape(16, 128, 1024))
    wv_t = np.ascontiguousarray(wqkv[:, 2048:3072].reshape(8, 128, 1024))
    wout_t = np.ascontiguousarray(wout.reshape(8, 128, 1024))

    scols = np.zeros((128, 160), f)
    scols[:, 0:32] = w1s.reshape(32, 128).T
    scols[:, 32:64] = w2s.reshape(32, 128).T
    scols[:, 64:96] = h1s.reshape(32, 128).T
    scols[:, 96:128] = h2s.reshape(32, 128).T
    scols[:, 128:136] = w3s.reshape(8, 128).T
    scols[:, 136:152] = bqkv[0:2048].reshape(16, 128).T

    brows = np.ascontiguousarray(np.concatenate([bqkv[2048:3072], bout]).reshape(1, 2048))
    gam_rep = np.ascontiguousarray(np.broadcast_to(gam, (128, 1024)))
    bet_rep = np.ascontiguousarray(np.broadcast_to(bet, (128, 1024)))

    idbf = np.eye(128, dtype=ml_dtypes.bfloat16)
    id32 = np.eye(128, dtype=f)
    ones1 = np.ones((1, 128), ml_dtypes.bfloat16)
    sel4 = np.zeros((128, 32), ml_dtypes.bfloat16)
    for a in range(4):
        sel4[0:64, a * 8 + 2 * a] = 1.0
        sel4[64:128, a * 8 + 2 * a + 1] = 1.0

    fconst = np.zeros((128, 4), f)
    fconst[:, 0] = EPS
    fconst[:, 1] = EPS * EPS
    fconst[:, 2] = 1e-18
    zrow = np.zeros((1, 512), ml_dtypes.bfloat16)
    common = dict(w12=w12, h1=h1t, h2=h2t, w3=w3t, wqkv=wqkv_t, wv=wv_t,
                  wout=wout_t, scols=scols, gam=gam_rep, bet=bet_rep,
                  brows=brows, idbf=idbf, id32=id32, ones1=ones1, sel4=sel4,
                  fconst=fconst, zrow=zrow)
    shards = [np.ascontiguousarray(x[2 * cid:2 * cid + 2].reshape(TOK, D))
              for cid in range(8)]
    return common, shards


def kernel(**inp):
    common, shards = _host_prep(inp)
    nc = _build()
    in_maps = [dict(common, x=shards[cid]) for cid in range(8)]
    res = run_bass_kernel_spmd(nc, in_maps, core_ids=list(range(8)))
    out = np.concatenate(
        [np.asarray(res.results[cid]["out"], np.float32).reshape(2, T, D)
         for cid in range(8)], axis=0)
    return out
